# version 02
# baseline (speedup 1.0000x reference)
import contextlib
import numpy as np
import concourse.bass as bass
import concourse.mybir as mybir
from concourse.bass_utils import run_bass_kernel_spmd

F32 = mybir.dt.float32
F32R = mybir.dt.float32r
BF16 = mybir.dt.bfloat16
U16 = mybir.dt.uint16
U32 = mybir.dt.uint32
I32 = mybir.dt.int32
I16 = mybir.dt.int16
Alu = mybir.AluOpType
ActF = mybir.ActivationFunctionType
AX = mybir.AxisListType

NCORES = 8
D, DC = 2048, 16
EI, ECP = 4096, 32
H, HC = 8192, 64
NT, THF = 1024, 512
HGRP = 8
NGRP = HC // HGRP
M_BND = 0.501953125
P25 = 2.0 ** -25
LN_EPS = 1e-5
EPS32 = float(np.finfo(np.float32).eps)
MAGIC = 0x5F3759DF
QW = 1024
QG = QW // 64

ENGINES = ("sync", "gpsimd", "tensor", "vector", "scalar")


class Sched:

    def __init__(self):
        self.q = {e: [] for e in ENGINES}
        self.cnt = {}

    def n(self, sem):
        return self.cnt.get(sem, 0)

    def emit(self, eng, fn, waits=(), inc=None, amt=1):
        waits = tuple((s, v) for (s, v) in waits if v > 0)
        self.q[eng].append((waits, fn, inc, amt))
        if inc is not None:
            self.cnt[inc] = self.cnt.get(inc, 0) + amt
        return self.cnt.get(inc, 0)

    def dma(self, eng, fn, waits=(), inc=None):
        return self.emit(eng, fn, waits, inc, 16)


def build_core(taps=None, barriers=None):
    taps = taps or []
    barriers = barriers or []
    nc = bass.Bass("TRN2", target_bir_lowering=False, debug=False,
                   detect_race_conditions=False)

    x_in = nc.dram_tensor("x", [D, NT], F32, kind="ExternalInput").ap()
    winT = nc.dram_tensor("winT", [D, 2 * EI], F32, kind="ExternalInput").ap()
    woutT = nc.dram_tensor("woutT", [EI, D], F32, kind="ExternalInput").ap()
    wfc = nc.dram_tensor("wfc", [H, D], F32, kind="ExternalInput").ap()
    wproj = nc.dram_tensor("wproj", [D, H], F32, kind="ExternalInput").ap()
    gamma = nc.dram_tensor("gamma", [128, DC], F32, kind="ExternalInput").ap()
    beta = nc.dram_tensor("beta", [128, DC], F32, kind="ExternalInput").ap()
    dt_fl = nc.dram_tensor("dt_fl", [1, EI], F32, kind="ExternalInput").ap()
    dt_pc = nc.dram_tensor("dt_pc", [128, ECP], F32, kind="ExternalInput").ap()
    ones_in = nc.dram_tensor("ones_in", [128, 128], F32, kind="ExternalInput").ap()
    out = nc.dram_tensor("out", [D, NT], F32, kind="ExternalOutput").ap()
    qfc = nc.dram_tensor("qfc", [H, D], BF16, kind="Internal").ap()
    qproj = nc.dram_tensor("qproj", [D, H], BF16, kind="Internal").ap()

    S = Sched()
    es = contextlib.ExitStack()
    with es:
        sb = lambda n, sh, dt: es.enter_context(nc.sbuf_tensor(n, sh, dt))
        psu = lambda n, sh: es.enter_context(nc.psum_tensor(n, sh, F32))

        B = sb("B", [128, 57344], BF16)
        xnh = B[:, 0:8192].rearrange("p (c t) -> p c t", c=DC)
        xnl = B[:, 8192:16384].rearrange("p (c t) -> p c t", c=DC)
        Th = B[:, 16384:32768].rearrange("p (c t) -> p c t", c=ECP)
        Tl = B[:, 32768:49152].rearrange("p (c t) -> p c t", c=ECP)
        whi = B[:, 49152:53248]
        wlo = B[:, 53248:57344]
        x2h = B[:, 0:8192].rearrange("p (c t) -> p c t", c=DC)
        x2l = B[:, 8192:16384].rearrange("p (c t) -> p c t", c=DC)
        gh = B[:, 16384:20480].rearrange("p (c t) -> p c t", c=HGRP)
        gl = B[:, 20480:24576].rearrange("p (c t) -> p c t", c=HGRP)
        wq1 = [B[:, 24576:32768].rearrange("p (c t) -> p c t", c=DC),
               B[:, 32768:40960].rearrange("p (c t) -> p c t", c=DC)]
        wq2 = [B[:, 40960:45056].rearrange("p (c t) -> p c t", c=HGRP),
               B[:, 45056:49152].rearrange("p (c t) -> p c t", c=HGRP)]

        F = sb("F", [128, 20480], F32)
        X = F[:, 0:8192].rearrange("p (c t) -> p c t", c=DC)
        e1 = F[:, 8192:8704]
        e2 = F[:, 8704:9216]
        e3 = F[:, 9216:9728]
        e4 = F[:, 9728:10240]
        mu_bc = F[:, 10240:10752]
        r_bc = F[:, 10752:11264]
        share = F[:, 11264:19456]
        wbuf = share[:, 0:4096]
        Oacc = share.rearrange("p (c t) -> p c t", c=DC)
        dtf = share[0:1, 4096:8192]
        qw = F[:, 19456:20480]
        ob0 = F[:, 19456:19968]
        ob1 = F[:, 19968:20480]
        gsc = [e1, e2]

        R = sb("R", [128, 1024], F32R)
        xr = R[:, 0:512]
        xsq = R[:, 512:1024]
        g2 = [xr, xsq]

        IA = sb("IA", [128, 3072], U16)
        qwb = IA[:, 0:1024]
        qau = IA[:, 1024:2048]
        qmk = IA[:, 2048:3072]
        useed = IA[:, 2048:3072].bitcast(U32)

        qgs = sb("qgs", [128, QG], F32)
        qs16 = sb("qs16", [128, QG], U16)
        qsf = sb("qsf", [128, QG], U32)
        qthr = sb("qthr", [128, QG], U32)

        ones_f = sb("ones_f", [128, 128], F32)
        ones_r = sb("ones_r", [128, 128], F32R)
        ones1 = sb("ones1", [1, 128], F32)
        gam = sb("gam", [128, DC], F32)
        bet = sb("bet", [128, DC], F32)
        dtw = sb("dtw", [128, ECP], F32)
        dtp = sb("dtp", [128, ECP], F32)
        sc1 = sb("sc1", [1, 8], F32)
        mb1 = sb("mb1", [128, 1], F32)
        rb1 = sb("rb1", [128, 1], F32)

        pZ = psu("pZ", [128, THF])
        pG = psu("pG", [128, THF])
        pY = [psu("pY0", [128, THF]), psu("pY1", [128, THF])]
        pS = psu("pS", [128, THF])
        pS2 = psu("pS2", [128, THF])
        pB = psu("pB", [128, 8])

        sems = {}
        for s in ("lds", "ldg", "mm", "vv", "ss", "stq", "wq"):
            sems[s] = es.enter_context(nc.semaphore(s))

        blk = es.enter_context(nc.Block())

        def tap(name, src_ap, shape, dtype=F32):
            if name not in taps:
                return
            t = nc.dram_tensor("tap_" + name, shape, dtype, kind="ExternalOutput").ap()
            S.dma("sync", lambda e: e.dma_start(out=t, in_=src_ap),
                  waits=[("vv", S.n("vv")), ("ss", S.n("ss")), ("mm", S.n("mm"))], inc="stq")
            v = S.n("stq")
            S.emit("vector", lambda e: e.nop(), waits=[("stq", v)], inc="vv")
            S.emit("scalar", lambda e: e.nop(), waits=[("stq", v)], inc="ss")

        S.dma("sync", lambda e: e.dma_start(out=gam[:], in_=gamma), inc="lds")
        S.dma("sync", lambda e: e.dma_start(out=bet[:], in_=beta), inc="lds")
        S.dma("sync", lambda e: e.dma_start(out=dtf, in_=dt_fl), inc="lds")
        S.dma("sync", lambda e: e.dma_start(out=dtp[:], in_=dt_pc), inc="lds")
        S.dma("sync", lambda e: e.dma_start(out=ones_f[:], in_=ones_in), inc="lds")
        ld0 = S.n("lds")

        S.emit("vector", lambda e: e.tensor_copy(ones_r[:], ones_f[:]),
               waits=[("lds", ld0)], inc="vv")
        S.emit("vector", lambda e: e.tensor_copy(ones1[:], ones_f[0:1, :]), inc="vv")
        S.emit("vector", lambda e: e.memset(qsf[:], 0), inc="vv")
        S.emit("vector", lambda e: e.tensor_scalar(dtf, dtf, -1.0, None, Alu.mult), inc="vv")
        vrm = S.emit("vector", lambda e: e.reduce_max(sc1[0:1, 0:1], dtf, AX.X), inc="vv")
        S.emit("vector", lambda e: e.tensor_scalar(sc1[0:1, 1:2], sc1[0:1, 0:1], -1.0, None, Alu.mult),
               waits=[("vv", vrm)], inc="vv")
        v = S.n("vv")
        S.emit("scalar", lambda e: e.activation(dtf, dtf, ActF.Exp, bias=sc1[0:1, 1:2], scale=1.0),
               waits=[("vv", v)], inc="ss")
        s = S.n("ss")
        vrs = S.emit("vector", lambda e: e.reduce_sum(sc1[0:1, 2:3], dtf, AX.X),
               waits=[("ss", s)], inc="vv")
        vrc = S.emit("vector", lambda e: e.reciprocal(sc1[0:1, 3:4], sc1[0:1, 2:3]),
               waits=[("vv", vrs)], inc="vv")
        S.emit("vector", lambda e: e.nop(), waits=[("vv", vrc)], inc="vv")
        v = S.n("vv")
        S.emit("tensor", lambda e: e.matmul(pB[:, 0:1], ones1[:], sc1[0:1, 1:2], start=True, stop=True),
               waits=[("vv", v)], inc="mm")
        S.emit("tensor", lambda e: e.matmul(pB[:, 1:2], ones1[:], sc1[0:1, 3:4], start=True, stop=True), inc="mm")
        mm = S.n("mm")
        S.emit("vector", lambda e: e.tensor_copy(mb1[:], pB[:, 0:1]), waits=[("mm", mm)], inc="vv")
        S.emit("vector", lambda e: e.tensor_copy(rb1[:], pB[:, 1:2]), inc="vv")
        S.emit("vector", lambda e: e.tensor_scalar(dtp[:], dtp[:], -1.0, None, Alu.mult), inc="vv")
        v = S.n("vv")
        S.emit("scalar", lambda e: e.activation(dtw[:], dtp[:], ActF.Exp, bias=mb1[:], scale=1.0),
               waits=[("vv", v)], inc="ss")
        s = S.n("ss")
        S.emit("vector", lambda e: e.tensor_scalar(dtw[:], dtw[:], rb1[:], None, Alu.mult),
               waits=[("ss", s)], inc="vv")
        v_init = S.n("vv")
        tap("dtw", dtw[:], [128, ECP])
        tap("sc1", sc1[:], [1, 8])
        tap("mb1", mb1[:], [128, 1])
        tap("rb1", rb1[:], [128, 1])
        tap("dtf", dtf, [1, EI])
        tap("dtp2", dtp[:], [128, ECP])

        def barrier(name):
            if name not in barriers:
                return
            w = [(s, S.n(s)) for s in ("lds", "ldg", "mm", "vv", "ss", "stq", "wq")]
            for eng in ("vector", "scalar", "tensor", "sync", "gpsimd"):
                S.emit(eng, lambda e: e.nop(), waits=w)

        def newton_rsqrt(dst):
            qau16 = qau[:, 0:512]
            S.emit("vector", lambda e: e.memset(useed, 0),
                   waits=[("wq", S.n("wq"))], inc="vv")
            S.emit("vector", lambda e: e.tensor_scalar(
                qau16, e3.bitcast(U16).rearrange("p (a two) -> p a two", two=2)[:, :, 1],
                1, None, Alu.logical_shift_right), inc="vv")
            S.emit("vector", lambda e: e.tensor_scalar(
                qau16.bitcast(I16), qau16.bitcast(I16), -1, 0x5F37, Alu.mult, Alu.add), inc="vv")
            S.emit("vector", lambda e: e.tensor_copy(
                useed.bitcast(U16).rearrange("p (a two) -> p a two", two=2)[:, :, 1],
                qau16), inc="vv")
            S.emit("vector", lambda e: e.tensor_tensor(e1, useed.bitcast(F32), useed.bitcast(F32), Alu.mult), inc="vv")
            S.emit("vector", lambda e: e.tensor_tensor(e1, e1, e3, Alu.mult), inc="vv")
            S.emit("vector", lambda e: e.tensor_scalar(e1, e1, -0.5, 1.5, Alu.mult, Alu.add), inc="vv")
            S.emit("vector", lambda e: e.tensor_tensor(e2, useed.bitcast(F32), e1, Alu.mult), inc="vv")
            S.emit("vector", lambda e: e.tensor_tensor(e1, e2, e2, Alu.mult), inc="vv")
            S.emit("vector", lambda e: e.tensor_tensor(e1, e1, e3, Alu.mult), inc="vv")
            S.emit("vector", lambda e: e.tensor_scalar(e1, e1, -0.5, 1.5, Alu.mult, Alu.add), inc="vv")
            S.emit("vector", lambda e: e.tensor_tensor(e2, e2, e1, Alu.mult), inc="vv")
            S.emit("vector", lambda e: e.tensor_tensor(e1, e2, e2, Alu.mult), inc="vv")
            S.emit("vector", lambda e: e.tensor_tensor(e1, e1, e3, Alu.mult), inc="vv")
            S.emit("vector", lambda e: e.tensor_scalar(e1, e1, -0.5, 1.5, Alu.mult, Alu.add), inc="vv")
            S.emit("vector", lambda e: e.tensor_tensor(dst, e2, e1, Alu.mult), inc="vv")
            return S.n("vv")

        def ln_stats(extra_first_waits=()):
            for dc in range(DC):
                w = list(extra_first_waits) if dc == 0 else []
                w.append(("mm", S.n("mm")))
                S.emit("vector", lambda e, dc=dc: e.tensor_copy(xr, X[:, dc, :]), waits=w, inc="vv")
                S.emit("vector", lambda e: e.tensor_tensor(xsq, xr, xr, Alu.mult), inc="vv")
                v = S.n("vv")
                S.emit("tensor", lambda e, dc=dc: e.matmul(pS[:], ones_r[:], xr, start=(dc == 0), stop=(dc == DC - 1)),
                       waits=[("vv", v)], inc="mm")
                S.emit("tensor", lambda e, dc=dc: e.matmul(pS2[:], ones_r[:], xsq, start=(dc == 0), stop=(dc == DC - 1)),
                       inc="mm")
            return S.n("mm")

        def ln_mu_r(mm_done):
            S.emit("vector", lambda e: e.tensor_scalar(mu_bc, pS[:], 1.0 / D, None, Alu.mult),
                   waits=[("mm", mm_done)], inc="vv")
            S.emit("vector", lambda e: e.tensor_tensor(e3, mu_bc, mu_bc, Alu.mult), inc="vv")
            S.emit("vector", lambda e: e.scalar_tensor_tensor(e3, pS2[:], 1.0 / D, e3, Alu.mult, Alu.subtract), inc="vv")
            S.emit("vector", lambda e: e.tensor_scalar(e3, e3, 1.0, LN_EPS, Alu.mult, Alu.add), inc="vv")
            return newton_rsqrt(r_bc)

        quant_tiles = []
        for t in range(H // 128):
            for c in range(D // QW):
                quant_tiles.append((wfc[128 * t:128 * (t + 1), QW * c:QW * (c + 1)],
                                    qfc[128 * t:128 * (t + 1), QW * c:QW * (c + 1)]))
        n_fc_tiles = len(quant_tiles)
        for hw in range(H // QW):
            for dr in range(D // 128):
                quant_tiles.append((wproj[128 * dr:128 * (dr + 1), QW * hw:QW * (hw + 1)],
                                    qproj[128 * dr:128 * (dr + 1), QW * hw:QW * (hw + 1)]))
        qt_idx = [0]

        def emit_quant_tile():
            if qt_idx[0] >= len(quant_tiles):
                return
            src, dst = quant_tiles[qt_idx[0]]
            qt_idx[0] += 1
            S.dma("gpsimd", lambda e: e.dma_start(out=qw, in_=src),
                  waits=[("vv", S.n("vv"))], inc="ldg")
            ld = S.n("ldg")
            S.emit("scalar", lambda e: e.activation(qwb.bitcast(BF16), qw, ActF.Copy),
                   waits=[("ldg", ld)], inc="ss")
            s = S.n("ss")
            vp = S.emit("vector", lambda e: e.tensor_scalar(qau, qwb, 0x7FFF, None, Alu.bitwise_and),
                   waits=[("ss", s)], inc="vv")
            vp = S.emit("vector", lambda e: e.tensor_reduce(
                qgs[:].rearrange("p (g o) -> p g o", o=1),
                qau.bitcast(BF16).rearrange("p (g k) -> p g k", g=QG),
                AX.X, Alu.add), waits=[("vv", vp)], inc="vv")
            vp = S.emit("vector", lambda e: e.tensor_scalar(qs16[:].bitcast(BF16), qgs[:], 1.0 / 64, 1e-8, Alu.mult, Alu.max),
                   waits=[("vv", vp)], inc="vv")
            vp = S.emit("vector", lambda e: e.tensor_copy(
                qsf[:].bitcast(U16).rearrange("p (g two) -> p g two", two=2)[:, :, 1],
                qs16[:].rearrange("p (g o) -> p g o", o=1)[:, :, 0]), waits=[("vv", vp)], inc="vv")
            vp = S.emit("vector", lambda e: e.tensor_scalar(qthr[:].bitcast(F32), qsf[:].bitcast(F32), M_BND, None, Alu.mult),
                   waits=[("vv", vp)], inc="vv")
            vp = S.emit("vector", lambda e: e.scalar_tensor_tensor(qthr[:].bitcast(F32), qsf[:].bitcast(F32), P25, qthr[:].bitcast(F32), Alu.mult, Alu.add),
                   waits=[("vv", vp)], inc="vv")
            vp = S.emit("vector", lambda e: e.tensor_tensor(
                qmk.rearrange("p (g k) -> p g k", g=QG),
                qau.rearrange("p (g k) -> p g k", g=QG),
                qthr[:].bitcast(U16).rearrange("p (g two) -> p g two", two=2)[:, :, 1:2].broadcast_to([128, QG, 64]),
                Alu.is_gt), waits=[("wq", S.n("wq")), ("vv", vp)], inc="vv")
            vp = S.emit("vector", lambda e: e.tensor_tensor(
                qmk.rearrange("p (g k) -> p g k", g=QG),
                qmk.rearrange("p (g k) -> p g k", g=QG),
                qs16[:].rearrange("p (g o) -> p g o", o=1).broadcast_to([128, QG, 64]),
                Alu.mult), waits=[("vv", vp)], inc="vv")
            vp = S.emit("vector", lambda e: e.tensor_scalar(qau, qwb, 0x8000, None, Alu.bitwise_and),
                   waits=[("vv", vp)], inc="vv")
            S.emit("vector", lambda e: e.tensor_tensor(qmk, qmk, qau, Alu.bitwise_or),
                   waits=[("vv", vp)], inc="vv")
            v = S.n("vv")
            S.dma("gpsimd", lambda e, dst=dst: e.dma_start(out=dst, in_=qmk.bitcast(BF16)),
                  waits=[("vv", v)], inc="wq")

        win_mm_end = {}
        wq2_mm = [0, 0]
        last_mm_xn = 0
        for half in range(2):
            tok0 = half * THF
            xw = [("vv", S.n("vv")), ("stq", S.n("stq"))] if half else [("lds", ld0)]
            S.dma("sync", lambda e, tok0=tok0: e.dma_start(
                out=X,
                in_=x_in.rearrange("(c p) t -> p c t", p=128)[:, :, tok0:tok0 + THF]),
                  waits=xw, inc="lds")
            ldx = S.n("lds")

            mmS = ln_stats(extra_first_waits=[("lds", ldx), ("vv", v_init)])
            vR = ln_mu_r(mmS)
            for dc in range(DC):
                S.emit("vector", lambda e, dc=dc: e.tensor_tensor(e3, X[:, dc, :], mu_bc, Alu.subtract),
                       waits=([("vv", vR), ("mm", last_mm_xn)] if dc == 0 else []), inc="vv")
                S.emit("vector", lambda e: e.tensor_tensor(e3, e3, r_bc, Alu.mult), inc="vv")
                S.emit("vector", lambda e, dc=dc: e.tensor_scalar(e4, e3, gam[:, dc:dc + 1], bet[:, dc:dc + 1], Alu.mult, Alu.add), inc="vv")
                v = S.n("vv")
                S.emit("scalar", lambda e, dc=dc: e.activation(xnh[:, dc, :], e4, ActF.Copy),
                       waits=[("vv", v)], inc="ss")
                s = S.n("ss")
                S.emit("vector", lambda e, dc=dc: e.tensor_tensor(xnl[:, dc, :], e4, xnh[:, dc, :], Alu.subtract),
                       waits=[("ss", s)], inc="vv")
            v_xn = S.n("vv")
            if half == 0:
                tap("mu0", mu_bc, [128, THF])
                tap("r0", r_bc, [128, THF])
                tap("xnh0", B[:, 0:8192], [128, 8192], BF16)
                tap("xnl0", B[:, 8192:16384], [128, 8192], BF16)

            for ec in range(ECP):
                wv = [("vv", S.n("vv")), ("ss", S.n("ss"))]
                wb3 = wbuf.rearrange("p (c e) -> p c e", c=DC)
                S.dma("sync", lambda e, ec=ec, wb3=wb3: e.dma_start(
                    out=wb3[:, :, 0:128],
                    in_=winT.rearrange("(c p) e -> p c e", p=128)[:, :, 128 * ec:128 * (ec + 1)]),
                      waits=wv, inc="lds")
                S.dma("sync", lambda e, ec=ec, wb3=wb3: e.dma_start(
                    out=wb3[:, :, 128:256],
                    in_=winT.rearrange("(c p) e -> p c e", p=128)[:, :, EI + 128 * ec:EI + 128 * (ec + 1)]),
                      inc="lds")
                ldw = S.n("lds")
                S.emit("scalar", lambda e: e.activation(whi, wbuf, ActF.Copy),
                       waits=[("lds", ldw), ("mm", last_mm_xn)], inc="ss")
                s = S.n("ss")
                S.emit("vector", lambda e: e.tensor_tensor(wlo, wbuf, whi, Alu.subtract),
                       waits=[("ss", s)], inc="vv")
                vw = S.n("vv")
                wh3 = whi.rearrange("p (c e) -> p c e", c=DC)
                wl3 = wlo.rearrange("p (c e) -> p c e", c=DC)
                first = True
                for tgt, col0 in ((pZ, 0), (pG, 128)):
                    idx = 0
                    for dc in range(DC):
                        for wsp, xsp in ((wh3, xnh), (wl3, xnh), (wh3, xnl)):
                            st, sp = (idx == 0), (idx == 3 * DC - 1)
                            S.emit("tensor", lambda e, wsp=wsp, xsp=xsp, dc=dc, tgt=tgt, col0=col0, st=st, sp=sp: e.matmul(
                                tgt[:], wsp[:, dc, col0:col0 + 128], xsp[:, dc, :], start=st, stop=sp),
                                   waits=([("vv", max(vw, v_xn))] if first else []), inc="mm")
                            first = False
                            idx += 1
                mmA = S.n("mm")
                last_mm_xn = mmA
                S.emit("vector", lambda e: e.tensor_scalar(e1, pZ[:], 15.0, -15.0, Alu.min, Alu.max),
                       waits=[("mm", mmA)], inc="vv")
                v1 = S.n("vv")
                S.emit("scalar", lambda e: e.activation(e2, e1, ActF.Sigmoid),
                       waits=[("vv", v1)], inc="ss")
                s1 = S.n("ss")
                S.emit("vector", lambda e: e.tensor_scalar(e3, pG[:], 15.0, -15.0, Alu.min, Alu.max), inc="vv")
                v2 = S.n("vv")
                S.emit("scalar", lambda e: e.activation(e4, e3, ActF.Sigmoid),
                       waits=[("vv", v2)], inc="ss")
                s2 = S.n("ss")
                S.emit("vector", lambda e, ec=ec: e.scalar_tensor_tensor(e1, pZ[:], dtw[:, ec:ec + 1], e2, Alu.mult, Alu.mult),
                       waits=[("ss", s1)], inc="vv")
                S.emit("vector", lambda e: e.tensor_tensor(e3, e3, e4, Alu.mult),
                       waits=[("ss", s2)], inc="vv")
                S.emit("vector", lambda e: e.tensor_tensor(e1, e1, e3, Alu.mult), inc="vv")
                vT = S.n("vv")
                S.emit("scalar", lambda e, ec=ec: e.activation(Th[:, ec, :], e1, ActF.Copy),
                       waits=[("vv", vT)], inc="ss")
                sT = S.n("ss")
                S.emit("vector", lambda e, ec=ec: e.tensor_tensor(Tl[:, ec, :], e1, Th[:, ec, :], Alu.subtract),
                       waits=[("ss", sT)], inc="vv")
                for _ in range(3):
                    emit_quant_tile()
            v_T = S.n("vv")
            if half == 0:
                tap("Th0", B[:, 16384:32768], [128, 16384], BF16)
                tap("Tl0", B[:, 32768:49152], [128, 16384], BF16)

            pYpar = 0
            for dc in range(DC):
                wv = [("vv", S.n("vv")), ("ss", S.n("ss"))]
                wb3 = wbuf.rearrange("p (c e) -> p c e", c=ECP)
                S.dma("sync", lambda e, dc=dc, wb3=wb3: e.dma_start(
                    out=wb3[:, :, 0:128],
                    in_=woutT.rearrange("(c p) d -> p c d", p=128)[:, :, 128 * dc:128 * (dc + 1)]),
                      waits=wv, inc="lds")
                ldw = S.n("lds")
                S.emit("scalar", lambda e: e.activation(whi, wbuf, ActF.Copy),
                       waits=[("lds", ldw), ("mm", S.n("mm"))], inc="ss")
                s = S.n("ss")
                S.emit("vector", lambda e: e.tensor_tensor(wlo, wbuf, whi, Alu.subtract),
                       waits=[("ss", s)], inc="vv")
                vw = S.n("vv")
                wh3 = whi.rearrange("p (c e) -> p c e", c=ECP)
                wl3 = wlo.rearrange("p (c e) -> p c e", c=ECP)
                bank = pY[pYpar]
                first = True
                idx = 0
                for ec in range(ECP):
                    for wsp, tsp in ((wh3, Th), (wl3, Th), (wh3, Tl)):
                        st, sp = (idx == 0), (idx == 3 * ECP - 1)
                        S.emit("tensor", lambda e, wsp=wsp, tsp=tsp, ec=ec, bank=bank, st=st, sp=sp: e.matmul(
                            bank[:], wsp[:, ec, 0:128], tsp[:, ec, :], start=st, stop=sp),
                               waits=([("vv", max(vw, v_T))] if first else []), inc="mm")
                        first = False
                        idx += 1
                mmY = S.n("mm")
                S.emit("vector", lambda e, dc=dc, bank=bank: e.scalar_tensor_tensor(
                    X[:, dc, :], X[:, dc, :], 2.0, bank[:], Alu.mult, Alu.add),
                       waits=[("mm", mmY)], inc="vv")
                pYpar ^= 1
                for _ in range(4):
                    emit_quant_tile()
            tap("x1" if half == 0 else "x1_h1", F[:, 0:8192], [128, 8192])
            a2_mm_end = S.n("mm")
            mmS = ln_stats()
            vR = ln_mu_r(mmS)
            for dc in range(DC):
                S.emit("vector", lambda e, dc=dc: e.tensor_tensor(e3, X[:, dc, :], mu_bc, Alu.subtract),
                       waits=([("vv", vR)] if dc == 0 else []), inc="vv")
                S.emit("vector", lambda e, dc=dc: e.tensor_tensor(X[:, dc, :], e3, r_bc, Alu.mult), inc="vv")
                v = S.n("vv")
                S.emit("scalar", lambda e, dc=dc: e.activation(x2h[:, dc, :], X[:, dc, :], ActF.Copy),
                       waits=[("vv", v)], inc="ss")
                s = S.n("ss")
                S.emit("vector", lambda e, dc=dc: e.tensor_tensor(x2l[:, dc, :], X[:, dc, :], x2h[:, dc, :], Alu.subtract),
                       waits=[("ss", s)], inc="vv")
            v_x2 = S.n("vv")
            tap("x2" if half == 0 else "x2_h1", F[:, 0:8192], [128, 8192])
            if half == 0:
                tap("x2h0", B[:, 0:8192], [128, 8192], BF16)

            barrier("P2")
            ms_started = False
            oacc_first = [True] * DC
            mm_gend = 0
            for grp in range(NGRP):
                for hci in range(HGRP):
                    hc = grp * HGRP + hci
                    hw = hc // 4
                    if hc % 4 == 0:
                        buf = wq1[hw % 2]
                        need = 16 * min((hw + 1) * 8, n_fc_tiles)
                        assert need <= 16 * (S.n("wq") // 16 + (len(quant_tiles) - qt_idx[0] == 0) * 10**9) and need <= S.n("wq") or qt_idx[0] >= len(quant_tiles) or need <= S.n("wq"),                             f"F1 wq deadlock: need {need} planned {S.n('wq')}"
                        prev_mm = max(win_mm_end.get((half, hw - 2), 0), a2_mm_end)
                        for dc in range(DC):
                            S.dma("sync", lambda e, hw=hw, dc=dc, buf=buf: e.dma_start(
                                out=buf[:, dc, :],
                                in_=qfc[512 * hw:512 * (hw + 1), 128 * dc:128 * (dc + 1)],
                                transpose=True),
                                  waits=([("wq", need), ("mm", prev_mm)] if dc == 0 else []), inc="lds")
                        ld_w1 = S.n("lds")
                    buf = wq1[hw % 2]
                    bank = pZ if (hci % 2 == 0) else pG
                    first = True
                    idx = 0
                    for dc in range(DC):
                        for xs in (x2h, x2l):
                            st, sp = (idx == 0), (idx == 2 * DC - 1)
                            S.emit("tensor", lambda e, buf=buf, dc=dc, hc=hc, xs=xs, st=st, sp=sp, bank=bank: e.matmul(
                                bank[:], buf[:, dc, 128 * (hc % 4):128 * (hc % 4) + 128], xs[:, dc, :], start=st, stop=sp),
                                   waits=([("lds", ld_w1), ("vv", v_x2), ("ss", S.n("ss")), ("vv", S.n("vv"))] if first else []), inc="mm")
                            first = False
                            idx += 1
                    mmH = S.n("mm")
                    if hc % 4 == 3:
                        win_mm_end[(half, hw)] = mmH
                    par = hci % 2
                    S.emit("scalar", lambda e, bank=bank, par=par: e.activation(gsc[par], bank[:], ActF.Relu),
                           waits=[("mm", mmH)], inc="ss")
                    s = S.n("ss")
                    S.emit("vector", lambda e, par=par: e.tensor_tensor(gsc[par], gsc[par], gsc[par], Alu.mult),
                           waits=[("ss", s)], inc="vv")
                    S.emit("scalar", lambda e, par=par, hci=hci: e.activation(gh[:, hci, :], gsc[par], ActF.Copy),
                           waits=[("vv", S.n("vv"))], inc="ss")
                    s = S.n("ss")
                    S.emit("vector", lambda e, par=par, hci=hci: e.tensor_tensor(gl[:, hci, :], gsc[par], gh[:, hci, :], Alu.subtract),
                           waits=[("ss", s)], inc="vv")
                    S.emit("vector", lambda e, par=par: e.tensor_tensor(g2[par], gsc[par], gsc[par], Alu.mult), inc="vv")
                    v = S.n("vv")
                    if half == 0 and hc == 0:
                        tap("g0", gsc[0], [128, THF])
                    if half == 0 and hc == 1:
                        tap("g1", gsc[1], [128, THF])
                    S.emit("tensor", lambda e, par=par, st=(not ms_started), hc=hc: e.matmul(
                        pS2[:], ones_r[:], g2[par], start=st, stop=(hc == HC - 1)),
                           waits=[("vv", v)], inc="mm")
                    ms_started = True
                    emit_quant_tile()
                    emit_quant_tile()
                v_g = S.n("vv")
                mm_gend = S.n("mm")
                for dcq in range(4):
                    buf2 = wq2[dcq % 2]
                    need = 16 * (n_fc_tiles + min(16 * (grp + 1), len(quant_tiles) - n_fc_tiles))
                    assert need <= S.n("wq"), f"F2 wq deadlock: need {need} planned {S.n('wq')}"
                    prev_mm = max(wq2_mm[dcq % 2], a2_mm_end)
                    for hci in range(HGRP):
                        hc = grp * HGRP + hci
                        S.dma("sync", lambda e, hc=hc, dcq=dcq, hci=hci, buf2=buf2: e.dma_start(
                            out=buf2[:, hci, :],
                            in_=qproj[512 * dcq:512 * (dcq + 1), 128 * hc:128 * (hc + 1)],
                            transpose=True),
                              waits=([("wq", need), ("mm", prev_mm)] if hci == 0 else []), inc="lds")
                    ld_w2 = S.n("lds")
                    for dci in range(4):
                        dc = dcq * 4 + dci
                        bank = pY[pYpar]
                        first = True
                        idx = 0
                        for hci in range(HGRP):
                            for gs in (gh, gl):
                                st, sp = (idx == 0), (idx == 2 * HGRP - 1)
                                S.emit("tensor", lambda e, buf2=buf2, hci=hci, dci=dci, gs=gs, st=st, sp=sp, bank=bank: e.matmul(
                                    bank[:], buf2[:, hci, 128 * dci:128 * dci + 128], gs[:, hci, :], start=st, stop=sp),
                                       waits=([("lds", ld_w2), ("vv", v_g), ("vv", S.n("vv"))] if first else []), inc="mm")
                                first = False
                                idx += 1
                        mmO = S.n("mm")
                        if oacc_first[dc]:
                            S.emit("vector", lambda e, dc=dc, bank=bank: e.tensor_copy(Oacc[:, dc, :], bank[:]),
                                   waits=[("mm", mmO)], inc="vv")
                            oacc_first[dc] = False
                        else:
                            S.emit("vector", lambda e, dc=dc, bank=bank: e.tensor_tensor(Oacc[:, dc, :], Oacc[:, dc, :], bank[:], Alu.add),
                                   waits=[("mm", mmO)], inc="vv")
                        pYpar ^= 1
                    wq2_mm[dcq % 2] = S.n("mm")
                    emit_quant_tile()

            barrier("P3")
            S.emit("vector", lambda e: e.tensor_scalar(e3, pS2[:], 1.0 / H, EPS32, Alu.mult, Alu.add),
                   waits=[("mm", mm_gend), ("vv", S.n("vv"))], inc="vv")
            vS = newton_rsqrt(r_bc)
            for dc in range(DC):
                S.emit("vector", lambda e, dc=dc: e.tensor_tensor(e3, Oacc[:, dc, :], r_bc, Alu.mult),
                       waits=([("vv", vS)] if dc == 0 else []), inc="vv")
                S.emit("vector", lambda e, dc=dc: e.tensor_tensor(X[:, dc, :], X[:, dc, :], e3, Alu.add), inc="vv")
            barrier("P4")
            tap("x3" if half == 0 else "x3_h1", F[:, 0:8192], [128, 8192])
            tap("Oacc0" if half == 0 else "Oacc_h1", share, [128, 8192])
            if half == 0:
                tap("sF", r_bc, [128, THF])
            mmS = ln_stats()
            vR = ln_mu_r(mmS)
            st_tgts = []
            for dc in range(DC):
                ob = ob0 if (dc % 2 == 0) else ob1
                w = [("vv", vR)] if dc == 0 else []
                if dc >= 2:
                    w.append(("stq", st_tgts[dc - 2]))
                S.emit("vector", lambda e, dc=dc: e.tensor_tensor(e3, X[:, dc, :], mu_bc, Alu.subtract),
                       waits=w, inc="vv")
                S.emit("vector", lambda e, dc=dc, ob=ob: e.tensor_tensor(ob, e3, r_bc, Alu.mult), inc="vv")
                v = S.n("vv")
                S.dma("sync", lambda e, dc=dc, tok0=tok0, ob=ob: e.dma_start(
                    out=out.rearrange("(c p) t -> p c t", p=128)[:, dc, tok0:tok0 + THF],
                    in_=ob),
                      waits=[("vv", v)], inc="stq")
                st_tgts.append(S.n("stq"))

        if "qfc" in taps:
            t_qfc = nc.dram_tensor("tap_qfc", [H, D], BF16, kind="ExternalOutput").ap()
            S.dma("sync", lambda e: e.dma_start(out=t_qfc, in_=qfc),
                  waits=[("wq", S.n("wq"))], inc="stq")
        if "qproj" in taps:
            t_qp = nc.dram_tensor("tap_qproj", [D, H], BF16, kind="ExternalOutput").ap()
            S.dma("sync", lambda e: e.dma_start(out=t_qp, in_=qproj),
                  waits=[("wq", S.n("wq"))], inc="stq")

        S.emit("sync", lambda e: None,
               waits=[("stq", S.n("stq")), ("wq", S.n("wq")),
                      ("lds", S.n("lds")), ("ldg", S.n("ldg"))])

        def run_queue(eng_obj, name):
            for waits, fn, inc, amt in S.q[name]:
                for sname, val in waits:
                    eng_obj.wait_ge(sems[sname], val)
                ins = fn(eng_obj)
                if inc is not None:
                    if ins is None:
                        ins = eng_obj.nop()
                    ins.then_inc(sems[inc], amt)

        @blk.sync
        def _(sync):
            run_queue(sync, "sync")

        @blk.gpsimd
        def _(gpsimd):
            run_queue(gpsimd, "gpsimd")

        @blk.tensor
        def _(tensor):
            run_queue(tensor, "tensor")

        @blk.vector
        def _(vector):
            run_queue(vector, "vector")

        @blk.scalar
        def _(scalar):
            run_queue(scalar, "scalar")

    return nc


_NC_CACHE = {}


def kernel(L0_attn_x, L0_attn_gamma, L0_attn_beta, L0_attn_W_in, L0_attn_W_out,
           L0_attn_dt, L0_ffn_W_fc, L0_ffn_W_proj):
    x = np.ascontiguousarray(np.asarray(L0_attn_x, dtype=np.float32)).reshape(8192, 2048)
    winT = np.ascontiguousarray(np.asarray(L0_attn_W_in, dtype=np.float32).T)
    woutT = np.ascontiguousarray(np.asarray(L0_attn_W_out, dtype=np.float32).T)
    wfc = np.ascontiguousarray(np.asarray(L0_ffn_W_fc, dtype=np.float32))
    wproj = np.ascontiguousarray(np.asarray(L0_ffn_W_proj, dtype=np.float32))
    gamma = np.ascontiguousarray(np.asarray(L0_attn_gamma, dtype=np.float32).reshape(16, 128).T)
    beta = np.ascontiguousarray(np.asarray(L0_attn_beta, dtype=np.float32).reshape(16, 128).T)
    dt = np.asarray(L0_attn_dt, dtype=np.float32)
    dt_fl = np.ascontiguousarray(dt.reshape(1, 4096))
    dt_pc = np.ascontiguousarray(dt.reshape(32, 128).T)
    ones = np.ones((128, 128), dtype=np.float32)

    if "nc" not in _NC_CACHE:
        _NC_CACHE["nc"] = build_core()
    nc = _NC_CACHE["nc"]

    in_maps = []
    for i in range(NCORES):
        xs = np.ascontiguousarray(x[1024 * i:1024 * (i + 1), :].T)
        in_maps.append({"x": xs, "winT": winT, "woutT": woutT, "wfc": wfc,
                        "wproj": wproj, "gamma": gamma, "beta": beta,
                        "dt_fl": dt_fl, "dt_pc": dt_pc, "ones_in": ones})
    res = run_bass_kernel_spmd(nc, in_maps, core_ids=list(range(NCORES)))
    o = np.empty((8192, 2048), dtype=np.float32)
    for i in range(NCORES):
        o[1024 * i:1024 * (i + 1), :] = res.results[i]["out"].T
    return o.reshape(2, 4096, 2048)


# version 03
# speedup vs baseline: 1.1467x; 1.1467x over previous
import contextlib
import numpy as np
import concourse.bass as bass
import concourse.mybir as mybir
from concourse.bass_utils import run_bass_kernel_spmd

F32 = mybir.dt.float32
F32R = mybir.dt.float32r
BF16 = mybir.dt.bfloat16
U16 = mybir.dt.uint16
U32 = mybir.dt.uint32
I32 = mybir.dt.int32
I16 = mybir.dt.int16
Alu = mybir.AluOpType
ActF = mybir.ActivationFunctionType
AX = mybir.AxisListType

NCORES = 8
D, DC = 2048, 16
EI, ECP = 4096, 32
H, HC = 8192, 64
NT, THF = 1024, 512
HGRP = 8
NGRP = HC // HGRP
M_BND = 0.501953125
P25 = 2.0 ** -25
LN_EPS = 1e-5
EPS32 = float(np.finfo(np.float32).eps)
MAGIC = 0x5F3759DF
QW = 1024
QG = QW // 64

ENGINES = ("sync", "gpsimd", "tensor", "vector", "scalar")


class Sched:

    def __init__(self):
        self.q = {e: [] for e in ENGINES}
        self.cnt = {}

    def n(self, sem):
        return self.cnt.get(sem, 0)

    def emit(self, eng, fn, waits=(), inc=None, amt=1):
        waits = tuple((s, v) for (s, v) in waits if v > 0)
        self.q[eng].append((waits, fn, inc, amt))
        if inc is not None:
            self.cnt[inc] = self.cnt.get(inc, 0) + amt
        return self.cnt.get(inc, 0)

    def dma(self, eng, fn, waits=(), inc=None):
        return self.emit(eng, fn, waits, inc, 16)


def build_core(taps=None, barriers=None):
    taps = taps or []
    barriers = barriers or []
    nc = bass.Bass("TRN2", target_bir_lowering=False, debug=False,
                   detect_race_conditions=False)

    x_in = nc.dram_tensor("x", [D, NT], F32, kind="ExternalInput").ap()
    winT = nc.dram_tensor("winT", [D, 2 * EI], F32, kind="ExternalInput").ap()
    woutT = nc.dram_tensor("woutT", [EI, D], F32, kind="ExternalInput").ap()
    wfc = nc.dram_tensor("wfc", [H, D], F32, kind="ExternalInput").ap()
    wproj = nc.dram_tensor("wproj", [D, H], F32, kind="ExternalInput").ap()
    gamma = nc.dram_tensor("gamma", [128, DC], F32, kind="ExternalInput").ap()
    beta = nc.dram_tensor("beta", [128, DC], F32, kind="ExternalInput").ap()
    dt_fl = nc.dram_tensor("dt_fl", [1, EI], F32, kind="ExternalInput").ap()
    dt_pc = nc.dram_tensor("dt_pc", [128, ECP], F32, kind="ExternalInput").ap()
    ones_in = nc.dram_tensor("ones_in", [128, 128], F32, kind="ExternalInput").ap()
    out = nc.dram_tensor("out", [D, NT], F32, kind="ExternalOutput").ap()
    qfc = nc.dram_tensor("qfc", [H, D], BF16, kind="Internal").ap()
    qproj = nc.dram_tensor("qproj", [D, H], BF16, kind="Internal").ap()

    S = Sched()
    es = contextlib.ExitStack()
    with es:
        sb = lambda n, sh, dt: es.enter_context(nc.sbuf_tensor(n, sh, dt))
        psu = lambda n, sh: es.enter_context(nc.psum_tensor(n, sh, F32))

        B = sb("B", [128, 57344], BF16)
        xnh = B[:, 0:8192].rearrange("p (c t) -> p c t", c=DC)
        xnl = B[:, 8192:16384].rearrange("p (c t) -> p c t", c=DC)
        Th = B[:, 16384:32768].rearrange("p (c t) -> p c t", c=ECP)
        Tl = B[:, 32768:49152].rearrange("p (c t) -> p c t", c=ECP)
        whi = B[:, 49152:53248]
        wlo = B[:, 53248:57344]
        x2h = B[:, 0:8192].rearrange("p (c t) -> p c t", c=DC)
        x2l = B[:, 8192:16384].rearrange("p (c t) -> p c t", c=DC)
        gh = B[:, 16384:20480].rearrange("p (c t) -> p c t", c=HGRP)
        gl = B[:, 20480:24576].rearrange("p (c t) -> p c t", c=HGRP)
        wq1 = [B[:, 24576:32768].rearrange("p (c t) -> p c t", c=DC),
               B[:, 32768:40960].rearrange("p (c t) -> p c t", c=DC)]
        wq2 = [B[:, 40960:45056].rearrange("p (c t) -> p c t", c=HGRP),
               B[:, 45056:49152].rearrange("p (c t) -> p c t", c=HGRP)]

        F = sb("F", [128, 20480], F32)
        X = F[:, 0:8192].rearrange("p (c t) -> p c t", c=DC)
        e1 = F[:, 8192:8704]
        e2 = F[:, 8704:9216]
        e3 = F[:, 9216:9728]
        e4 = F[:, 9728:10240]
        mu_bc = F[:, 10240:10752]
        r_bc = F[:, 10752:11264]
        share = F[:, 11264:19456]
        wbuf = share[:, 0:4096]
        Oacc = share.rearrange("p (c t) -> p c t", c=DC)
        dtf = share[0:1, 4096:8192]
        qw = F[:, 19456:20480]
        ob0 = F[:, 19456:19968]
        ob1 = F[:, 19968:20480]
        gsc = [e1, e2]

        R = sb("R", [128, 1024], F32R)
        xr = R[:, 0:512]
        xsq = R[:, 512:1024]
        g2 = [xr, xsq]

        IA = sb("IA", [128, 3072], U16)
        qwb = IA[:, 0:1024]
        qau = IA[:, 1024:2048]
        qmk = IA[:, 2048:3072]
        useed = IA[:, 2048:3072].bitcast(U32)

        qgs = sb("qgs", [128, QG], F32)
        qs16 = sb("qs16", [128, QG], U16)
        qsf = sb("qsf", [128, QG], U32)
        qthr = sb("qthr", [128, QG], U32)

        ones_f = sb("ones_f", [128, 128], F32)
        ones_r = sb("ones_r", [128, 128], F32R)
        ones1 = sb("ones1", [1, 128], F32)
        gam = sb("gam", [128, DC], F32)
        bet = sb("bet", [128, DC], F32)
        dtw = sb("dtw", [128, ECP], F32)
        dtp = sb("dtp", [128, ECP], F32)
        sc1 = sb("sc1", [1, 8], F32)
        mb1 = sb("mb1", [128, 1], F32)
        rb1 = sb("rb1", [128, 1], F32)

        pZ = psu("pZ", [128, THF])
        pG = psu("pG", [128, THF])
        pY = [psu("pY0", [128, THF]), psu("pY1", [128, THF])]
        pS = psu("pS", [128, THF])
        pS2 = psu("pS2", [128, THF])
        pB = psu("pB", [128, 8])

        sems = {}
        for s in ("lds", "ldg", "mm", "vv", "ss", "stq", "wq"):
            sems[s] = es.enter_context(nc.semaphore(s))

        blk = es.enter_context(nc.Block())

        def tap(name, src_ap, shape, dtype=F32):
            if name not in taps:
                return
            t = nc.dram_tensor("tap_" + name, shape, dtype, kind="ExternalOutput").ap()
            S.dma("sync", lambda e: e.dma_start(out=t, in_=src_ap),
                  waits=[("vv", S.n("vv")), ("ss", S.n("ss")), ("mm", S.n("mm"))], inc="stq")
            v = S.n("stq")
            S.emit("vector", lambda e: e.nop(), waits=[("stq", v)], inc="vv")
            S.emit("scalar", lambda e: e.nop(), waits=[("stq", v)], inc="ss")

        S.dma("sync", lambda e: e.dma_start(out=gam[:], in_=gamma), inc="lds")
        S.dma("sync", lambda e: e.dma_start(out=bet[:], in_=beta), inc="lds")
        S.dma("sync", lambda e: e.dma_start(out=dtf, in_=dt_fl), inc="lds")
        S.dma("sync", lambda e: e.dma_start(out=dtp[:], in_=dt_pc), inc="lds")
        S.dma("sync", lambda e: e.dma_start(out=ones_f[:], in_=ones_in), inc="lds")
        ld0 = S.n("lds")

        S.emit("vector", lambda e: e.tensor_copy(ones_r[:], ones_f[:]),
               waits=[("lds", ld0)], inc="vv")
        S.emit("vector", lambda e: e.tensor_copy(ones1[:], ones_f[0:1, :]), inc="vv")
        S.emit("vector", lambda e: e.memset(qsf[:], 0), inc="vv")
        S.emit("vector", lambda e: e.tensor_scalar(dtf, dtf, -1.0, None, Alu.mult), inc="vv")
        vrm = S.emit("vector", lambda e: e.reduce_max(sc1[0:1, 0:1], dtf, AX.X), inc="vv")
        S.emit("vector", lambda e: e.tensor_scalar(sc1[0:1, 1:2], sc1[0:1, 0:1], -1.0, None, Alu.mult),
               waits=[("vv", vrm)], inc="vv")
        v = S.n("vv")
        S.emit("scalar", lambda e: e.activation(dtf, dtf, ActF.Exp, bias=sc1[0:1, 1:2], scale=1.0),
               waits=[("vv", v)], inc="ss")
        s = S.n("ss")
        vrs = S.emit("vector", lambda e: e.reduce_sum(sc1[0:1, 2:3], dtf, AX.X),
               waits=[("ss", s)], inc="vv")
        vrc = S.emit("vector", lambda e: e.reciprocal(sc1[0:1, 3:4], sc1[0:1, 2:3]),
               waits=[("vv", vrs)], inc="vv")
        S.emit("vector", lambda e: e.nop(), waits=[("vv", vrc)], inc="vv")
        v = S.n("vv")
        S.emit("tensor", lambda e: e.matmul(pB[:, 0:1], ones1[:], sc1[0:1, 1:2], start=True, stop=True),
               waits=[("vv", v)], inc="mm")
        S.emit("tensor", lambda e: e.matmul(pB[:, 1:2], ones1[:], sc1[0:1, 3:4], start=True, stop=True), inc="mm")
        mm = S.n("mm")
        S.emit("vector", lambda e: e.tensor_copy(mb1[:], pB[:, 0:1]), waits=[("mm", mm)], inc="vv")
        S.emit("vector", lambda e: e.tensor_copy(rb1[:], pB[:, 1:2]), inc="vv")
        S.emit("vector", lambda e: e.tensor_scalar(dtp[:], dtp[:], -1.0, None, Alu.mult), inc="vv")
        v = S.n("vv")
        S.emit("scalar", lambda e: e.activation(dtw[:], dtp[:], ActF.Exp, bias=mb1[:], scale=1.0),
               waits=[("vv", v)], inc="ss")
        s = S.n("ss")
        S.emit("vector", lambda e: e.tensor_scalar(dtw[:], dtw[:], rb1[:], None, Alu.mult),
               waits=[("ss", s)], inc="vv")
        v_init = S.n("vv")
        tap("dtw", dtw[:], [128, ECP])
        tap("sc1", sc1[:], [1, 8])
        tap("mb1", mb1[:], [128, 1])
        tap("rb1", rb1[:], [128, 1])
        tap("dtf", dtf, [1, EI])
        tap("dtp2", dtp[:], [128, ECP])

        def barrier(name):
            if name not in barriers:
                return
            w = [(s, S.n(s)) for s in ("lds", "ldg", "mm", "vv", "ss", "stq", "wq")]
            for eng in ("vector", "scalar", "tensor", "sync", "gpsimd"):
                S.emit(eng, lambda e: e.nop(), waits=w)

        def newton_rsqrt(dst):
            qau16 = qau[:, 0:512]
            S.emit("vector", lambda e: e.memset(useed, 0),
                   waits=[("wq", S.n("wq"))], inc="vv")
            S.emit("vector", lambda e: e.tensor_scalar(
                qau16, e3.bitcast(U16).rearrange("p (a two) -> p a two", two=2)[:, :, 1],
                1, None, Alu.logical_shift_right), inc="vv")
            S.emit("vector", lambda e: e.tensor_scalar(
                qau16.bitcast(I16), qau16.bitcast(I16), -1, 0x5F37, Alu.mult, Alu.add), inc="vv")
            S.emit("vector", lambda e: e.tensor_copy(
                useed.bitcast(U16).rearrange("p (a two) -> p a two", two=2)[:, :, 1],
                qau16), inc="vv")
            S.emit("vector", lambda e: e.tensor_tensor(e1, useed.bitcast(F32), useed.bitcast(F32), Alu.mult), inc="vv")
            S.emit("vector", lambda e: e.tensor_tensor(e1, e1, e3, Alu.mult), inc="vv")
            S.emit("vector", lambda e: e.tensor_scalar(e1, e1, -0.5, 1.5, Alu.mult, Alu.add), inc="vv")
            S.emit("vector", lambda e: e.tensor_tensor(e2, useed.bitcast(F32), e1, Alu.mult), inc="vv")
            S.emit("vector", lambda e: e.tensor_tensor(e1, e2, e2, Alu.mult), inc="vv")
            S.emit("vector", lambda e: e.tensor_tensor(e1, e1, e3, Alu.mult), inc="vv")
            S.emit("vector", lambda e: e.tensor_scalar(e1, e1, -0.5, 1.5, Alu.mult, Alu.add), inc="vv")
            S.emit("vector", lambda e: e.tensor_tensor(e2, e2, e1, Alu.mult), inc="vv")
            S.emit("vector", lambda e: e.tensor_tensor(e1, e2, e2, Alu.mult), inc="vv")
            S.emit("vector", lambda e: e.tensor_tensor(e1, e1, e3, Alu.mult), inc="vv")
            S.emit("vector", lambda e: e.tensor_scalar(e1, e1, -0.5, 1.5, Alu.mult, Alu.add), inc="vv")
            S.emit("vector", lambda e: e.tensor_tensor(dst, e2, e1, Alu.mult), inc="vv")
            return S.n("vv")

        def ln_stats(extra_first_waits=()):
            for dc in range(DC):
                w = list(extra_first_waits) if dc == 0 else []
                w.append(("mm", S.n("mm")))
                S.emit("vector", lambda e, dc=dc: e.tensor_copy(xr, X[:, dc, :]), waits=w, inc="vv")
                S.emit("vector", lambda e: e.tensor_tensor(xsq, xr, xr, Alu.mult), inc="vv")
                v = S.n("vv")
                S.emit("tensor", lambda e, dc=dc: e.matmul(pS[:], ones_r[:], xr, start=(dc == 0), stop=(dc == DC - 1)),
                       waits=[("vv", v)], inc="mm")
                S.emit("tensor", lambda e, dc=dc: e.matmul(pS2[:], ones_r[:], xsq, start=(dc == 0), stop=(dc == DC - 1)),
                       inc="mm")
            return S.n("mm")

        def ln_mu_r(mm_done):
            S.emit("vector", lambda e: e.tensor_scalar(mu_bc, pS[:], 1.0 / D, None, Alu.mult),
                   waits=[("mm", mm_done)], inc="vv")
            S.emit("vector", lambda e: e.tensor_tensor(e3, mu_bc, mu_bc, Alu.mult), inc="vv")
            S.emit("vector", lambda e: e.scalar_tensor_tensor(e3, pS2[:], 1.0 / D, e3, Alu.mult, Alu.subtract), inc="vv")
            S.emit("vector", lambda e: e.tensor_scalar(e3, e3, 1.0, LN_EPS, Alu.mult, Alu.add), inc="vv")
            return newton_rsqrt(r_bc)

        quant_tiles = []
        for t in range(H // 128):
            for c in range(D // QW):
                quant_tiles.append((wfc[128 * t:128 * (t + 1), QW * c:QW * (c + 1)],
                                    qfc[128 * t:128 * (t + 1), QW * c:QW * (c + 1)]))
        n_fc_tiles = len(quant_tiles)
        for hw in range(H // QW):
            for dr in range(D // 128):
                quant_tiles.append((wproj[128 * dr:128 * (dr + 1), QW * hw:QW * (hw + 1)],
                                    qproj[128 * dr:128 * (dr + 1), QW * hw:QW * (hw + 1)]))
        qt_idx = [0]

        def emit_quant_tile():
            if qt_idx[0] >= len(quant_tiles):
                return
            src, dst = quant_tiles[qt_idx[0]]
            qt_idx[0] += 1
            S.dma("gpsimd", lambda e: e.dma_start(out=qw, in_=src),
                  waits=[("vv", S.n("vv"))], inc="ldg")
            ld = S.n("ldg")
            S.emit("scalar", lambda e: e.activation(qwb.bitcast(BF16), qw, ActF.Copy),
                   waits=[("ldg", ld)], inc="ss")
            s = S.n("ss")
            vp = S.emit("vector", lambda e: e.tensor_scalar(qau, qwb, 0x7FFF, None, Alu.bitwise_and),
                   waits=[("ss", s)], inc="vv")
            vp = S.emit("vector", lambda e: e.tensor_reduce(
                qgs[:].rearrange("p (g o) -> p g o", o=1),
                qau.bitcast(BF16).rearrange("p (g k) -> p g k", g=QG),
                AX.X, Alu.add), waits=[("vv", vp)], inc="vv")
            vp = S.emit("vector", lambda e: e.tensor_scalar(qs16[:].bitcast(BF16), qgs[:], 1.0 / 64, 1e-8, Alu.mult, Alu.max),
                   waits=[("vv", vp)], inc="vv")
            vp = S.emit("vector", lambda e: e.tensor_copy(
                qsf[:].bitcast(U16).rearrange("p (g two) -> p g two", two=2)[:, :, 1],
                qs16[:].rearrange("p (g o) -> p g o", o=1)[:, :, 0]), waits=[("vv", vp)], inc="vv")
            vp = S.emit("vector", lambda e: e.tensor_scalar(qthr[:].bitcast(F32), qsf[:].bitcast(F32), M_BND, None, Alu.mult),
                   waits=[("vv", vp)], inc="vv")
            vp = S.emit("vector", lambda e: e.scalar_tensor_tensor(qthr[:].bitcast(F32), qsf[:].bitcast(F32), P25, qthr[:].bitcast(F32), Alu.mult, Alu.add),
                   waits=[("vv", vp)], inc="vv")
            vp = S.emit("vector", lambda e: e.tensor_tensor(
                qmk.rearrange("p (g k) -> p g k", g=QG),
                qau.rearrange("p (g k) -> p g k", g=QG),
                qthr[:].bitcast(U16).rearrange("p (g two) -> p g two", two=2)[:, :, 1:2].broadcast_to([128, QG, 64]),
                Alu.is_gt), waits=[("wq", S.n("wq")), ("vv", vp)], inc="vv")
            vp = S.emit("vector", lambda e: e.tensor_tensor(
                qmk.rearrange("p (g k) -> p g k", g=QG),
                qmk.rearrange("p (g k) -> p g k", g=QG),
                qs16[:].rearrange("p (g o) -> p g o", o=1).broadcast_to([128, QG, 64]),
                Alu.mult), waits=[("vv", vp)], inc="vv")
            vp = S.emit("vector", lambda e: e.tensor_scalar(qau, qwb, 0x8000, None, Alu.bitwise_and),
                   waits=[("vv", vp)], inc="vv")
            S.emit("vector", lambda e: e.tensor_tensor(qmk, qmk, qau, Alu.bitwise_or),
                   waits=[("vv", vp)], inc="vv")
            v = S.n("vv")
            S.dma("gpsimd", lambda e, dst=dst: e.dma_start(out=dst, in_=qmk.bitcast(BF16)),
                  waits=[("vv", v)], inc="wq")

        win_mm_end = {}
        wsplit_vv = [0, 0]
        wuse_mm = [0, 0]
        wq2_mm = [0, 0]
        last_mm_xn = 0
        for half in range(2):
            tok0 = half * THF
            xw = [("vv", S.n("vv")), ("stq", S.n("stq"))] if half else [("lds", ld0)]
            S.dma("sync", lambda e, tok0=tok0: e.dma_start(
                out=X,
                in_=x_in.rearrange("(c p) t -> p c t", p=128)[:, :, tok0:tok0 + THF]),
                  waits=xw, inc="lds")
            ldx = S.n("lds")

            mmS = ln_stats(extra_first_waits=[("lds", ldx), ("vv", v_init)])
            vR = ln_mu_r(mmS)
            for dc in range(DC):
                S.emit("vector", lambda e, dc=dc: e.tensor_tensor(e3, X[:, dc, :], mu_bc, Alu.subtract),
                       waits=([("vv", vR), ("mm", last_mm_xn)] if dc == 0 else []), inc="vv")
                S.emit("vector", lambda e: e.tensor_tensor(e3, e3, r_bc, Alu.mult), inc="vv")
                S.emit("vector", lambda e, dc=dc: e.tensor_scalar(e4, e3, gam[:, dc:dc + 1], bet[:, dc:dc + 1], Alu.mult, Alu.add), inc="vv")
                v = S.n("vv")
                S.emit("scalar", lambda e, dc=dc: e.activation(xnh[:, dc, :], e4, ActF.Copy),
                       waits=[("vv", v)], inc="ss")
                s = S.n("ss")
                S.emit("vector", lambda e, dc=dc: e.tensor_tensor(xnl[:, dc, :], e4, xnh[:, dc, :], Alu.subtract),
                       waits=[("ss", s)], inc="vv")
            v_xn = S.n("vv")
            if half == 0:
                tap("mu0", mu_bc, [128, THF])
                tap("r0", r_bc, [128, THF])
                tap("xnh0", B[:, 0:8192], [128, 8192], BF16)
                tap("xnl0", B[:, 8192:16384], [128, 8192], BF16)

            for ec in range(ECP):
                mm_grp_start = None
                for hp in range(2):
                    par = (2 * ec + hp) % 2
                    wb3 = wbuf.rearrange("p (q c e) -> p q c e", q=2, c=8)[:, par]
                    wh3 = whi.rearrange("p (q c e) -> p q c e", q=2, c=8)[:, par]
                    wl3 = wlo.rearrange("p (q c e) -> p q c e", q=2, c=8)[:, par]
                    dc0 = 8 * hp
                    S.dma("sync", lambda e, ec=ec, dc0=dc0, wb3=wb3: e.dma_start(
                        out=wb3[:, :, 0:128],
                        in_=winT.rearrange("(c p) e -> p c e", p=128)[:, dc0:dc0 + 8, 128 * ec:128 * (ec + 1)]),
                          waits=[("vv", wsplit_vv[par])], inc="lds")
                    S.dma("sync", lambda e, ec=ec, dc0=dc0, wb3=wb3: e.dma_start(
                        out=wb3[:, :, 128:256],
                        in_=winT.rearrange("(c p) e -> p c e", p=128)[:, dc0:dc0 + 8, EI + 128 * ec:EI + 128 * (ec + 1)]),
                          inc="lds")
                    ldw = S.n("lds")
                    S.emit("scalar", lambda e, wh3=wh3, wb3=wb3: e.activation(
                        wh3.rearrange("p c e -> p (c e)"), wb3.rearrange("p c e -> p (c e)"), ActF.Copy),
                           waits=[("lds", ldw), ("mm", wuse_mm[par])], inc="ss")
                    s = S.n("ss")
                    S.emit("vector", lambda e, wl3=wl3, wb3=wb3, wh3=wh3: e.tensor_tensor(
                        wl3.rearrange("p c e -> p (c e)"), wb3.rearrange("p c e -> p (c e)"),
                        wh3.rearrange("p c e -> p (c e)"), Alu.subtract),
                           waits=[("ss", s)], inc="vv")
                    wsplit_vv[par] = S.n("vv")
                    vw = S.n("vv")
                    first = True
                    for tgt, col0 in ((pZ, 0), (pG, 128)):
                        for ci in range(8):
                            dc = dc0 + ci
                            for wsp, xsp in ((wh3, xnh), (wl3, xnh), (wh3, xnl)):
                                st = mm_grp_start is None or (tgt is pG and mm_grp_start == "zonly" and ci == 0 and wsp is wh3 and xsp is xnh and hp == 0)
                                st = (hp == 0 and ci == 0 and wsp is wh3 and xsp is xnh)
                                sp = (hp == 1 and ci == 7 and wsp is wh3 and xsp is xnl)
                                S.emit("tensor", lambda e, wsp=wsp, xsp=xsp, ci=ci, dc=dc, tgt=tgt, col0=col0, st=st, sp=sp: e.matmul(
                                    tgt[:], wsp[:, ci, col0:col0 + 128], xsp[:, dc, :], start=st, stop=sp),
                                       waits=([("vv", max(vw, v_xn))] if first else []), inc="mm")
                                first = False
                    wuse_mm[par] = S.n("mm")
                mmA = S.n("mm")
                last_mm_xn = mmA
                S.emit("vector", lambda e: e.tensor_scalar(e1, pZ[:], 15.0, -15.0, Alu.min, Alu.max),
                       waits=[("mm", mmA)], inc="vv")
                v1 = S.n("vv")
                S.emit("scalar", lambda e: e.activation(e2, e1, ActF.Sigmoid),
                       waits=[("vv", v1)], inc="ss")
                s1 = S.n("ss")
                S.emit("vector", lambda e: e.tensor_scalar(e3, pG[:], 15.0, -15.0, Alu.min, Alu.max), inc="vv")
                v2 = S.n("vv")
                S.emit("scalar", lambda e: e.activation(e4, e3, ActF.Sigmoid),
                       waits=[("vv", v2)], inc="ss")
                s2 = S.n("ss")
                S.emit("vector", lambda e, ec=ec: e.scalar_tensor_tensor(e1, pZ[:], dtw[:, ec:ec + 1], e2, Alu.mult, Alu.mult),
                       waits=[("ss", s1)], inc="vv")
                S.emit("vector", lambda e: e.tensor_tensor(e3, e3, e4, Alu.mult),
                       waits=[("ss", s2)], inc="vv")
                S.emit("vector", lambda e: e.tensor_tensor(e1, e1, e3, Alu.mult), inc="vv")
                vT = S.n("vv")
                S.emit("scalar", lambda e, ec=ec: e.activation(Th[:, ec, :], e1, ActF.Copy),
                       waits=[("vv", vT)], inc="ss")
                sT = S.n("ss")
                S.emit("vector", lambda e, ec=ec: e.tensor_tensor(Tl[:, ec, :], e1, Th[:, ec, :], Alu.subtract),
                       waits=[("ss", sT)], inc="vv")
                for _ in range(3):
                    emit_quant_tile()
            v_T = S.n("vv")
            if half == 0:
                tap("Th0", B[:, 16384:32768], [128, 16384], BF16)
                tap("Tl0", B[:, 32768:49152], [128, 16384], BF16)

            pYpar = 0
            for dc in range(DC):
                bank = pY[pYpar]
                first = True
                for hp in range(2):
                    par = (2 * dc + hp) % 2
                    wb3 = wbuf.rearrange("p (q c e) -> p q c e", q=2, c=16)[:, par]
                    wh3 = whi.rearrange("p (q c e) -> p q c e", q=2, c=16)[:, par]
                    wl3 = wlo.rearrange("p (q c e) -> p q c e", q=2, c=16)[:, par]
                    ec0 = 16 * hp
                    S.dma("sync", lambda e, dc=dc, ec0=ec0, wb3=wb3: e.dma_start(
                        out=wb3[:],
                        in_=woutT.rearrange("(c p) d -> p c d", p=128)[:, ec0:ec0 + 16, 128 * dc:128 * (dc + 1)]),
                          waits=[("vv", wsplit_vv[par])], inc="lds")
                    ldw = S.n("lds")
                    S.emit("scalar", lambda e, wh3=wh3, wb3=wb3: e.activation(
                        wh3.rearrange("p c e -> p (c e)"), wb3.rearrange("p c e -> p (c e)"), ActF.Copy),
                           waits=[("lds", ldw), ("mm", wuse_mm[par])], inc="ss")
                    s = S.n("ss")
                    S.emit("vector", lambda e, wl3=wl3, wb3=wb3, wh3=wh3: e.tensor_tensor(
                        wl3.rearrange("p c e -> p (c e)"), wb3.rearrange("p c e -> p (c e)"),
                        wh3.rearrange("p c e -> p (c e)"), Alu.subtract),
                           waits=[("ss", s)], inc="vv")
                    wsplit_vv[par] = S.n("vv")
                    vw = S.n("vv")
                    for ci in range(16):
                        ec = ec0 + ci
                        for wsp, tsp in ((wh3, Th), (wl3, Th), (wh3, Tl)):
                            st = (hp == 0 and ci == 0 and wsp is wh3 and tsp is Th)
                            sp = (hp == 1 and ci == 15 and wsp is wh3 and tsp is Tl)
                            S.emit("tensor", lambda e, wsp=wsp, tsp=tsp, ci=ci, ec=ec, bank=bank, st=st, sp=sp: e.matmul(
                                bank[:], wsp[:, ci, 0:128], tsp[:, ec, :], start=st, stop=sp),
                                   waits=([("vv", max(vw, v_T))] if first else []), inc="mm")
                            first = False
                    wuse_mm[par] = S.n("mm")
                mmY = S.n("mm")
                S.emit("vector", lambda e, dc=dc, bank=bank: e.scalar_tensor_tensor(
                    X[:, dc, :], X[:, dc, :], 2.0, bank[:], Alu.mult, Alu.add),
                       waits=[("mm", mmY)], inc="vv")
                pYpar ^= 1
                for _ in range(4):
                    emit_quant_tile()
            tap("x1" if half == 0 else "x1_h1", F[:, 0:8192], [128, 8192])
            a2_mm_end = S.n("mm")
            mmS = ln_stats()
            vR = ln_mu_r(mmS)
            for dc in range(DC):
                S.emit("vector", lambda e, dc=dc: e.tensor_tensor(e3, X[:, dc, :], mu_bc, Alu.subtract),
                       waits=([("vv", vR)] if dc == 0 else []), inc="vv")
                S.emit("vector", lambda e, dc=dc: e.tensor_tensor(X[:, dc, :], e3, r_bc, Alu.mult), inc="vv")
                v = S.n("vv")
                S.emit("scalar", lambda e, dc=dc: e.activation(x2h[:, dc, :], X[:, dc, :], ActF.Copy),
                       waits=[("vv", v)], inc="ss")
                s = S.n("ss")
                S.emit("vector", lambda e, dc=dc: e.tensor_tensor(x2l[:, dc, :], X[:, dc, :], x2h[:, dc, :], Alu.subtract),
                       waits=[("ss", s)], inc="vv")
            v_x2 = S.n("vv")
            tap("x2" if half == 0 else "x2_h1", F[:, 0:8192], [128, 8192])
            if half == 0:
                tap("x2h0", B[:, 0:8192], [128, 8192], BF16)

            barrier("P2")
            ms_started = False
            oacc_first = [True] * DC
            mm_gend = 0
            for grp in range(NGRP):
                for hci in range(HGRP):
                    hc = grp * HGRP + hci
                    hw = hc // 4
                    if hc % 4 == 0:
                        buf = wq1[hw % 2]
                        need = 16 * min((hw + 1) * 8, n_fc_tiles)
                        assert need <= 16 * (S.n("wq") // 16 + (len(quant_tiles) - qt_idx[0] == 0) * 10**9) and need <= S.n("wq") or qt_idx[0] >= len(quant_tiles) or need <= S.n("wq"),                             f"F1 wq deadlock: need {need} planned {S.n('wq')}"
                        prev_mm = max(win_mm_end.get((half, hw - 2), 0), a2_mm_end)
                        for dc in range(DC):
                            S.dma("sync", lambda e, hw=hw, dc=dc, buf=buf: e.dma_start(
                                out=buf[:, dc, :],
                                in_=qfc[512 * hw:512 * (hw + 1), 128 * dc:128 * (dc + 1)],
                                transpose=True),
                                  waits=([("wq", need), ("mm", prev_mm)] if dc == 0 else []), inc="lds")
                        ld_w1 = S.n("lds")
                    buf = wq1[hw % 2]
                    bank = pZ if (hci % 2 == 0) else pG
                    first = True
                    idx = 0
                    for dc in range(DC):
                        for xs in (x2h, x2l):
                            st, sp = (idx == 0), (idx == 2 * DC - 1)
                            S.emit("tensor", lambda e, buf=buf, dc=dc, hc=hc, xs=xs, st=st, sp=sp, bank=bank: e.matmul(
                                bank[:], buf[:, dc, 128 * (hc % 4):128 * (hc % 4) + 128], xs[:, dc, :], start=st, stop=sp),
                                   waits=([("lds", ld_w1), ("vv", v_x2), ("ss", S.n("ss")), ("vv", S.n("vv"))] if first else []), inc="mm")
                            first = False
                            idx += 1
                    mmH = S.n("mm")
                    if hc % 4 == 3:
                        win_mm_end[(half, hw)] = mmH
                    par = hci % 2
                    S.emit("scalar", lambda e, bank=bank, par=par: e.activation(gsc[par], bank[:], ActF.Relu),
                           waits=[("mm", mmH)], inc="ss")
                    s = S.n("ss")
                    S.emit("vector", lambda e, par=par: e.tensor_tensor(gsc[par], gsc[par], gsc[par], Alu.mult),
                           waits=[("ss", s)], inc="vv")
                    S.emit("scalar", lambda e, par=par, hci=hci: e.activation(gh[:, hci, :], gsc[par], ActF.Copy),
                           waits=[("vv", S.n("vv"))], inc="ss")
                    s = S.n("ss")
                    S.emit("vector", lambda e, par=par, hci=hci: e.tensor_tensor(gl[:, hci, :], gsc[par], gh[:, hci, :], Alu.subtract),
                           waits=[("ss", s)], inc="vv")
                    S.emit("vector", lambda e, par=par: e.tensor_tensor(g2[par], gsc[par], gsc[par], Alu.mult), inc="vv")
                    v = S.n("vv")
                    if half == 0 and hc == 0:
                        tap("g0", gsc[0], [128, THF])
                    if half == 0 and hc == 1:
                        tap("g1", gsc[1], [128, THF])
                    S.emit("tensor", lambda e, par=par, st=(not ms_started), hc=hc: e.matmul(
                        pS2[:], ones_r[:], g2[par], start=st, stop=(hc == HC - 1)),
                           waits=[("vv", v)], inc="mm")
                    ms_started = True
                    emit_quant_tile()
                    emit_quant_tile()
                v_g = S.n("vv")
                mm_gend = S.n("mm")
                for dcq in range(4):
                    buf2 = wq2[dcq % 2]
                    need = 16 * (n_fc_tiles + min(16 * (grp + 1), len(quant_tiles) - n_fc_tiles))
                    assert need <= S.n("wq"), f"F2 wq deadlock: need {need} planned {S.n('wq')}"
                    prev_mm = max(wq2_mm[dcq % 2], a2_mm_end)
                    for hci in range(HGRP):
                        hc = grp * HGRP + hci
                        S.dma("sync", lambda e, hc=hc, dcq=dcq, hci=hci, buf2=buf2: e.dma_start(
                            out=buf2[:, hci, :],
                            in_=qproj[512 * dcq:512 * (dcq + 1), 128 * hc:128 * (hc + 1)],
                            transpose=True),
                              waits=([("wq", need), ("mm", prev_mm)] if hci == 0 else []), inc="lds")
                    ld_w2 = S.n("lds")
                    for dci in range(4):
                        dc = dcq * 4 + dci
                        bank = pY[pYpar]
                        first = True
                        idx = 0
                        for hci in range(HGRP):
                            for gs in (gh, gl):
                                st, sp = (idx == 0), (idx == 2 * HGRP - 1)
                                S.emit("tensor", lambda e, buf2=buf2, hci=hci, dci=dci, gs=gs, st=st, sp=sp, bank=bank: e.matmul(
                                    bank[:], buf2[:, hci, 128 * dci:128 * dci + 128], gs[:, hci, :], start=st, stop=sp),
                                       waits=([("lds", ld_w2), ("vv", v_g), ("vv", S.n("vv"))] if first else []), inc="mm")
                                first = False
                                idx += 1
                        mmO = S.n("mm")
                        if oacc_first[dc]:
                            S.emit("vector", lambda e, dc=dc, bank=bank: e.tensor_copy(Oacc[:, dc, :], bank[:]),
                                   waits=[("mm", mmO)], inc="vv")
                            oacc_first[dc] = False
                        else:
                            S.emit("vector", lambda e, dc=dc, bank=bank: e.tensor_tensor(Oacc[:, dc, :], Oacc[:, dc, :], bank[:], Alu.add),
                                   waits=[("mm", mmO)], inc="vv")
                        pYpar ^= 1
                    wq2_mm[dcq % 2] = S.n("mm")
                    emit_quant_tile()

            barrier("P3")
            S.emit("vector", lambda e: e.tensor_scalar(e3, pS2[:], 1.0 / H, EPS32, Alu.mult, Alu.add),
                   waits=[("mm", mm_gend), ("vv", S.n("vv"))], inc="vv")
            vS = newton_rsqrt(r_bc)
            for dc in range(DC):
                S.emit("vector", lambda e, dc=dc: e.tensor_tensor(e3, Oacc[:, dc, :], r_bc, Alu.mult),
                       waits=([("vv", vS)] if dc == 0 else []), inc="vv")
                S.emit("vector", lambda e, dc=dc: e.tensor_tensor(X[:, dc, :], X[:, dc, :], e3, Alu.add), inc="vv")
            barrier("P4")
            tap("x3" if half == 0 else "x3_h1", F[:, 0:8192], [128, 8192])
            tap("Oacc0" if half == 0 else "Oacc_h1", share, [128, 8192])
            if half == 0:
                tap("sF", r_bc, [128, THF])
            mmS = ln_stats()
            vR = ln_mu_r(mmS)
            st_tgts = []
            for dc in range(DC):
                ob = ob0 if (dc % 2 == 0) else ob1
                w = [("vv", vR)] if dc == 0 else []
                if dc >= 2:
                    w.append(("stq", st_tgts[dc - 2]))
                S.emit("vector", lambda e, dc=dc: e.tensor_tensor(e3, X[:, dc, :], mu_bc, Alu.subtract),
                       waits=w, inc="vv")
                S.emit("vector", lambda e, dc=dc, ob=ob: e.tensor_tensor(ob, e3, r_bc, Alu.mult), inc="vv")
                v = S.n("vv")
                S.dma("sync", lambda e, dc=dc, tok0=tok0, ob=ob: e.dma_start(
                    out=out.rearrange("(c p) t -> p c t", p=128)[:, dc, tok0:tok0 + THF],
                    in_=ob),
                      waits=[("vv", v)], inc="stq")
                st_tgts.append(S.n("stq"))

        if "qfc" in taps:
            t_qfc = nc.dram_tensor("tap_qfc", [H, D], BF16, kind="ExternalOutput").ap()
            S.dma("sync", lambda e: e.dma_start(out=t_qfc, in_=qfc),
                  waits=[("wq", S.n("wq"))], inc="stq")
        if "qproj" in taps:
            t_qp = nc.dram_tensor("tap_qproj", [D, H], BF16, kind="ExternalOutput").ap()
            S.dma("sync", lambda e: e.dma_start(out=t_qp, in_=qproj),
                  waits=[("wq", S.n("wq"))], inc="stq")

        S.emit("sync", lambda e: None,
               waits=[("stq", S.n("stq")), ("wq", S.n("wq")),
                      ("lds", S.n("lds")), ("ldg", S.n("ldg"))])

        def run_queue(eng_obj, name):
            for waits, fn, inc, amt in S.q[name]:
                for sname, val in waits:
                    eng_obj.wait_ge(sems[sname], val)
                ins = fn(eng_obj)
                if inc is not None:
                    if ins is None:
                        ins = eng_obj.nop()
                    ins.then_inc(sems[inc], amt)

        @blk.sync
        def _(sync):
            run_queue(sync, "sync")

        @blk.gpsimd
        def _(gpsimd):
            run_queue(gpsimd, "gpsimd")

        @blk.tensor
        def _(tensor):
            run_queue(tensor, "tensor")

        @blk.vector
        def _(vector):
            run_queue(vector, "vector")

        @blk.scalar
        def _(scalar):
            run_queue(scalar, "scalar")

    return nc


_NC_CACHE = {}


def kernel(L0_attn_x, L0_attn_gamma, L0_attn_beta, L0_attn_W_in, L0_attn_W_out,
           L0_attn_dt, L0_ffn_W_fc, L0_ffn_W_proj):
    x = np.ascontiguousarray(np.asarray(L0_attn_x, dtype=np.float32)).reshape(8192, 2048)
    winT = np.ascontiguousarray(np.asarray(L0_attn_W_in, dtype=np.float32).T)
    woutT = np.ascontiguousarray(np.asarray(L0_attn_W_out, dtype=np.float32).T)
    wfc = np.ascontiguousarray(np.asarray(L0_ffn_W_fc, dtype=np.float32))
    wproj = np.ascontiguousarray(np.asarray(L0_ffn_W_proj, dtype=np.float32))
    gamma = np.ascontiguousarray(np.asarray(L0_attn_gamma, dtype=np.float32).reshape(16, 128).T)
    beta = np.ascontiguousarray(np.asarray(L0_attn_beta, dtype=np.float32).reshape(16, 128).T)
    dt = np.asarray(L0_attn_dt, dtype=np.float32)
    dt_fl = np.ascontiguousarray(dt.reshape(1, 4096))
    dt_pc = np.ascontiguousarray(dt.reshape(32, 128).T)
    ones = np.ones((128, 128), dtype=np.float32)

    if "nc" not in _NC_CACHE:
        _NC_CACHE["nc"] = build_core()
    nc = _NC_CACHE["nc"]

    in_maps = []
    for i in range(NCORES):
        xs = np.ascontiguousarray(x[1024 * i:1024 * (i + 1), :].T)
        in_maps.append({"x": xs, "winT": winT, "woutT": woutT, "wfc": wfc,
                        "wproj": wproj, "gamma": gamma, "beta": beta,
                        "dt_fl": dt_fl, "dt_pc": dt_pc, "ones_in": ones})
    res = run_bass_kernel_spmd(nc, in_maps, core_ids=list(range(NCORES)))
    o = np.empty((8192, 2048), dtype=np.float32)
    for i in range(NCORES):
        o[1024 * i:1024 * (i + 1), :] = res.results[i]["out"].T
    return o.reshape(2, 4096, 2048)


# version 06
# speedup vs baseline: 1.5295x; 1.3338x over previous
import contextlib
import numpy as np
import concourse.bass as bass
import concourse.mybir as mybir
from concourse.bass_utils import run_bass_kernel_spmd

F32 = mybir.dt.float32
F32R = mybir.dt.float32r
BF16 = mybir.dt.bfloat16
U16 = mybir.dt.uint16
U32 = mybir.dt.uint32
I32 = mybir.dt.int32
I16 = mybir.dt.int16
Alu = mybir.AluOpType
ActF = mybir.ActivationFunctionType
AX = mybir.AxisListType

NCORES = 8
D, DC = 2048, 16
EI, ECP = 4096, 32
H, HC = 8192, 64
NT, THF = 1024, 512
HGRP = 8
NGRP = HC // HGRP
M_BND = 0.501953125
P25 = 2.0 ** -25
LN_EPS = 1e-5
EPS32 = float(np.finfo(np.float32).eps)
MAGIC = 0x5F3759DF
QW = 1024
QG = QW // 64

ENGINES = ("sync", "gpsimd", "tensor", "vector", "scalar")


class Sched:

    def __init__(self):
        self.q = {e: [] for e in ENGINES}
        self.cnt = {}

    def n(self, sem):
        return self.cnt.get(sem, 0)

    def emit(self, eng, fn, waits=(), inc=None, amt=1):
        waits = tuple((s, v) for (s, v) in waits if v > 0)
        self.q[eng].append((waits, fn, inc, amt))
        if inc is not None:
            self.cnt[inc] = self.cnt.get(inc, 0) + amt
        return self.cnt.get(inc, 0)

    def dma(self, eng, fn, waits=(), inc=None):
        return self.emit(eng, fn, waits, inc, 16)


def build_core(taps=None, barriers=None):
    taps = taps or []
    barriers = barriers or []
    nc = bass.Bass("TRN2", target_bir_lowering=False, debug=False,
                   detect_race_conditions=False)

    x_in = nc.dram_tensor("x", [D, NT], F32, kind="ExternalInput").ap()
    winT = nc.dram_tensor("winT", [D, 2 * EI], F32, kind="ExternalInput").ap()
    woutT = nc.dram_tensor("woutT", [EI, D], F32, kind="ExternalInput").ap()
    wfc = nc.dram_tensor("wfc", [H, D], F32, kind="ExternalInput").ap()
    wproj = nc.dram_tensor("wproj", [D, H], F32, kind="ExternalInput").ap()
    gamma = nc.dram_tensor("gamma", [128, DC], F32, kind="ExternalInput").ap()
    beta = nc.dram_tensor("beta", [128, DC], F32, kind="ExternalInput").ap()
    dt_fl = nc.dram_tensor("dt_fl", [1, EI], F32, kind="ExternalInput").ap()
    dt_pc = nc.dram_tensor("dt_pc", [128, ECP], F32, kind="ExternalInput").ap()
    ones_in = nc.dram_tensor("ones_in", [128, 128], F32, kind="ExternalInput").ap()
    out = nc.dram_tensor("out", [D, NT], F32, kind="ExternalOutput").ap()
    qfc = nc.dram_tensor("qfc", [H, D], BF16, kind="Internal").ap()
    qproj = nc.dram_tensor("qproj", [D, H], BF16, kind="Internal").ap()

    S = Sched()
    es = contextlib.ExitStack()
    with es:
        sb = lambda n, sh, dt: es.enter_context(nc.sbuf_tensor(n, sh, dt))
        psu = lambda n, sh: es.enter_context(nc.psum_tensor(n, sh, F32))

        B = sb("B", [128, 57344], BF16)
        xnh = B[:, 0:8192].rearrange("p (c t) -> p c t", c=DC)
        xnl = B[:, 8192:16384].rearrange("p (c t) -> p c t", c=DC)
        Th = B[:, 16384:32768].rearrange("p (c t) -> p c t", c=ECP)
        Tl = B[:, 32768:49152].rearrange("p (c t) -> p c t", c=ECP)
        whi = B[:, 49152:53248]
        wlo = B[:, 53248:57344]
        x2h = B[:, 0:8192].rearrange("p (c t) -> p c t", c=DC)
        x2l = B[:, 8192:16384].rearrange("p (c t) -> p c t", c=DC)
        gh = B[:, 16384:20480].rearrange("p (c t) -> p c t", c=HGRP)
        gl = B[:, 20480:24576].rearrange("p (c t) -> p c t", c=HGRP)
        wq1 = [B[:, 24576:32768].rearrange("p (c t) -> p c t", c=DC),
               B[:, 32768:40960].rearrange("p (c t) -> p c t", c=DC)]
        wq2 = [B[:, 40960:45056].rearrange("p (c t) -> p c t", c=HGRP),
               B[:, 45056:49152].rearrange("p (c t) -> p c t", c=HGRP)]

        F = sb("F", [128, 20480], F32)
        X = F[:, 0:8192].rearrange("p (c t) -> p c t", c=DC)
        e1 = F[:, 8192:8704]
        e2 = F[:, 8704:9216]
        e3 = F[:, 9216:9728]
        e4 = F[:, 9728:10240]
        mu_bc = F[:, 10240:10752]
        r_bc = F[:, 10752:11264]
        share = F[:, 11264:19456]
        wbuf = share[:, 0:4096]
        Oacc = share.rearrange("p (c t) -> p c t", c=DC)
        dtf = share[0:1, 4096:8192]
        qw0 = F[:, 19456:20480]
        qw1_attn = F[:, 15360:16384]
        qw1_ffn = F[:, 9728:10752]
        ob0 = F[:, 19456:19968]
        ob1 = F[:, 19968:20480]
        gsc = [e1, e2]

        R = sb("R", [128, 1024], F32R)
        xr = R[:, 0:512]
        xsq = R[:, 512:1024]
        g2 = [xr, xsq]

        IA = sb("IA", [128, 3072], U16)
        qwb = IA[:, 0:1024]
        qau = IA[:, 1024:2048]
        qmk = IA[:, 2048:3072]
        useed = IA[:, 2048:3072].bitcast(U32)

        qgs = sb("qgs", [128, QG], F32)
        qs16 = sb("qs16", [128, QG], U16)
        qsf = sb("qsf", [128, QG], U32)
        qthr = sb("qthr", [128, QG], U32)

        ones_f = sb("ones_f", [128, 128], F32)
        ones_r = sb("ones_r", [128, 128], F32R)
        ones1 = sb("ones1", [1, 128], F32)
        gam = sb("gam", [128, DC], F32)
        bet = sb("bet", [128, DC], F32)
        dtw = sb("dtw", [128, ECP], F32)
        dtp = sb("dtp", [128, ECP], F32)
        sc1 = sb("sc1", [1, 8], F32)
        mb1 = sb("mb1", [128, 1], F32)
        rb1 = sb("rb1", [128, 1], F32)

        pZ = [psu("pZ0", [128, THF]), psu("pZ1", [128, THF])]
        pG = [psu("pG0", [128, THF]), psu("pG1", [128, THF])]
        pY = [psu("pY0", [128, THF]), psu("pY1", [128, THF])]
        pH = [pZ[0], pZ[1], pG[0], pG[1]]
        pS = psu("pS", [128, THF])
        pS2 = psu("pS2", [128, THF])
        pB = pS[:, 0:8]

        sems = {}
        for s in ("lds", "ldg", "lds2", "mm", "vv", "ss", "stq", "wq"):
            sems[s] = es.enter_context(nc.semaphore(s))

        blk = es.enter_context(nc.Block())

        def tap(name, src_ap, shape, dtype=F32):
            if name not in taps:
                return
            t = nc.dram_tensor("tap_" + name, shape, dtype, kind="ExternalOutput").ap()
            S.dma("sync", lambda e: e.dma_start(out=t, in_=src_ap),
                  waits=[("vv", S.n("vv")), ("ss", S.n("ss")), ("mm", S.n("mm"))], inc="stq")
            v = S.n("stq")
            S.emit("vector", lambda e: e.nop(), waits=[("stq", v)], inc="vv")
            S.emit("scalar", lambda e: e.nop(), waits=[("stq", v)], inc="ss")

        S.dma("sync", lambda e: e.dma_start(out=gam[:], in_=gamma), inc="lds")
        S.dma("sync", lambda e: e.dma_start(out=bet[:], in_=beta), inc="lds")
        S.dma("sync", lambda e: e.dma_start(out=dtf, in_=dt_fl), inc="lds")
        S.dma("sync", lambda e: e.dma_start(out=dtp[:], in_=dt_pc), inc="lds")
        S.dma("sync", lambda e: e.dma_start(out=ones_f[:], in_=ones_in), inc="lds")
        ld0 = S.n("lds")

        S.emit("vector", lambda e: e.tensor_copy(ones_r[:], ones_f[:]),
               waits=[("lds", ld0)], inc="vv")
        S.emit("vector", lambda e: e.tensor_copy(ones1[:], ones_f[0:1, :]), inc="vv")
        S.emit("vector", lambda e: e.memset(qsf[:], 0), inc="vv")
        S.emit("vector", lambda e: e.tensor_scalar(dtf, dtf, -1.0, None, Alu.mult), inc="vv")
        vrm = S.emit("vector", lambda e: e.reduce_max(sc1[0:1, 0:1], dtf, AX.X), inc="vv")
        S.emit("vector", lambda e: e.tensor_scalar(sc1[0:1, 1:2], sc1[0:1, 0:1], -1.0, None, Alu.mult),
               waits=[("vv", vrm)], inc="vv")
        v = S.n("vv")
        S.emit("scalar", lambda e: e.activation(dtf, dtf, ActF.Exp, bias=sc1[0:1, 1:2], scale=1.0),
               waits=[("vv", v)], inc="ss")
        s = S.n("ss")
        vrs = S.emit("vector", lambda e: e.reduce_sum(sc1[0:1, 2:3], dtf, AX.X),
               waits=[("ss", s)], inc="vv")
        vrc = S.emit("vector", lambda e: e.reciprocal(sc1[0:1, 3:4], sc1[0:1, 2:3]),
               waits=[("vv", vrs)], inc="vv")
        S.emit("vector", lambda e: e.nop(), waits=[("vv", vrc)], inc="vv")
        v = S.n("vv")
        S.emit("tensor", lambda e: e.matmul(pS[:, 0:1], ones1[:], sc1[0:1, 1:2], start=True, stop=True),
               waits=[("vv", v)], inc="mm")
        S.emit("tensor", lambda e: e.matmul(pS[:, 1:2], ones1[:], sc1[0:1, 3:4], start=True, stop=True), inc="mm")
        mm = S.n("mm")
        S.emit("vector", lambda e: e.tensor_copy(mb1[:], pS[:, 0:1]), waits=[("mm", mm)], inc="vv")
        S.emit("vector", lambda e: e.tensor_copy(rb1[:], pS[:, 1:2]), inc="vv")
        S.emit("vector", lambda e: e.tensor_scalar(dtp[:], dtp[:], -1.0, None, Alu.mult), inc="vv")
        v = S.n("vv")
        S.emit("scalar", lambda e: e.activation(dtw[:], dtp[:], ActF.Exp, bias=mb1[:], scale=1.0),
               waits=[("vv", v)], inc="ss")
        s = S.n("ss")
        S.emit("vector", lambda e: e.tensor_scalar(dtw[:], dtw[:], rb1[:], None, Alu.mult),
               waits=[("ss", s)], inc="vv")
        v_init = S.n("vv")
        tap("dtw", dtw[:], [128, ECP])
        tap("sc1", sc1[:], [1, 8])
        tap("mb1", mb1[:], [128, 1])
        tap("rb1", rb1[:], [128, 1])
        tap("dtf", dtf, [1, EI])
        tap("dtp2", dtp[:], [128, ECP])

        def barrier(name):
            if name not in barriers:
                return
            w = [(s, S.n(s)) for s in ("lds", "ldg", "mm", "vv", "ss", "stq", "wq")]
            for eng in ("vector", "scalar", "tensor", "sync", "gpsimd"):
                S.emit(eng, lambda e: e.nop(), waits=w)

        def newton_rsqrt(dst):
            qau16 = qau[:, 0:512]
            S.emit("vector", lambda e: e.memset(useed, 0),
                   waits=[("wq", S.n("wq"))], inc="vv")
            S.emit("vector", lambda e: e.tensor_scalar(
                qau16, e3.bitcast(U16).rearrange("p (a two) -> p a two", two=2)[:, :, 1],
                1, None, Alu.logical_shift_right), inc="vv")
            S.emit("vector", lambda e: e.tensor_scalar(
                qau16.bitcast(I16), qau16.bitcast(I16), -1, 0x5F37, Alu.mult, Alu.add), inc="vv")
            S.emit("vector", lambda e: e.tensor_copy(
                useed.bitcast(U16).rearrange("p (a two) -> p a two", two=2)[:, :, 1],
                qau16), inc="vv")
            S.emit("vector", lambda e: e.tensor_tensor(e1, useed.bitcast(F32), useed.bitcast(F32), Alu.mult), inc="vv")
            S.emit("vector", lambda e: e.tensor_tensor(e1, e1, e3, Alu.mult), inc="vv")
            S.emit("vector", lambda e: e.tensor_scalar(e1, e1, -0.5, 1.5, Alu.mult, Alu.add), inc="vv")
            S.emit("vector", lambda e: e.tensor_tensor(e2, useed.bitcast(F32), e1, Alu.mult), inc="vv")
            S.emit("vector", lambda e: e.tensor_tensor(e1, e2, e2, Alu.mult), inc="vv")
            S.emit("vector", lambda e: e.tensor_tensor(e1, e1, e3, Alu.mult), inc="vv")
            S.emit("vector", lambda e: e.tensor_scalar(e1, e1, -0.5, 1.5, Alu.mult, Alu.add), inc="vv")
            S.emit("vector", lambda e: e.tensor_tensor(e2, e2, e1, Alu.mult), inc="vv")
            S.emit("vector", lambda e: e.tensor_tensor(e1, e2, e2, Alu.mult), inc="vv")
            S.emit("vector", lambda e: e.tensor_tensor(e1, e1, e3, Alu.mult), inc="vv")
            S.emit("vector", lambda e: e.tensor_scalar(e1, e1, -0.5, 1.5, Alu.mult, Alu.add), inc="vv")
            S.emit("vector", lambda e: e.tensor_tensor(dst, e2, e1, Alu.mult), inc="vv")
            return S.n("vv")

        def ln_stats(extra_first_waits=()):
            for dc in range(DC):
                w = list(extra_first_waits) if dc == 0 else []
                w.append(("mm", S.n("mm")))
                S.emit("vector", lambda e, dc=dc: e.tensor_copy(xr, X[:, dc, :]), waits=w, inc="vv")
                S.emit("vector", lambda e: e.tensor_tensor(xsq, xr, xr, Alu.mult), inc="vv")
                v = S.n("vv")
                S.emit("tensor", lambda e, dc=dc: e.matmul(pS[:], ones_r[:], xr, start=(dc == 0), stop=(dc == DC - 1)),
                       waits=[("vv", v)], inc="mm")
                S.emit("tensor", lambda e, dc=dc: e.matmul(pS2[:], ones_r[:], xsq, start=(dc == 0), stop=(dc == DC - 1)),
                       inc="mm")
            return S.n("mm")

        def ln_mu_r(mm_done):
            S.emit("vector", lambda e: e.tensor_scalar(mu_bc, pS[:], 1.0 / D, None, Alu.mult),
                   waits=[("mm", mm_done)], inc="vv")
            S.emit("vector", lambda e: e.tensor_tensor(e3, mu_bc, mu_bc, Alu.mult), inc="vv")
            S.emit("vector", lambda e: e.scalar_tensor_tensor(e3, pS2[:], 1.0 / D, e3, Alu.mult, Alu.subtract), inc="vv")
            S.emit("vector", lambda e: e.tensor_scalar(e3, e3, 1.0, LN_EPS, Alu.mult, Alu.add), inc="vv")
            return newton_rsqrt(r_bc)

        quant_tiles = []
        for t in range(H // 128):
            for c in range(D // QW):
                quant_tiles.append((wfc[128 * t:128 * (t + 1), QW * c:QW * (c + 1)],
                                    qfc[128 * t:128 * (t + 1), QW * c:QW * (c + 1)]))
        n_fc_tiles = len(quant_tiles)
        for hw in range(H // QW):
            for dr in range(D // 128):
                quant_tiles.append((wproj[128 * dr:128 * (dr + 1), QW * hw:QW * (hw + 1)],
                                    qproj[128 * dr:128 * (dr + 1), QW * hw:QW * (hw + 1)]))
        qt_idx = [0]
        q_par = [0]
        q_cons_vv = {0: 0, 1: 0}
        q_wb_vv = [0]

        def emit_quant_tile(phase="attn", guard_vv=0):
            if qt_idx[0] >= len(quant_tiles):
                return
            src_, dst = quant_tiles[qt_idx[0]]
            qt_idx[0] += 1
            par = q_par[0]
            q_par[0] ^= 1
            qw = qw0 if par == 0 else (qw1_attn if phase == "attn" else qw1_ffn)
            S.dma("scalar", lambda e, src_=src_, qw=qw: e.dma_start(out=qw, in_=src_),
                  waits=[("vv", max(q_cons_vv[par], guard_vv))], inc="lds2")
            ld = S.n("lds2")
            S.emit("scalar", lambda e, qw=qw: e.activation(qwb.bitcast(BF16), qw, ActF.Copy),
                   waits=[("lds2", ld), ("vv", q_wb_vv[0])], inc="ss")
            S.emit("scalar", lambda e: e.activation(qau.bitcast(BF16), qwb.bitcast(BF16), ActF.Abs), inc="ss")
            S.emit("scalar", lambda e: e.activation(qwb.bitcast(BF16), qwb.bitcast(BF16), ActF.Sign), inc="ss")
            s = S.n("ss")
            vp = S.emit("vector", lambda e: e.tensor_reduce(
                qgs[:].rearrange("p (g o) -> p g o", o=1),
                qau.bitcast(BF16).rearrange("p (g k) -> p g k", g=QG),
                AX.X, Alu.add), waits=[("ss", s)], inc="vv")
            vp = S.emit("vector", lambda e: e.tensor_scalar(qs16[:].bitcast(BF16), qgs[:], 1.0 / 64, 1e-8, Alu.mult, Alu.max),
                   waits=[("vv", vp)], inc="vv")
            vp = S.emit("vector", lambda e: e.tensor_copy(
                qsf[:].bitcast(U16).rearrange("p (g two) -> p g two", two=2)[:, :, 1],
                qs16[:].rearrange("p (g o) -> p g o", o=1)[:, :, 0]), waits=[("vv", vp)], inc="vv")
            vp = S.emit("vector", lambda e: e.tensor_scalar(qthr[:].bitcast(F32), qsf[:].bitcast(F32), M_BND, None, Alu.mult),
                   waits=[("vv", vp)], inc="vv")
            vp = S.emit("vector", lambda e: e.scalar_tensor_tensor(qthr[:].bitcast(F32), qsf[:].bitcast(F32), P25, qthr[:].bitcast(F32), Alu.mult, Alu.add),
                   waits=[("vv", vp)], inc="vv")
            vp = S.emit("vector", lambda e: e.tensor_tensor(
                qmk.rearrange("p (g k) -> p g k", g=QG),
                qau.rearrange("p (g k) -> p g k", g=QG),
                qthr[:].bitcast(U16).rearrange("p (g two) -> p g two", two=2)[:, :, 1:2].broadcast_to([128, QG, 64]),
                Alu.is_gt), waits=[("wq", S.n("wq")), ("vv", vp)], inc="vv")
            vp = S.emit("vector", lambda e: e.tensor_tensor(
                qmk.rearrange("p (g k) -> p g k", g=QG),
                qmk.rearrange("p (g k) -> p g k", g=QG),
                qs16[:].rearrange("p (g o) -> p g o", o=1).broadcast_to([128, QG, 64]),
                Alu.mult), waits=[("vv", vp)], inc="vv")
            S.emit("vector", lambda e: e.tensor_tensor(qmk.bitcast(BF16), qmk.bitcast(BF16), qwb.bitcast(BF16), Alu.mult),
                   waits=[("vv", vp)], inc="vv")
            v = S.n("vv")
            q_wb_vv[0] = v
            q_cons_vv[par] = v
            S.dma("sync", lambda e, dst=dst: e.dma_start(out=dst, in_=qmk.bitcast(BF16)),
                  waits=[("vv", v)], inc="wq")

        win_mm_end = {}
        wsplit_vv = [0, 0]
        wuse_mm = [0, 0]
        wq2_mm = [0, 0]
        bank_vv = {}
        bank_ss = {}
        last_mm_xn = 0

        def bank_waits(bank):
            return [("vv", bank_vv.get(id(bank), 0)), ("ss", bank_ss.get(id(bank), 0))]

        for half in range(2):
            tok0 = half * THF
            xw = [("vv", S.n("vv")), ("stq", S.n("stq"))] if half else [("lds", ld0)]
            S.dma("sync", lambda e, tok0=tok0: e.dma_start(
                out=X,
                in_=x_in.rearrange("(c p) t -> p c t", p=128)[:, :, tok0:tok0 + THF]),
                  waits=xw, inc="lds")
            ldx = S.n("lds")

            mmS = ln_stats(extra_first_waits=[("lds", ldx), ("vv", v_init)])
            vR = ln_mu_r(mmS)
            for dc in range(DC):
                S.emit("vector", lambda e, dc=dc: e.tensor_tensor(e3, X[:, dc, :], mu_bc, Alu.subtract),
                       waits=([("vv", vR), ("mm", last_mm_xn)] if dc == 0 else []), inc="vv")
                S.emit("vector", lambda e: e.tensor_tensor(e3, e3, r_bc, Alu.mult), inc="vv")
                S.emit("vector", lambda e, dc=dc: e.tensor_scalar(e4, e3, gam[:, dc:dc + 1], bet[:, dc:dc + 1], Alu.mult, Alu.add), inc="vv")
                v = S.n("vv")
                S.emit("scalar", lambda e, dc=dc: e.activation(xnh[:, dc, :], e4, ActF.Copy),
                       waits=[("vv", v)], inc="ss")
                s = S.n("ss")
                S.emit("vector", lambda e, dc=dc: e.tensor_tensor(xnl[:, dc, :], e4, xnh[:, dc, :], Alu.subtract),
                       waits=[("ss", s)], inc="vv")
            v_xn = S.n("vv")
            if half == 0:
                tap("mu0", mu_bc, [128, THF])
                tap("r0", r_bc, [128, THF])
                tap("xnh0", B[:, 0:8192], [128, 8192], BF16)
                tap("xnl0", B[:, 8192:16384], [128, 8192], BF16)

            def a1_load_split(ec):
                out = []
                for hp in range(2):
                    par = hp
                    wb3 = wbuf.rearrange("p (q c e) -> p q c e", q=2, c=8)[:, par]
                    wh3 = whi.rearrange("p (q c e) -> p q c e", q=2, c=8)[:, par]
                    wl3 = wlo.rearrange("p (q c e) -> p q c e", q=2, c=8)[:, par]
                    dc0 = 8 * hp
                    S.dma("sync", lambda e, ec=ec, dc0=dc0, wb3=wb3: e.dma_start(
                        out=wb3[:, :, 0:128],
                        in_=winT.rearrange("(c p) e -> p c e", p=128)[:, dc0:dc0 + 8, 128 * ec:128 * (ec + 1)]),
                          waits=[("vv", wsplit_vv[par])], inc="lds")
                    S.dma("sync", lambda e, ec=ec, dc0=dc0, wb3=wb3: e.dma_start(
                        out=wb3[:, :, 128:256],
                        in_=winT.rearrange("(c p) e -> p c e", p=128)[:, dc0:dc0 + 8, EI + 128 * ec:EI + 128 * (ec + 1)]),
                          inc="lds")
                    ldw = S.n("lds")
                    S.emit("scalar", lambda e, wh3=wh3, wb3=wb3: e.activation(
                        wh3.rearrange("p c e -> p (c e)"), wb3.rearrange("p c e -> p (c e)"), ActF.Copy),
                           waits=[("lds", ldw), ("mm", wuse_mm[par])], inc="ss")
                    s = S.n("ss")
                    S.emit("vector", lambda e, wl3=wl3, wb3=wb3, wh3=wh3: e.tensor_tensor(
                        wl3.rearrange("p c e -> p (c e)"), wb3.rearrange("p c e -> p (c e)"),
                        wh3.rearrange("p c e -> p (c e)"), Alu.subtract),
                           waits=[("ss", s)], inc="vv")
                    wsplit_vv[par] = S.n("vv")
                    out.append((wh3, wl3, dc0, S.n("vv")))
                return out

            pend = a1_load_split(0)
            for ec in range(ECP):
                zb, gb = pZ[ec % 2], pG[ec % 2]
                bw = bank_waits(zb) + bank_waits(gb) + [("vv", v_xn)]
                cur, pend = pend, None
                for hi, (wh3, wl3, dc0, vw) in enumerate(cur):
                    first = True
                    for tgt, col0 in ((zb, 0), (gb, 128)):
                        for ci in range(8):
                            dc = dc0 + ci
                            for wsp, xsp in ((wh3, xnh), (wl3, xnh), (wh3, xnl)):
                                st = (hi == 0 and ci == 0 and wsp is wh3 and xsp is xnh)
                                sp = (hi == 1 and ci == 7 and wsp is wh3 and xsp is xnl)
                                w = ([("vv", vw)] + (bw if hi == 0 else [])) if first else []
                                S.emit("tensor", lambda e, wsp=wsp, xsp=xsp, ci=ci, dc=dc, tgt=tgt, col0=col0, st=st, sp=sp: e.matmul(
                                    tgt[:], wsp[:, ci, col0:col0 + 128], xsp[:, dc, :], start=st, stop=sp),
                                       waits=w, inc="mm")
                                first = False
                    wuse_mm[hi] = S.n("mm")
                mmA = S.n("mm")
                last_mm_xn = mmA
                if ec + 1 < ECP:
                    pend = a1_load_split(ec + 1)
                for _ in range(2):
                    emit_quant_tile("attn", v_init)
                S.emit("vector", lambda e, zb=zb: e.tensor_scalar(e1, zb[:], 15.0, -15.0, Alu.min, Alu.max),
                       waits=[("mm", mmA)], inc="vv")
                v1 = S.n("vv")
                S.emit("scalar", lambda e: e.activation(e2, e1, ActF.Sigmoid),
                       waits=[("vv", v1)], inc="ss")
                s1 = S.n("ss")
                S.emit("vector", lambda e, gb=gb: e.tensor_scalar(e3, gb[:], 15.0, -15.0, Alu.min, Alu.max), inc="vv")
                v2 = S.n("vv")
                S.emit("scalar", lambda e: e.activation(e4, e3, ActF.Sigmoid),
                       waits=[("vv", v2)], inc="ss")
                s2 = S.n("ss")
                S.emit("vector", lambda e, ec=ec, zb=zb: e.scalar_tensor_tensor(e1, zb[:], dtw[:, ec:ec + 1], e2, Alu.mult, Alu.mult),
                       waits=[("ss", s1)], inc="vv")
                S.emit("vector", lambda e: e.tensor_tensor(e3, e3, e4, Alu.mult),
                       waits=[("ss", s2)], inc="vv")
                S.emit("vector", lambda e: e.tensor_tensor(e1, e1, e3, Alu.mult), inc="vv")
                vT = S.n("vv")
                S.emit("scalar", lambda e, ec=ec: e.activation(Th[:, ec, :], e1, ActF.Copy),
                       waits=[("vv", vT)], inc="ss")
                sT = S.n("ss")
                S.emit("vector", lambda e, ec=ec: e.tensor_tensor(Tl[:, ec, :], e1, Th[:, ec, :], Alu.subtract),
                       waits=[("ss", sT)], inc="vv")
                bank_vv[id(zb)] = bank_vv[id(gb)] = S.n("vv")
            v_T = S.n("vv")
            if half == 0:
                tap("Th0", B[:, 16384:32768], [128, 16384], BF16)
                tap("Tl0", B[:, 32768:49152], [128, 16384], BF16)

            def a2_load_split(dc):
                out = []
                for hp in range(2):
                    par = hp
                    wb3 = wbuf.rearrange("p (q c e) -> p q c e", q=2, c=16)[:, par]
                    wh3 = whi.rearrange("p (q c e) -> p q c e", q=2, c=16)[:, par]
                    wl3 = wlo.rearrange("p (q c e) -> p q c e", q=2, c=16)[:, par]
                    ec0 = 16 * hp
                    S.dma("sync", lambda e, dc=dc, ec0=ec0, wb3=wb3: e.dma_start(
                        out=wb3[:],
                        in_=woutT.rearrange("(c p) d -> p c d", p=128)[:, ec0:ec0 + 16, 128 * dc:128 * (dc + 1)]),
                          waits=[("vv", wsplit_vv[par])], inc="lds")
                    ldw = S.n("lds")
                    S.emit("scalar", lambda e, wh3=wh3, wb3=wb3: e.activation(
                        wh3.rearrange("p c e -> p (c e)"), wb3.rearrange("p c e -> p (c e)"), ActF.Copy),
                           waits=[("lds", ldw), ("mm", wuse_mm[par])], inc="ss")
                    s = S.n("ss")
                    S.emit("vector", lambda e, wl3=wl3, wb3=wb3, wh3=wh3: e.tensor_tensor(
                        wl3.rearrange("p c e -> p (c e)"), wb3.rearrange("p c e -> p (c e)"),
                        wh3.rearrange("p c e -> p (c e)"), Alu.subtract),
                           waits=[("ss", s)], inc="vv")
                    wsplit_vv[par] = S.n("vv")
                    out.append((wh3, wl3, ec0, S.n("vv")))
                return out

            pend = a2_load_split(0)
            for dc in range(DC):
                bank = pY[dc % 2]
                bw = bank_waits(bank) + [("vv", v_T)]
                cur, pend = pend, None
                for hi, (wh3, wl3, ec0, vw) in enumerate(cur):
                    first = True
                    for ci in range(16):
                        ec = ec0 + ci
                        for wsp, tsp in ((wh3, Th), (wl3, Th), (wh3, Tl)):
                            st = (hi == 0 and ci == 0 and wsp is wh3 and tsp is Th)
                            sp = (hi == 1 and ci == 15 and wsp is wh3 and tsp is Tl)
                            w = ([("vv", vw)] + (bw if hi == 0 else [])) if first else []
                            S.emit("tensor", lambda e, wsp=wsp, tsp=tsp, ci=ci, ec=ec, bank=bank, st=st, sp=sp: e.matmul(
                                bank[:], wsp[:, ci, 0:128], tsp[:, ec, :], start=st, stop=sp),
                                   waits=w, inc="mm")
                            first = False
                    wuse_mm[hi] = S.n("mm")
                mmY = S.n("mm")
                if dc + 1 < DC:
                    pend = a2_load_split(dc + 1)
                for _ in range(4):
                    emit_quant_tile("attn", v_init)
                S.emit("vector", lambda e, dc=dc, bank=bank: e.scalar_tensor_tensor(
                    X[:, dc, :], X[:, dc, :], 2.0, bank[:], Alu.mult, Alu.add),
                       waits=[("mm", mmY)], inc="vv")
                bank_vv[id(bank)] = S.n("vv")
            a2_mm_end = S.n("mm")
            if half == 0:
                tap("x1", F[:, 0:8192], [128, 8192])
            mmS = ln_stats()
            vR = ln_mu_r(mmS)
            for dc in range(DC):
                S.emit("vector", lambda e, dc=dc: e.tensor_tensor(e3, X[:, dc, :], mu_bc, Alu.subtract),
                       waits=([("vv", vR)] if dc == 0 else []), inc="vv")
                S.emit("vector", lambda e, dc=dc: e.tensor_tensor(X[:, dc, :], e3, r_bc, Alu.mult), inc="vv")
                v = S.n("vv")
                S.emit("scalar", lambda e, dc=dc: e.activation(x2h[:, dc, :], X[:, dc, :], ActF.Copy),
                       waits=[("vv", v)], inc="ss")
                s = S.n("ss")
                S.emit("vector", lambda e, dc=dc: e.tensor_tensor(x2l[:, dc, :], X[:, dc, :], x2h[:, dc, :], Alu.subtract),
                       waits=[("ss", s)], inc="vv")
            v_x2 = S.n("vv")
            tap("x2" if half == 0 else "x2_h1", F[:, 0:8192], [128, 8192])
            if half == 0:
                tap("x2h0", B[:, 0:8192], [128, 8192], BF16)

            ms_started = False
            oacc_first = [True] * DC
            mm_gend = 0
            for grp in range(NGRP):
                for hci in range(HGRP):
                    hc = grp * HGRP + hci
                    hw = hc // 4
                    if hc % 4 == 0:
                        buf = wq1[hw % 2]
                        need = 16 * min((hw + 1) * 8, n_fc_tiles)
                        assert need <= S.n("wq"), f"F1 wq deadlock: need {need} planned {S.n('wq')}"
                        prev_mm = max(win_mm_end.get((half, hw - 2), 0), a2_mm_end)
                        for dc in range(DC):
                            S.dma("sync", lambda e, hw=hw, dc=dc, buf=buf: e.dma_start(
                                out=buf[:, dc, :],
                                in_=qfc[512 * hw:512 * (hw + 1), 128 * dc:128 * (dc + 1)],
                                transpose=True),
                                  waits=([("wq", need), ("mm", prev_mm)] if dc == 0 else []), inc="lds")
                        ld_w1 = S.n("lds")
                    buf = wq1[hw % 2]
                    bank = pH[hc % 4]
                    first = True
                    idx = 0
                    for dc in range(DC):
                        for xs in (x2h, x2l):
                            st, sp = (idx == 0), (idx == 2 * DC - 1)
                            w = ([("lds", ld_w1), ("vv", v_x2)] + bank_waits(bank)) if first else []
                            S.emit("tensor", lambda e, buf=buf, dc=dc, hc=hc, xs=xs, st=st, sp=sp, bank=bank: e.matmul(
                                bank[:], buf[:, dc, 128 * (hc % 4):128 * (hc % 4) + 128], xs[:, dc, :], start=st, stop=sp),
                                   waits=w, inc="mm")
                            first = False
                            idx += 1
                    mmH = S.n("mm")
                    if hc % 4 == 3:
                        win_mm_end[(half, hw)] = mmH
                    par = hci % 2
                    S.emit("scalar", lambda e, bank=bank, par=par: e.activation(gsc[par], bank[:], ActF.Relu),
                           waits=[("mm", mmH)], inc="ss")
                    s = S.n("ss")
                    bank_ss[id(bank)] = s
                    S.emit("vector", lambda e, par=par: e.tensor_tensor(gsc[par], gsc[par], gsc[par], Alu.mult),
                           waits=[("ss", s)], inc="vv")
                    S.emit("vector", lambda e, par=par, hci=hci: e.tensor_copy(gh[:, hci, :], gsc[par]), inc="vv")
                    S.emit("vector", lambda e, par=par, hci=hci: e.tensor_tensor(gl[:, hci, :], gsc[par], gh[:, hci, :], Alu.subtract), inc="vv")
                    S.emit("vector", lambda e, par=par: e.tensor_tensor(g2[par], gsc[par], gsc[par], Alu.mult), inc="vv")
                    v = S.n("vv")
                    S.emit("tensor", lambda e, par=par, st=(not ms_started), hc=hc: e.matmul(
                        pS2[:], ones_r[:], g2[par], start=st, stop=(hc == HC - 1)),
                           waits=[("vv", v)], inc="mm")
                    ms_started = True
                    emit_quant_tile("ffn", v_x2)
                    emit_quant_tile("ffn", v_x2)
                    if half == 0 and hc == 0:
                        tap("g0", gsc[0], [128, THF])
                    if half == 0 and hc == 1:
                        tap("g1", gsc[1], [128, THF])
                v_g = S.n("vv")
                mm_gend = S.n("mm")
                for dcq in range(4):
                    buf2 = wq2[dcq % 2]
                    need = 16 * (n_fc_tiles + min(16 * (grp + 1), len(quant_tiles) - n_fc_tiles))
                    assert need <= S.n("wq"), f"F2 wq deadlock: need {need} planned {S.n('wq')}"
                    prev_mm = max(wq2_mm[dcq % 2], a2_mm_end)
                    for hci in range(HGRP):
                        hc = grp * HGRP + hci
                        S.dma("sync", lambda e, hc=hc, dcq=dcq, hci=hci, buf2=buf2: e.dma_start(
                            out=buf2[:, hci, :],
                            in_=qproj[512 * dcq:512 * (dcq + 1), 128 * hc:128 * (hc + 1)],
                            transpose=True),
                              waits=([("wq", need), ("mm", prev_mm)] if hci == 0 else []), inc="lds")
                    ld_w2 = S.n("lds")
                    for dci in range(4):
                        dc = dcq * 4 + dci
                        bank = pY[dci % 2]
                        first = True
                        idx = 0
                        for hci in range(HGRP):
                            for gs in (gh, gl):
                                st, sp = (idx == 0), (idx == 2 * HGRP - 1)
                                w = ([("lds", ld_w2), ("vv", v_g)] + bank_waits(bank)) if first else []
                                S.emit("tensor", lambda e, buf2=buf2, hci=hci, dci=dci, gs=gs, st=st, sp=sp, bank=bank: e.matmul(
                                    bank[:], buf2[:, hci, 128 * dci:128 * dci + 128], gs[:, hci, :], start=st, stop=sp),
                                       waits=w, inc="mm")
                                first = False
                                idx += 1
                        mmO = S.n("mm")
                        if oacc_first[dc]:
                            S.emit("vector", lambda e, dc=dc, bank=bank: e.tensor_copy(Oacc[:, dc, :], bank[:]),
                                   waits=[("mm", mmO)], inc="vv")
                            oacc_first[dc] = False
                        else:
                            S.emit("vector", lambda e, dc=dc, bank=bank: e.tensor_tensor(Oacc[:, dc, :], Oacc[:, dc, :], bank[:], Alu.add),
                                   waits=[("mm", mmO)], inc="vv")
                        bank_vv[id(bank)] = S.n("vv")
                    wq2_mm[dcq % 2] = S.n("mm")
                    emit_quant_tile("ffn", v_x2)

            S.emit("vector", lambda e: e.tensor_scalar(e3, pS2[:], 1.0 / H, EPS32, Alu.mult, Alu.add),
                   waits=[("mm", mm_gend), ("vv", S.n("vv")), ("lds2", S.n("lds2"))], inc="vv")
            vS = newton_rsqrt(r_bc)
            for dc in range(DC):
                S.emit("vector", lambda e, dc=dc: e.tensor_tensor(e3, Oacc[:, dc, :], r_bc, Alu.mult),
                       waits=([("vv", vS)] if dc == 0 else []), inc="vv")
                S.emit("vector", lambda e, dc=dc: e.tensor_tensor(X[:, dc, :], X[:, dc, :], e3, Alu.add), inc="vv")
            tap("x3" if half == 0 else "x3_h1", F[:, 0:8192], [128, 8192])
            tap("Oacc0" if half == 0 else "Oacc_h1", share, [128, 8192])
            if half == 0:
                tap("sF", r_bc, [128, THF])
            mmS = ln_stats()
            vR = ln_mu_r(mmS)
            st_tgts = []
            for dc in range(DC):
                ob = ob0 if (dc % 2 == 0) else ob1
                w = [("vv", vR)] if dc == 0 else []
                if dc >= 2:
                    w.append(("stq", st_tgts[dc - 2]))
                S.emit("vector", lambda e, dc=dc: e.tensor_tensor(e3, X[:, dc, :], mu_bc, Alu.subtract),
                       waits=w, inc="vv")
                S.emit("vector", lambda e, dc=dc, ob=ob: e.tensor_tensor(ob, e3, r_bc, Alu.mult), inc="vv")
                v = S.n("vv")
                S.dma("sync", lambda e, dc=dc, tok0=tok0, ob=ob: e.dma_start(
                    out=out.rearrange("(c p) t -> p c t", p=128)[:, dc, tok0:tok0 + THF],
                    in_=ob),
                      waits=[("vv", v)], inc="stq")
                st_tgts.append(S.n("stq"))

        if "qfc" in taps:
            t_qfc = nc.dram_tensor("tap_qfc", [H, D], BF16, kind="ExternalOutput").ap()
            S.dma("sync", lambda e: e.dma_start(out=t_qfc, in_=qfc),
                  waits=[("wq", S.n("wq"))], inc="stq")
        if "qproj" in taps:
            t_qp = nc.dram_tensor("tap_qproj", [D, H], BF16, kind="ExternalOutput").ap()
            S.dma("sync", lambda e: e.dma_start(out=t_qp, in_=qproj),
                  waits=[("wq", S.n("wq"))], inc="stq")

        S.emit("sync", lambda e: None,
               waits=[("stq", S.n("stq")), ("wq", S.n("wq")),
                      ("lds", S.n("lds")), ("ldg", S.n("ldg"))])

        def run_queue(eng_obj, name):
            for waits, fn, inc, amt in S.q[name]:
                for sname, val in waits:
                    eng_obj.wait_ge(sems[sname], val)
                ins = fn(eng_obj)
                if inc is not None:
                    if ins is None:
                        ins = eng_obj.nop()
                    ins.then_inc(sems[inc], amt)

        @blk.sync
        def _(sync):
            run_queue(sync, "sync")

        @blk.gpsimd
        def _(gpsimd):
            run_queue(gpsimd, "gpsimd")

        @blk.tensor
        def _(tensor):
            run_queue(tensor, "tensor")

        @blk.vector
        def _(vector):
            run_queue(vector, "vector")

        @blk.scalar
        def _(scalar):
            run_queue(scalar, "scalar")

    return nc


_NC_CACHE = {}


def kernel(L0_attn_x, L0_attn_gamma, L0_attn_beta, L0_attn_W_in, L0_attn_W_out,
           L0_attn_dt, L0_ffn_W_fc, L0_ffn_W_proj):
    x = np.ascontiguousarray(np.asarray(L0_attn_x, dtype=np.float32)).reshape(8192, 2048)
    winT = np.ascontiguousarray(np.asarray(L0_attn_W_in, dtype=np.float32).T)
    woutT = np.ascontiguousarray(np.asarray(L0_attn_W_out, dtype=np.float32).T)
    wfc = np.ascontiguousarray(np.asarray(L0_ffn_W_fc, dtype=np.float32))
    wproj = np.ascontiguousarray(np.asarray(L0_ffn_W_proj, dtype=np.float32))
    gamma = np.ascontiguousarray(np.asarray(L0_attn_gamma, dtype=np.float32).reshape(16, 128).T)
    beta = np.ascontiguousarray(np.asarray(L0_attn_beta, dtype=np.float32).reshape(16, 128).T)
    dt = np.asarray(L0_attn_dt, dtype=np.float32)
    dt_fl = np.ascontiguousarray(dt.reshape(1, 4096))
    dt_pc = np.ascontiguousarray(dt.reshape(32, 128).T)
    ones = np.ones((128, 128), dtype=np.float32)

    if "nc" not in _NC_CACHE:
        _NC_CACHE["nc"] = build_core()
    nc = _NC_CACHE["nc"]

    in_maps = []
    for i in range(NCORES):
        xs = np.ascontiguousarray(x[1024 * i:1024 * (i + 1), :].T)
        in_maps.append({"x": xs, "winT": winT, "woutT": woutT, "wfc": wfc,
                        "wproj": wproj, "gamma": gamma, "beta": beta,
                        "dt_fl": dt_fl, "dt_pc": dt_pc, "ones_in": ones})
    res = run_bass_kernel_spmd(nc, in_maps, core_ids=list(range(NCORES)))
    o = np.empty((8192, 2048), dtype=np.float32)
    for i in range(NCORES):
        o[1024 * i:1024 * (i + 1), :] = res.results[i]["out"].T
    return o.reshape(2, 4096, 2048)


# version 07
# speedup vs baseline: 1.8610x; 1.2168x over previous
import contextlib
import numpy as np
import concourse.bass as bass
import concourse.mybir as mybir
from concourse.bass_utils import run_bass_kernel_spmd

F32 = mybir.dt.float32
F32R = mybir.dt.float32r
BF16 = mybir.dt.bfloat16
U16 = mybir.dt.uint16
U32 = mybir.dt.uint32
I32 = mybir.dt.int32
I16 = mybir.dt.int16
Alu = mybir.AluOpType
ActF = mybir.ActivationFunctionType
AX = mybir.AxisListType

NCORES = 8
D, DC = 2048, 16
EI, ECP = 4096, 32
H, HC = 8192, 64
NT, THF = 1024, 512
HGRP = 8
NGRP = HC // HGRP
M_BND = 0.501953125
P25 = 2.0 ** -25
LN_EPS = 1e-5
EPS32 = float(np.finfo(np.float32).eps)
MAGIC = 0x5F3759DF
QW = 1024
QG = QW // 64

ENGINES = ("sync", "gpsimd", "tensor", "vector", "scalar")


class Sched:

    def __init__(self):
        self.q = {e: [] for e in ENGINES}
        self.cnt = {}

    def n(self, sem):
        return self.cnt.get(sem, 0)

    def emit(self, eng, fn, waits=(), inc=None, amt=1):
        waits = tuple((s, v) for (s, v) in waits if v > 0)
        self.q[eng].append((waits, fn, inc, amt))
        if inc is not None:
            self.cnt[inc] = self.cnt.get(inc, 0) + amt
        return self.cnt.get(inc, 0)

    def dma(self, eng, fn, waits=(), inc=None):
        return self.emit(eng, fn, waits, inc, 16)


def build_core(taps=None, barriers=None):
    taps = taps or []
    barriers = barriers or []
    nc = bass.Bass("TRN2", target_bir_lowering=False, debug=False,
                   detect_race_conditions=False)

    x_in = nc.dram_tensor("x", [D, NT], F32, kind="ExternalInput").ap()
    winT = nc.dram_tensor("winT", [D, 2 * EI], F32, kind="ExternalInput").ap()
    woutT = nc.dram_tensor("woutT", [EI, D], F32, kind="ExternalInput").ap()
    wfc = nc.dram_tensor("wfc", [H, D], F32, kind="ExternalInput").ap()
    wproj = nc.dram_tensor("wproj", [D, H], F32, kind="ExternalInput").ap()
    gamma = nc.dram_tensor("gamma", [128, DC], F32, kind="ExternalInput").ap()
    beta = nc.dram_tensor("beta", [128, DC], F32, kind="ExternalInput").ap()
    dt_fl = nc.dram_tensor("dt_fl", [1, EI], F32, kind="ExternalInput").ap()
    dt_pc = nc.dram_tensor("dt_pc", [128, ECP], F32, kind="ExternalInput").ap()
    ones_in = nc.dram_tensor("ones_in", [128, 128], F32, kind="ExternalInput").ap()
    out = nc.dram_tensor("out", [D, NT], F32, kind="ExternalOutput").ap()
    qfc = nc.dram_tensor("qfc", [H, D], BF16, kind="Internal").ap()
    qproj = nc.dram_tensor("qproj", [D, H], BF16, kind="Internal").ap()

    S = Sched()
    es = contextlib.ExitStack()
    with es:
        sb = lambda n, sh, dt: es.enter_context(nc.sbuf_tensor(n, sh, dt))
        psu = lambda n, sh: es.enter_context(nc.psum_tensor(n, sh, F32))

        B = sb("B", [128, 57344], BF16)
        xnh = B[:, 0:8192].rearrange("p (c t) -> p c t", c=DC)
        xnl = B[:, 8192:16384].rearrange("p (c t) -> p c t", c=DC)
        Th = B[:, 16384:32768].rearrange("p (c t) -> p c t", c=ECP)
        Tl = B[:, 32768:49152].rearrange("p (c t) -> p c t", c=ECP)
        whi = B[:, 49152:53248]
        wlo = B[:, 53248:57344]
        x2h = B[:, 0:8192].rearrange("p (c t) -> p c t", c=DC)
        x2l = B[:, 8192:16384].rearrange("p (c t) -> p c t", c=DC)
        gh = B[:, 16384:20480].rearrange("p (c t) -> p c t", c=HGRP)
        gl = B[:, 20480:24576].rearrange("p (c t) -> p c t", c=HGRP)
        wq1 = [B[:, 24576:32768].rearrange("p (c t) -> p c t", c=DC),
               B[:, 32768:40960].rearrange("p (c t) -> p c t", c=DC)]
        wq2 = [B[:, 40960:45056].rearrange("p (c t) -> p c t", c=HGRP),
               B[:, 45056:49152].rearrange("p (c t) -> p c t", c=HGRP)]

        F = sb("F", [128, 20480], F32)
        X = F[:, 0:8192].rearrange("p (c t) -> p c t", c=DC)
        e1 = F[:, 8192:8704]
        e2 = F[:, 8704:9216]
        e3 = F[:, 9216:9728]
        e4 = F[:, 9728:10240]
        mu_bc = F[:, 10240:10752]
        r_bc = F[:, 10752:11264]
        share = F[:, 11264:19456]
        wbuf = share[:, 0:4096]
        Oacc = share.rearrange("p (c t) -> p c t", c=DC)
        dtf = share[0:1, 4096:8192]
        qw0 = F[:, 19456:20480]
        qw1_attn = F[:, 15360:16384]
        qw1_ffn = F[:, 9728:10752]
        ob0 = F[:, 19456:19968]
        ob1 = F[:, 19968:20480]
        gsc = [e1, e2]

        R = sb("R", [128, 1024], F32R)
        xr = R[:, 0:512]
        xsq = R[:, 512:1024]
        g2 = [xr, xsq]

        IA = sb("IA", [128, 3072], U16)
        qwb = IA[:, 0:1024]
        qau = IA[:, 1024:2048]
        qmk = IA[:, 2048:3072]
        useed = IA[:, 2048:3072].bitcast(U32)

        qgs = sb("qgs", [128, QG], F32)
        qs16 = sb("qs16", [128, QG], U16)
        qsf = sb("qsf", [128, QG], U32)
        qthr = sb("qthr", [128, QG], U32)

        ones_f = sb("ones_f", [128, 128], F32)
        ones_r = sb("ones_r", [128, 128], F32R)
        ones1 = sb("ones1", [1, 128], F32)
        gam = sb("gam", [128, DC], F32)
        bet = sb("bet", [128, DC], F32)
        dtw = sb("dtw", [128, ECP], F32)
        dtp = sb("dtp", [128, ECP], F32)
        sc1 = sb("sc1", [1, 8], F32)
        mb1 = sb("mb1", [128, 1], F32)
        rb1 = sb("rb1", [128, 1], F32)

        pZ = [psu("pZ0", [128, THF]), psu("pZ1", [128, THF])]
        pG = [psu("pG0", [128, THF]), psu("pG1", [128, THF])]
        pY = [psu("pY0", [128, THF]), psu("pY1", [128, THF])]
        pH = [pZ[0], pZ[1], pG[0], pG[1]]
        pS = psu("pS", [128, THF])
        pS2 = psu("pS2", [128, THF])
        pB = pS[:, 0:8]

        sems = {}
        for s in ("lds", "ldg", "lds2", "mm", "vv", "ss", "stq", "wq"):
            sems[s] = es.enter_context(nc.semaphore(s))

        blk = es.enter_context(nc.Block())

        def tap(name, src_ap, shape, dtype=F32):
            if name not in taps:
                return
            t = nc.dram_tensor("tap_" + name, shape, dtype, kind="ExternalOutput").ap()
            S.dma("sync", lambda e: e.dma_start(out=t, in_=src_ap),
                  waits=[("vv", S.n("vv")), ("ss", S.n("ss")), ("mm", S.n("mm"))], inc="stq")
            v = S.n("stq")
            S.emit("vector", lambda e: e.nop(), waits=[("stq", v)], inc="vv")
            S.emit("scalar", lambda e: e.nop(), waits=[("stq", v)], inc="ss")

        S.dma("sync", lambda e: e.dma_start(out=gam[:], in_=gamma), inc="lds")
        S.dma("sync", lambda e: e.dma_start(out=bet[:], in_=beta), inc="lds")
        S.dma("sync", lambda e: e.dma_start(out=dtf, in_=dt_fl), inc="lds")
        S.dma("sync", lambda e: e.dma_start(out=dtp[:], in_=dt_pc), inc="lds")
        S.dma("sync", lambda e: e.dma_start(out=ones_f[:], in_=ones_in), inc="lds")
        ld0 = S.n("lds")

        S.emit("vector", lambda e: e.tensor_copy(ones_r[:], ones_f[:]),
               waits=[("lds", ld0)], inc="vv")
        S.emit("vector", lambda e: e.tensor_copy(ones1[:], ones_f[0:1, :]), inc="vv")
        S.emit("vector", lambda e: e.memset(qsf[:], 0), inc="vv")
        S.emit("vector", lambda e: e.tensor_scalar(dtf, dtf, -1.0, None, Alu.mult), inc="vv")
        vrm = S.emit("vector", lambda e: e.reduce_max(sc1[0:1, 0:1], dtf, AX.X), inc="vv")
        S.emit("vector", lambda e: e.tensor_scalar(sc1[0:1, 1:2], sc1[0:1, 0:1], -1.0, None, Alu.mult),
               waits=[("vv", vrm)], inc="vv")
        v = S.n("vv")
        S.emit("scalar", lambda e: e.activation(dtf, dtf, ActF.Exp, bias=sc1[0:1, 1:2], scale=1.0),
               waits=[("vv", v)], inc="ss")
        s = S.n("ss")
        vrs = S.emit("vector", lambda e: e.reduce_sum(sc1[0:1, 2:3], dtf, AX.X),
               waits=[("ss", s)], inc="vv")
        vrc = S.emit("vector", lambda e: e.reciprocal(sc1[0:1, 3:4], sc1[0:1, 2:3]),
               waits=[("vv", vrs)], inc="vv")
        S.emit("vector", lambda e: e.nop(), waits=[("vv", vrc)], inc="vv")
        v = S.n("vv")
        S.emit("tensor", lambda e: e.matmul(pS[:, 0:1], ones1[:], sc1[0:1, 1:2], start=True, stop=True),
               waits=[("vv", v)], inc="mm")
        S.emit("tensor", lambda e: e.matmul(pS[:, 1:2], ones1[:], sc1[0:1, 3:4], start=True, stop=True), inc="mm")
        mm = S.n("mm")
        S.emit("vector", lambda e: e.tensor_copy(mb1[:], pS[:, 0:1]), waits=[("mm", mm)], inc="vv")
        S.emit("vector", lambda e: e.tensor_copy(rb1[:], pS[:, 1:2]), inc="vv")
        S.emit("vector", lambda e: e.tensor_scalar(dtp[:], dtp[:], -1.0, None, Alu.mult), inc="vv")
        v = S.n("vv")
        S.emit("scalar", lambda e: e.activation(dtw[:], dtp[:], ActF.Exp, bias=mb1[:], scale=1.0),
               waits=[("vv", v)], inc="ss")
        s = S.n("ss")
        S.emit("vector", lambda e: e.tensor_scalar(dtw[:], dtw[:], rb1[:], None, Alu.mult),
               waits=[("ss", s)], inc="vv")
        v_init = S.n("vv")
        tap("dtw", dtw[:], [128, ECP])
        tap("sc1", sc1[:], [1, 8])
        tap("mb1", mb1[:], [128, 1])
        tap("rb1", rb1[:], [128, 1])
        tap("dtf", dtf, [1, EI])
        tap("dtp2", dtp[:], [128, ECP])

        def barrier(name):
            if name not in barriers:
                return
            w = [(s, S.n(s)) for s in ("lds", "ldg", "mm", "vv", "ss", "stq", "wq")]
            for eng in ("vector", "scalar", "tensor", "sync", "gpsimd"):
                S.emit(eng, lambda e: e.nop(), waits=w)

        def newton_rsqrt(dst):
            qau16 = qau[:, 0:512]
            S.emit("vector", lambda e: e.memset(useed, 0),
                   waits=[("wq", S.n("wq"))], inc="vv")
            S.emit("vector", lambda e: e.tensor_scalar(
                qau16, e3.bitcast(U16).rearrange("p (a two) -> p a two", two=2)[:, :, 1],
                1, None, Alu.logical_shift_right), inc="vv")
            S.emit("vector", lambda e: e.tensor_scalar(
                qau16.bitcast(I16), qau16.bitcast(I16), -1, 0x5F37, Alu.mult, Alu.add), inc="vv")
            S.emit("vector", lambda e: e.tensor_copy(
                useed.bitcast(U16).rearrange("p (a two) -> p a two", two=2)[:, :, 1],
                qau16), inc="vv")
            S.emit("vector", lambda e: e.tensor_tensor(e1, useed.bitcast(F32), useed.bitcast(F32), Alu.mult), inc="vv")
            S.emit("vector", lambda e: e.tensor_tensor(e1, e1, e3, Alu.mult), inc="vv")
            S.emit("vector", lambda e: e.tensor_scalar(e1, e1, -0.5, 1.5, Alu.mult, Alu.add), inc="vv")
            S.emit("vector", lambda e: e.tensor_tensor(e2, useed.bitcast(F32), e1, Alu.mult), inc="vv")
            S.emit("vector", lambda e: e.tensor_tensor(e1, e2, e2, Alu.mult), inc="vv")
            S.emit("vector", lambda e: e.tensor_tensor(e1, e1, e3, Alu.mult), inc="vv")
            S.emit("vector", lambda e: e.tensor_scalar(e1, e1, -0.5, 1.5, Alu.mult, Alu.add), inc="vv")
            S.emit("vector", lambda e: e.tensor_tensor(e2, e2, e1, Alu.mult), inc="vv")
            S.emit("vector", lambda e: e.tensor_tensor(e1, e2, e2, Alu.mult), inc="vv")
            S.emit("vector", lambda e: e.tensor_tensor(e1, e1, e3, Alu.mult), inc="vv")
            S.emit("vector", lambda e: e.tensor_scalar(e1, e1, -0.5, 1.5, Alu.mult, Alu.add), inc="vv")
            S.emit("vector", lambda e: e.tensor_tensor(dst, e2, e1, Alu.mult), inc="vv")
            return S.n("vv")

        def ln_stats(extra_first_waits=()):
            for dc in range(DC):
                w = list(extra_first_waits) if dc == 0 else []
                w.append(("mm", S.n("mm")))
                S.emit("vector", lambda e, dc=dc: e.tensor_copy(xr, X[:, dc, :]), waits=w, inc="vv")
                S.emit("vector", lambda e: e.tensor_tensor(xsq, xr, xr, Alu.mult), inc="vv")
                v = S.n("vv")
                S.emit("tensor", lambda e, dc=dc: e.matmul(pS[:], ones_r[:], xr, start=(dc == 0), stop=(dc == DC - 1)),
                       waits=[("vv", v)], inc="mm")
                S.emit("tensor", lambda e, dc=dc: e.matmul(pS2[:], ones_r[:], xsq, start=(dc == 0), stop=(dc == DC - 1)),
                       inc="mm")
            return S.n("mm")

        def ln_mu_r(mm_done):
            S.emit("vector", lambda e: e.tensor_scalar(mu_bc, pS[:], 1.0 / D, None, Alu.mult),
                   waits=[("mm", mm_done)], inc="vv")
            S.emit("vector", lambda e: e.tensor_tensor(e3, mu_bc, mu_bc, Alu.mult), inc="vv")
            S.emit("vector", lambda e: e.scalar_tensor_tensor(e3, pS2[:], 1.0 / D, e3, Alu.mult, Alu.subtract), inc="vv")
            S.emit("vector", lambda e: e.tensor_scalar(e3, e3, 1.0, LN_EPS, Alu.mult, Alu.add), inc="vv")
            return newton_rsqrt(r_bc)

        quant_tiles = []
        for t in range(H // 128):
            for c in range(D // QW):
                quant_tiles.append((wfc[128 * t:128 * (t + 1), QW * c:QW * (c + 1)],
                                    qfc[128 * t:128 * (t + 1), QW * c:QW * (c + 1)]))
        n_fc_tiles = len(quant_tiles)
        for hw in range(H // QW):
            for dr in range(D // 128):
                quant_tiles.append((wproj[128 * dr:128 * (dr + 1), QW * hw:QW * (hw + 1)],
                                    qproj[128 * dr:128 * (dr + 1), QW * hw:QW * (hw + 1)]))
        qt_idx = [0]
        q_par = [0]
        q_cons_vv = {0: 0, 1: 0}
        q_wb_vv = [0]

        q_loads = []

        def _emit_quant_load(phase, guard_vv):
            k = len(q_loads)
            if k >= len(quant_tiles):
                return
            src_ = quant_tiles[k][0]
            par = k % 2
            qw = qw0 if par == 0 else (qw1_attn if phase == "attn" else qw1_ffn)
            S.dma("gpsimd", lambda e, src_=src_, qw=qw: e.dma_start(out=qw, in_=src_),
                  waits=[("vv", max(q_cons_vv[par], guard_vv))], inc="ldg")
            q_loads.append((S.n("ldg"), qw))

        def emit_quant_tile(phase="attn", guard_vv=0):
            if qt_idx[0] >= len(quant_tiles):
                return
            k = qt_idx[0]
            qt_idx[0] += 1
            if k == 0:
                _emit_quant_load(phase, guard_vv)
            _emit_quant_load(phase, guard_vv)
            src_, dst = quant_tiles[k]
            par = k % 2
            ld, qw = q_loads[k]
            S.emit("scalar", lambda e, qw=qw: e.activation(qwb.bitcast(BF16), qw, ActF.Copy),
                   waits=[("ldg", ld), ("vv", q_wb_vv[0])], inc="ss")
            S.emit("scalar", lambda e: e.activation(qau.bitcast(BF16), qwb.bitcast(BF16), ActF.Abs), inc="ss")
            S.emit("scalar", lambda e: e.activation(qwb.bitcast(BF16), qwb.bitcast(BF16), ActF.Sign), inc="ss")
            s = S.n("ss")
            vp = S.emit("vector", lambda e: e.tensor_reduce(
                qgs[:].rearrange("p (g o) -> p g o", o=1),
                qau.bitcast(BF16).rearrange("p (g k) -> p g k", g=QG),
                AX.X, Alu.add), waits=[("ss", s)], inc="vv")
            vp = S.emit("vector", lambda e: e.tensor_scalar(qs16[:].bitcast(BF16), qgs[:], 1.0 / 64, 1e-8, Alu.mult, Alu.max),
                   waits=[("vv", vp)], inc="vv")
            vp = S.emit("vector", lambda e: e.tensor_copy(
                qsf[:].bitcast(U16).rearrange("p (g two) -> p g two", two=2)[:, :, 1],
                qs16[:].rearrange("p (g o) -> p g o", o=1)[:, :, 0]), waits=[("vv", vp)], inc="vv")
            vp = S.emit("vector", lambda e: e.tensor_scalar(qthr[:].bitcast(F32), qsf[:].bitcast(F32), M_BND, None, Alu.mult),
                   waits=[("vv", vp)], inc="vv")
            vp = S.emit("vector", lambda e: e.scalar_tensor_tensor(qthr[:].bitcast(F32), qsf[:].bitcast(F32), P25, qthr[:].bitcast(F32), Alu.mult, Alu.add),
                   waits=[("vv", vp)], inc="vv")
            vp = S.emit("vector", lambda e: e.tensor_tensor(
                qmk.rearrange("p (g k) -> p g k", g=QG),
                qau.rearrange("p (g k) -> p g k", g=QG),
                qthr[:].bitcast(U16).rearrange("p (g two) -> p g two", two=2)[:, :, 1:2].broadcast_to([128, QG, 64]),
                Alu.is_gt), waits=[("wq", S.n("wq")), ("vv", vp)], inc="vv")
            vp = S.emit("vector", lambda e: e.tensor_tensor(
                qmk.rearrange("p (g k) -> p g k", g=QG),
                qmk.rearrange("p (g k) -> p g k", g=QG),
                qs16[:].rearrange("p (g o) -> p g o", o=1).broadcast_to([128, QG, 64]),
                Alu.mult), waits=[("vv", vp)], inc="vv")
            S.emit("vector", lambda e: e.tensor_tensor(qmk.bitcast(BF16), qmk.bitcast(BF16), qwb.bitcast(BF16), Alu.mult),
                   waits=[("vv", vp)], inc="vv")
            v = S.n("vv")
            q_wb_vv[0] = v
            q_cons_vv[par] = v
            S.dma("gpsimd", lambda e, dst=dst: e.dma_start(out=dst, in_=qmk.bitcast(BF16)),
                  waits=[("vv", v)], inc="wq")

        win_mm_end = {}
        wsplit_vv = [0, 0]
        wuse_mm = [0, 0]
        wq2_mm = [0, 0]
        bank_vv = {}
        bank_ss = {}
        last_mm_xn = 0

        def bank_waits(bank):
            return [("vv", bank_vv.get(id(bank), 0)), ("ss", bank_ss.get(id(bank), 0))]

        for half in range(2):
            tok0 = half * THF
            xw = [("vv", S.n("vv")), ("stq", S.n("stq"))] if half else [("lds", ld0)]
            S.dma("sync", lambda e, tok0=tok0: e.dma_start(
                out=X,
                in_=x_in.rearrange("(c p) t -> p c t", p=128)[:, :, tok0:tok0 + THF]),
                  waits=xw, inc="lds")
            ldx = S.n("lds")

            mmS = ln_stats(extra_first_waits=[("lds", ldx), ("vv", v_init)])
            vR = ln_mu_r(mmS)
            for dc in range(DC):
                S.emit("vector", lambda e, dc=dc: e.tensor_tensor(e3, X[:, dc, :], mu_bc, Alu.subtract),
                       waits=([("vv", vR), ("mm", last_mm_xn)] if dc == 0 else []), inc="vv")
                S.emit("vector", lambda e: e.tensor_tensor(e3, e3, r_bc, Alu.mult), inc="vv")
                S.emit("vector", lambda e, dc=dc: e.tensor_scalar(e4, e3, gam[:, dc:dc + 1], bet[:, dc:dc + 1], Alu.mult, Alu.add), inc="vv")
                v = S.n("vv")
                S.emit("scalar", lambda e, dc=dc: e.activation(xnh[:, dc, :], e4, ActF.Copy),
                       waits=[("vv", v)], inc="ss")
                s = S.n("ss")
                S.emit("vector", lambda e, dc=dc: e.tensor_tensor(xnl[:, dc, :], e4, xnh[:, dc, :], Alu.subtract),
                       waits=[("ss", s)], inc="vv")
            v_xn = S.n("vv")
            if half == 0:
                tap("mu0", mu_bc, [128, THF])
                tap("r0", r_bc, [128, THF])
                tap("xnh0", B[:, 0:8192], [128, 8192], BF16)
                tap("xnl0", B[:, 8192:16384], [128, 8192], BF16)

            def a1_load_split(ec):
                out = []
                for hp in range(2):
                    par = hp
                    wb3 = wbuf.rearrange("p (q c e) -> p q c e", q=2, c=8)[:, par]
                    wh3 = whi.rearrange("p (q c e) -> p q c e", q=2, c=8)[:, par]
                    wl3 = wlo.rearrange("p (q c e) -> p q c e", q=2, c=8)[:, par]
                    dc0 = 8 * hp
                    S.dma("sync", lambda e, ec=ec, dc0=dc0, wb3=wb3: e.dma_start(
                        out=wb3[:, :, 0:128],
                        in_=winT.rearrange("(c p) e -> p c e", p=128)[:, dc0:dc0 + 8, 128 * ec:128 * (ec + 1)]),
                          waits=[("vv", wsplit_vv[par])], inc="lds")
                    S.dma("sync", lambda e, ec=ec, dc0=dc0, wb3=wb3: e.dma_start(
                        out=wb3[:, :, 128:256],
                        in_=winT.rearrange("(c p) e -> p c e", p=128)[:, dc0:dc0 + 8, EI + 128 * ec:EI + 128 * (ec + 1)]),
                          inc="lds")
                    ldw = S.n("lds")
                    S.emit("scalar", lambda e, wh3=wh3, wb3=wb3: e.activation(
                        wh3.rearrange("p c e -> p (c e)"), wb3.rearrange("p c e -> p (c e)"), ActF.Copy),
                           waits=[("lds", ldw), ("mm", wuse_mm[par])], inc="ss")
                    s = S.n("ss")
                    S.emit("vector", lambda e, wl3=wl3, wb3=wb3, wh3=wh3: e.tensor_tensor(
                        wl3.rearrange("p c e -> p (c e)"), wb3.rearrange("p c e -> p (c e)"),
                        wh3.rearrange("p c e -> p (c e)"), Alu.subtract),
                           waits=[("ss", s)], inc="vv")
                    wsplit_vv[par] = S.n("vv")
                    out.append((wh3, wl3, dc0, S.n("vv")))
                return out

            pend = a1_load_split(0)
            for ec in range(ECP):
                zb, gb = pZ[ec % 2], pG[ec % 2]
                bw = bank_waits(zb) + bank_waits(gb) + [("vv", v_xn)]
                cur, pend = pend, None
                for hi, (wh3, wl3, dc0, vw) in enumerate(cur):
                    first = True
                    for tgt, col0 in ((zb, 0), (gb, 128)):
                        for ci in range(8):
                            dc = dc0 + ci
                            for wsp, xsp in ((wh3, xnh), (wl3, xnh), (wh3, xnl)):
                                st = (hi == 0 and ci == 0 and wsp is wh3 and xsp is xnh)
                                sp = (hi == 1 and ci == 7 and wsp is wh3 and xsp is xnl)
                                w = ([("vv", vw)] + (bw if hi == 0 else [])) if first else []
                                S.emit("tensor", lambda e, wsp=wsp, xsp=xsp, ci=ci, dc=dc, tgt=tgt, col0=col0, st=st, sp=sp: e.matmul(
                                    tgt[:], wsp[:, ci, col0:col0 + 128], xsp[:, dc, :], start=st, stop=sp),
                                       waits=w, inc="mm")
                                first = False
                    wuse_mm[hi] = S.n("mm")
                mmA = S.n("mm")
                last_mm_xn = mmA
                if ec + 1 < ECP:
                    pend = a1_load_split(ec + 1)
                for _ in range(2):
                    emit_quant_tile("attn", v_init)
                S.emit("vector", lambda e, zb=zb: e.tensor_scalar(e1, zb[:], 15.0, -15.0, Alu.min, Alu.max),
                       waits=[("mm", mmA)], inc="vv")
                v1 = S.n("vv")
                S.emit("scalar", lambda e: e.activation(e2, e1, ActF.Sigmoid),
                       waits=[("vv", v1)], inc="ss")
                s1 = S.n("ss")
                S.emit("vector", lambda e, gb=gb: e.tensor_scalar(e3, gb[:], 15.0, -15.0, Alu.min, Alu.max), inc="vv")
                v2 = S.n("vv")
                S.emit("scalar", lambda e: e.activation(e4, e3, ActF.Sigmoid),
                       waits=[("vv", v2)], inc="ss")
                s2 = S.n("ss")
                S.emit("vector", lambda e, ec=ec, zb=zb: e.scalar_tensor_tensor(e1, zb[:], dtw[:, ec:ec + 1], e2, Alu.mult, Alu.mult),
                       waits=[("ss", s1)], inc="vv")
                S.emit("vector", lambda e: e.tensor_tensor(e3, e3, e4, Alu.mult),
                       waits=[("ss", s2)], inc="vv")
                S.emit("vector", lambda e: e.tensor_tensor(e1, e1, e3, Alu.mult), inc="vv")
                vT = S.n("vv")
                S.emit("scalar", lambda e, ec=ec: e.activation(Th[:, ec, :], e1, ActF.Copy),
                       waits=[("vv", vT)], inc="ss")
                sT = S.n("ss")
                S.emit("vector", lambda e, ec=ec: e.tensor_tensor(Tl[:, ec, :], e1, Th[:, ec, :], Alu.subtract),
                       waits=[("ss", sT)], inc="vv")
                bank_vv[id(zb)] = bank_vv[id(gb)] = S.n("vv")
            v_T = S.n("vv")
            if half == 0:
                tap("Th0", B[:, 16384:32768], [128, 16384], BF16)
                tap("Tl0", B[:, 32768:49152], [128, 16384], BF16)

            def a2_load_split(dc):
                out = []
                for hp in range(2):
                    par = hp
                    wb3 = wbuf.rearrange("p (q c e) -> p q c e", q=2, c=16)[:, par]
                    wh3 = whi.rearrange("p (q c e) -> p q c e", q=2, c=16)[:, par]
                    wl3 = wlo.rearrange("p (q c e) -> p q c e", q=2, c=16)[:, par]
                    ec0 = 16 * hp
                    S.dma("sync", lambda e, dc=dc, ec0=ec0, wb3=wb3: e.dma_start(
                        out=wb3[:],
                        in_=woutT.rearrange("(c p) d -> p c d", p=128)[:, ec0:ec0 + 16, 128 * dc:128 * (dc + 1)]),
                          waits=[("vv", wsplit_vv[par])], inc="lds")
                    ldw = S.n("lds")
                    S.emit("scalar", lambda e, wh3=wh3, wb3=wb3: e.activation(
                        wh3.rearrange("p c e -> p (c e)"), wb3.rearrange("p c e -> p (c e)"), ActF.Copy),
                           waits=[("lds", ldw), ("mm", wuse_mm[par])], inc="ss")
                    s = S.n("ss")
                    S.emit("vector", lambda e, wl3=wl3, wb3=wb3, wh3=wh3: e.tensor_tensor(
                        wl3.rearrange("p c e -> p (c e)"), wb3.rearrange("p c e -> p (c e)"),
                        wh3.rearrange("p c e -> p (c e)"), Alu.subtract),
                           waits=[("ss", s)], inc="vv")
                    wsplit_vv[par] = S.n("vv")
                    out.append((wh3, wl3, ec0, S.n("vv")))
                return out

            pend = a2_load_split(0)
            for dc in range(DC):
                bank = pY[dc % 2]
                bw = bank_waits(bank) + [("vv", v_T)]
                cur, pend = pend, None
                for hi, (wh3, wl3, ec0, vw) in enumerate(cur):
                    first = True
                    for ci in range(16):
                        ec = ec0 + ci
                        for wsp, tsp in ((wh3, Th), (wl3, Th), (wh3, Tl)):
                            st = (hi == 0 and ci == 0 and wsp is wh3 and tsp is Th)
                            sp = (hi == 1 and ci == 15 and wsp is wh3 and tsp is Tl)
                            w = ([("vv", vw)] + (bw if hi == 0 else [])) if first else []
                            S.emit("tensor", lambda e, wsp=wsp, tsp=tsp, ci=ci, ec=ec, bank=bank, st=st, sp=sp: e.matmul(
                                bank[:], wsp[:, ci, 0:128], tsp[:, ec, :], start=st, stop=sp),
                                   waits=w, inc="mm")
                            first = False
                    wuse_mm[hi] = S.n("mm")
                mmY = S.n("mm")
                if dc + 1 < DC:
                    pend = a2_load_split(dc + 1)
                for _ in range(4):
                    emit_quant_tile("attn", v_init)
                S.emit("vector", lambda e, dc=dc, bank=bank: e.scalar_tensor_tensor(
                    X[:, dc, :], X[:, dc, :], 2.0, bank[:], Alu.mult, Alu.add),
                       waits=[("mm", mmY)], inc="vv")
                bank_vv[id(bank)] = S.n("vv")
            a2_mm_end = S.n("mm")
            if half == 0:
                tap("x1", F[:, 0:8192], [128, 8192])
            mmS = ln_stats()
            vR = ln_mu_r(mmS)
            for dc in range(DC):
                S.emit("vector", lambda e, dc=dc: e.tensor_tensor(e3, X[:, dc, :], mu_bc, Alu.subtract),
                       waits=([("vv", vR)] if dc == 0 else []), inc="vv")
                S.emit("vector", lambda e, dc=dc: e.tensor_tensor(X[:, dc, :], e3, r_bc, Alu.mult), inc="vv")
                v = S.n("vv")
                S.emit("scalar", lambda e, dc=dc: e.activation(x2h[:, dc, :], X[:, dc, :], ActF.Copy),
                       waits=[("vv", v)], inc="ss")
                s = S.n("ss")
                S.emit("vector", lambda e, dc=dc: e.tensor_tensor(x2l[:, dc, :], X[:, dc, :], x2h[:, dc, :], Alu.subtract),
                       waits=[("ss", s)], inc="vv")
            v_x2 = S.n("vv")
            tap("x2" if half == 0 else "x2_h1", F[:, 0:8192], [128, 8192])
            if half == 0:
                tap("x2h0", B[:, 0:8192], [128, 8192], BF16)

            ms_started = False
            oacc_first = [True] * DC
            mm_gend = 0
            for grp in range(NGRP):
                for hci in range(HGRP):
                    hc = grp * HGRP + hci
                    hw = hc // 4
                    if hc % 4 == 0:
                        buf = wq1[hw % 2]
                        need = 16 * min((hw + 1) * 8, n_fc_tiles)
                        assert need <= S.n("wq"), f"F1 wq deadlock: need {need} planned {S.n('wq')}"
                        prev_mm = max(win_mm_end.get((half, hw - 2), 0), a2_mm_end)
                        for dc in range(DC):
                            S.dma("sync", lambda e, hw=hw, dc=dc, buf=buf: e.dma_start(
                                out=buf[:, dc, :],
                                in_=qfc[512 * hw:512 * (hw + 1), 128 * dc:128 * (dc + 1)],
                                transpose=True),
                                  waits=([("wq", need), ("mm", prev_mm)] if dc == 0 else []), inc="lds")
                        ld_w1 = S.n("lds")
                    buf = wq1[hw % 2]
                    bank = pH[hc % 4]
                    first = True
                    idx = 0
                    for dc in range(DC):
                        for xs in (x2h, x2l):
                            st, sp = (idx == 0), (idx == 2 * DC - 1)
                            w = ([("lds", ld_w1), ("vv", v_x2)] + bank_waits(bank)) if first else []
                            S.emit("tensor", lambda e, buf=buf, dc=dc, hc=hc, xs=xs, st=st, sp=sp, bank=bank: e.matmul(
                                bank[:], buf[:, dc, 128 * (hc % 4):128 * (hc % 4) + 128], xs[:, dc, :], start=st, stop=sp),
                                   waits=w, inc="mm")
                            first = False
                            idx += 1
                    mmH = S.n("mm")
                    if hc % 4 == 3:
                        win_mm_end[(half, hw)] = mmH
                    par = hci % 2
                    S.emit("scalar", lambda e, bank=bank, par=par: e.activation(gsc[par], bank[:], ActF.Relu),
                           waits=[("mm", mmH)], inc="ss")
                    s = S.n("ss")
                    bank_ss[id(bank)] = s
                    S.emit("vector", lambda e, par=par: e.tensor_tensor(gsc[par], gsc[par], gsc[par], Alu.mult),
                           waits=[("ss", s)], inc="vv")
                    S.emit("vector", lambda e, par=par, hci=hci: e.tensor_copy(gh[:, hci, :], gsc[par]), inc="vv")
                    S.emit("vector", lambda e, par=par, hci=hci: e.tensor_tensor(gl[:, hci, :], gsc[par], gh[:, hci, :], Alu.subtract), inc="vv")
                    S.emit("vector", lambda e, par=par: e.tensor_tensor(g2[par], gsc[par], gsc[par], Alu.mult), inc="vv")
                    v = S.n("vv")
                    S.emit("tensor", lambda e, par=par, st=(not ms_started), hc=hc: e.matmul(
                        pS2[:], ones_r[:], g2[par], start=st, stop=(hc == HC - 1)),
                           waits=[("vv", v)], inc="mm")
                    ms_started = True
                    emit_quant_tile("ffn", v_x2)
                    emit_quant_tile("ffn", v_x2)
                    if half == 0 and hc == 0:
                        tap("g0", gsc[0], [128, THF])
                    if half == 0 and hc == 1:
                        tap("g1", gsc[1], [128, THF])
                v_g = S.n("vv")
                mm_gend = S.n("mm")
                for dcq in range(4):
                    buf2 = wq2[dcq % 2]
                    need = 16 * (n_fc_tiles + min(16 * (grp + 1), len(quant_tiles) - n_fc_tiles))
                    assert need <= S.n("wq"), f"F2 wq deadlock: need {need} planned {S.n('wq')}"
                    prev_mm = max(wq2_mm[dcq % 2], a2_mm_end)
                    for hci in range(HGRP):
                        hc = grp * HGRP + hci
                        S.dma("sync", lambda e, hc=hc, dcq=dcq, hci=hci, buf2=buf2: e.dma_start(
                            out=buf2[:, hci, :],
                            in_=qproj[512 * dcq:512 * (dcq + 1), 128 * hc:128 * (hc + 1)],
                            transpose=True),
                              waits=([("wq", need), ("mm", prev_mm)] if hci == 0 else []), inc="lds")
                    ld_w2 = S.n("lds")
                    for dci in range(4):
                        dc = dcq * 4 + dci
                        bank = pY[dci % 2]
                        first = True
                        idx = 0
                        for hci in range(HGRP):
                            for gs in (gh, gl):
                                st, sp = (idx == 0), (idx == 2 * HGRP - 1)
                                w = ([("lds", ld_w2), ("vv", v_g)] + bank_waits(bank)) if first else []
                                S.emit("tensor", lambda e, buf2=buf2, hci=hci, dci=dci, gs=gs, st=st, sp=sp, bank=bank: e.matmul(
                                    bank[:], buf2[:, hci, 128 * dci:128 * dci + 128], gs[:, hci, :], start=st, stop=sp),
                                       waits=w, inc="mm")
                                first = False
                                idx += 1
                        mmO = S.n("mm")
                        if oacc_first[dc]:
                            S.emit("vector", lambda e, dc=dc, bank=bank: e.tensor_copy(Oacc[:, dc, :], bank[:]),
                                   waits=[("mm", mmO)], inc="vv")
                            oacc_first[dc] = False
                        else:
                            S.emit("vector", lambda e, dc=dc, bank=bank: e.tensor_tensor(Oacc[:, dc, :], Oacc[:, dc, :], bank[:], Alu.add),
                                   waits=[("mm", mmO)], inc="vv")
                        bank_vv[id(bank)] = S.n("vv")
                    wq2_mm[dcq % 2] = S.n("mm")
                    emit_quant_tile("ffn", v_x2)

            S.emit("vector", lambda e: e.tensor_scalar(e3, pS2[:], 1.0 / H, EPS32, Alu.mult, Alu.add),
                   waits=[("mm", mm_gend), ("vv", S.n("vv")), ("ldg", S.n("ldg"))], inc="vv")
            vS = newton_rsqrt(r_bc)
            for dc in range(DC):
                S.emit("vector", lambda e, dc=dc: e.tensor_tensor(e3, Oacc[:, dc, :], r_bc, Alu.mult),
                       waits=([("vv", vS)] if dc == 0 else []), inc="vv")
                S.emit("vector", lambda e, dc=dc: e.tensor_tensor(X[:, dc, :], X[:, dc, :], e3, Alu.add), inc="vv")
            tap("x3" if half == 0 else "x3_h1", F[:, 0:8192], [128, 8192])
            tap("Oacc0" if half == 0 else "Oacc_h1", share, [128, 8192])
            if half == 0:
                tap("sF", r_bc, [128, THF])
            mmS = ln_stats()
            vR = ln_mu_r(mmS)
            st_tgts = []
            for dc in range(DC):
                ob = ob0 if (dc % 2 == 0) else ob1
                w = [("vv", vR)] if dc == 0 else []
                if dc >= 2:
                    w.append(("stq", st_tgts[dc - 2]))
                S.emit("vector", lambda e, dc=dc: e.tensor_tensor(e3, X[:, dc, :], mu_bc, Alu.subtract),
                       waits=w, inc="vv")
                S.emit("vector", lambda e, dc=dc, ob=ob: e.tensor_tensor(ob, e3, r_bc, Alu.mult), inc="vv")
                v = S.n("vv")
                S.dma("sync", lambda e, dc=dc, tok0=tok0, ob=ob: e.dma_start(
                    out=out.rearrange("(c p) t -> p c t", p=128)[:, dc, tok0:tok0 + THF],
                    in_=ob),
                      waits=[("vv", v)], inc="stq")
                st_tgts.append(S.n("stq"))

        if "qfc" in taps:
            t_qfc = nc.dram_tensor("tap_qfc", [H, D], BF16, kind="ExternalOutput").ap()
            S.dma("sync", lambda e: e.dma_start(out=t_qfc, in_=qfc),
                  waits=[("wq", S.n("wq"))], inc="stq")
        if "qproj" in taps:
            t_qp = nc.dram_tensor("tap_qproj", [D, H], BF16, kind="ExternalOutput").ap()
            S.dma("sync", lambda e: e.dma_start(out=t_qp, in_=qproj),
                  waits=[("wq", S.n("wq"))], inc="stq")

        S.emit("sync", lambda e: None,
               waits=[("stq", S.n("stq")), ("wq", S.n("wq")),
                      ("lds", S.n("lds")), ("ldg", S.n("ldg"))])

        def run_queue(eng_obj, name):
            for waits, fn, inc, amt in S.q[name]:
                for sname, val in waits:
                    eng_obj.wait_ge(sems[sname], val)
                ins = fn(eng_obj)
                if inc is not None:
                    if ins is None:
                        ins = eng_obj.nop()
                    ins.then_inc(sems[inc], amt)

        @blk.sync
        def _(sync):
            run_queue(sync, "sync")

        @blk.gpsimd
        def _(gpsimd):
            run_queue(gpsimd, "gpsimd")

        @blk.tensor
        def _(tensor):
            run_queue(tensor, "tensor")

        @blk.vector
        def _(vector):
            run_queue(vector, "vector")

        @blk.scalar
        def _(scalar):
            run_queue(scalar, "scalar")

    return nc


_NC_CACHE = {}


def kernel(L0_attn_x, L0_attn_gamma, L0_attn_beta, L0_attn_W_in, L0_attn_W_out,
           L0_attn_dt, L0_ffn_W_fc, L0_ffn_W_proj):
    x = np.ascontiguousarray(np.asarray(L0_attn_x, dtype=np.float32)).reshape(8192, 2048)
    winT = np.ascontiguousarray(np.asarray(L0_attn_W_in, dtype=np.float32).T)
    woutT = np.ascontiguousarray(np.asarray(L0_attn_W_out, dtype=np.float32).T)
    wfc = np.ascontiguousarray(np.asarray(L0_ffn_W_fc, dtype=np.float32))
    wproj = np.ascontiguousarray(np.asarray(L0_ffn_W_proj, dtype=np.float32))
    gamma = np.ascontiguousarray(np.asarray(L0_attn_gamma, dtype=np.float32).reshape(16, 128).T)
    beta = np.ascontiguousarray(np.asarray(L0_attn_beta, dtype=np.float32).reshape(16, 128).T)
    dt = np.asarray(L0_attn_dt, dtype=np.float32)
    dt_fl = np.ascontiguousarray(dt.reshape(1, 4096))
    dt_pc = np.ascontiguousarray(dt.reshape(32, 128).T)
    ones = np.ones((128, 128), dtype=np.float32)

    if "nc" not in _NC_CACHE:
        _NC_CACHE["nc"] = build_core()
    nc = _NC_CACHE["nc"]

    in_maps = []
    for i in range(NCORES):
        xs = np.ascontiguousarray(x[1024 * i:1024 * (i + 1), :].T)
        in_maps.append({"x": xs, "winT": winT, "woutT": woutT, "wfc": wfc,
                        "wproj": wproj, "gamma": gamma, "beta": beta,
                        "dt_fl": dt_fl, "dt_pc": dt_pc, "ones_in": ones})
    res = run_bass_kernel_spmd(nc, in_maps, core_ids=list(range(NCORES)))
    o = np.empty((8192, 2048), dtype=np.float32)
    for i in range(NCORES):
        o[1024 * i:1024 * (i + 1), :] = res.results[i]["out"].T
    return o.reshape(2, 4096, 2048)


# version 08
# speedup vs baseline: 1.9192x; 1.0313x over previous
import contextlib
import numpy as np
import concourse.bass as bass
import concourse.mybir as mybir
from concourse.bass_utils import run_bass_kernel_spmd

F32 = mybir.dt.float32
F32R = mybir.dt.float32r
BF16 = mybir.dt.bfloat16
U16 = mybir.dt.uint16
U32 = mybir.dt.uint32
I32 = mybir.dt.int32
I16 = mybir.dt.int16
Alu = mybir.AluOpType
ActF = mybir.ActivationFunctionType
AX = mybir.AxisListType

NCORES = 8
D, DC = 2048, 16
EI, ECP = 4096, 32
H, HC = 8192, 64
NT, THF = 1024, 512
HGRP = 8
NGRP = HC // HGRP
M_BND = 0.501953125
P25 = 2.0 ** -25
LN_EPS = 1e-5
EPS32 = float(np.finfo(np.float32).eps)
MAGIC = 0x5F3759DF
QW = 1024
QG = QW // 64

ENGINES = ("sync", "gpsimd", "tensor", "vector", "scalar")


class Sched:

    def __init__(self):
        self.q = {e: [] for e in ENGINES}
        self.cnt = {}

    def n(self, sem):
        return self.cnt.get(sem, 0)

    def emit(self, eng, fn, waits=(), inc=None, amt=1):
        waits = tuple((s, v) for (s, v) in waits if v > 0)
        self.q[eng].append((waits, fn, inc, amt))
        if inc is not None:
            self.cnt[inc] = self.cnt.get(inc, 0) + amt
        return self.cnt.get(inc, 0)

    def dma(self, eng, fn, waits=(), inc=None):
        return self.emit(eng, fn, waits, inc, 16)


def build_core(taps=None, barriers=None):
    taps = taps or []
    barriers = barriers or []
    nc = bass.Bass("TRN2", target_bir_lowering=False, debug=False,
                   detect_race_conditions=False)

    x_in = nc.dram_tensor("x", [D, NT], F32, kind="ExternalInput").ap()
    winT = nc.dram_tensor("winT", [D, 2 * EI], F32, kind="ExternalInput").ap()
    woutT = nc.dram_tensor("woutT", [EI, D], F32, kind="ExternalInput").ap()
    wfc = nc.dram_tensor("wfc", [H, D], F32, kind="ExternalInput").ap()
    wproj = nc.dram_tensor("wproj", [D, H], F32, kind="ExternalInput").ap()
    gamma = nc.dram_tensor("gamma", [128, DC], F32, kind="ExternalInput").ap()
    beta = nc.dram_tensor("beta", [128, DC], F32, kind="ExternalInput").ap()
    dt_fl = nc.dram_tensor("dt_fl", [1, EI], F32, kind="ExternalInput").ap()
    dt_pc = nc.dram_tensor("dt_pc", [128, ECP], F32, kind="ExternalInput").ap()
    ones_in = nc.dram_tensor("ones_in", [128, 128], F32, kind="ExternalInput").ap()
    out = nc.dram_tensor("out", [D, NT], F32, kind="ExternalOutput").ap()
    qfc = nc.dram_tensor("qfc", [H, D], BF16, kind="Internal").ap()
    qproj = nc.dram_tensor("qproj", [D, H], BF16, kind="Internal").ap()

    S = Sched()
    es = contextlib.ExitStack()
    with es:
        sb = lambda n, sh, dt: es.enter_context(nc.sbuf_tensor(n, sh, dt))
        psu = lambda n, sh: es.enter_context(nc.psum_tensor(n, sh, F32))

        B = sb("B", [128, 57344], BF16)
        xnh = B[:, 0:8192].rearrange("p (c t) -> p c t", c=DC)
        xnl = B[:, 8192:16384].rearrange("p (c t) -> p c t", c=DC)
        Th = B[:, 16384:32768].rearrange("p (c t) -> p c t", c=ECP)
        Tl = B[:, 32768:49152].rearrange("p (c t) -> p c t", c=ECP)
        whi = B[:, 49152:53248]
        wlo = B[:, 53248:57344]
        x2h = B[:, 0:8192].rearrange("p (c t) -> p c t", c=DC)
        x2l = B[:, 8192:16384].rearrange("p (c t) -> p c t", c=DC)
        gh = B[:, 16384:20480].rearrange("p (c t) -> p c t", c=HGRP)
        gl = B[:, 20480:24576].rearrange("p (c t) -> p c t", c=HGRP)
        wq1 = [B[:, 24576:32768].rearrange("p (c t) -> p c t", c=DC),
               B[:, 32768:40960].rearrange("p (c t) -> p c t", c=DC)]
        wq2 = [B[:, 40960:45056].rearrange("p (c t) -> p c t", c=HGRP),
               B[:, 45056:49152].rearrange("p (c t) -> p c t", c=HGRP)]

        F = sb("F", [128, 20480], F32)
        X = F[:, 0:8192].rearrange("p (c t) -> p c t", c=DC)
        e1 = F[:, 8192:8704]
        e2 = F[:, 8704:9216]
        e3 = F[:, 9216:9728]
        e4 = F[:, 9728:10240]
        mu_bc = F[:, 10240:10752]
        r_bc = F[:, 10752:11264]
        share = F[:, 11264:19456]
        wbuf = share[:, 0:4096]
        Oacc = share.rearrange("p (c t) -> p c t", c=DC)
        dtf = share[0:1, 4096:8192]
        qw0 = F[:, 19456:20480]
        qw1_attn = F[:, 15360:16384]
        qw1_ffn = F[:, 9728:10752]
        ob0 = F[:, 19456:19968]
        ob1 = F[:, 19968:20480]
        gsc = [e1, e2]

        R = sb("R", [128, 1024], F32R)
        xr = R[:, 0:512]
        xsq = R[:, 512:1024]
        g2 = [xr, xsq]

        IA = sb("IA", [128, 3072], U16)
        qwb = IA[:, 0:1024]
        qau = IA[:, 1024:2048]
        qmk = IA[:, 2048:3072]
        useed = IA[:, 2048:3072].bitcast(U32)

        qgs = sb("qgs", [128, QG], F32)
        qs16 = sb("qs16", [128, QG], U16)
        qsf = sb("qsf", [128, QG], U32)
        qthr = sb("qthr", [128, QG], U32)

        ones_f = sb("ones_f", [128, 128], F32)
        ones_r = sb("ones_r", [128, 128], F32R)
        ones1 = sb("ones1", [1, 128], F32)
        gam = sb("gam", [128, DC], F32)
        bet = sb("bet", [128, DC], F32)
        dtw = sb("dtw", [128, ECP], F32)
        dtp = sb("dtp", [128, ECP], F32)
        sc1 = sb("sc1", [1, 8], F32)
        mb1 = sb("mb1", [128, 1], F32)
        rb1 = sb("rb1", [128, 1], F32)

        pZ = [psu("pZ0", [128, THF]), psu("pZ1", [128, THF])]
        pG = [psu("pG0", [128, THF]), psu("pG1", [128, THF])]
        pY = [psu("pY0", [128, THF]), psu("pY1", [128, THF])]
        pH = [pZ[0], pZ[1], pG[0], pG[1]]
        pS = psu("pS", [128, THF])
        pS2 = psu("pS2", [128, THF])
        pB = pS[:, 0:8]

        sems = {}
        for s in ("lds", "ldg", "lds2", "mm", "vv", "ss", "stq", "wq"):
            sems[s] = es.enter_context(nc.semaphore(s))

        blk = es.enter_context(nc.Block())

        def tap(name, src_ap, shape, dtype=F32):
            if name not in taps:
                return
            t = nc.dram_tensor("tap_" + name, shape, dtype, kind="ExternalOutput").ap()
            S.dma("sync", lambda e: e.dma_start(out=t, in_=src_ap),
                  waits=[("vv", S.n("vv")), ("ss", S.n("ss")), ("mm", S.n("mm"))], inc="stq")
            v = S.n("stq")
            S.emit("vector", lambda e: e.nop(), waits=[("stq", v)], inc="vv")
            S.emit("scalar", lambda e: e.nop(), waits=[("stq", v)], inc="ss")

        S.dma("sync", lambda e: e.dma_start(out=gam[:], in_=gamma), inc="lds")
        S.dma("sync", lambda e: e.dma_start(out=bet[:], in_=beta), inc="lds")
        S.dma("sync", lambda e: e.dma_start(out=dtf, in_=dt_fl), inc="lds")
        S.dma("sync", lambda e: e.dma_start(out=dtp[:], in_=dt_pc), inc="lds")
        S.dma("sync", lambda e: e.dma_start(out=ones_f[:], in_=ones_in), inc="lds")
        ld0 = S.n("lds")

        S.emit("vector", lambda e: e.tensor_copy(ones_r[:], ones_f[:]),
               waits=[("lds", ld0)], inc="vv")
        S.emit("vector", lambda e: e.tensor_copy(ones1[:], ones_f[0:1, :]), inc="vv")
        S.emit("vector", lambda e: e.memset(qsf[:], 0), inc="vv")
        S.emit("vector", lambda e: e.tensor_scalar(dtf, dtf, -1.0, None, Alu.mult), inc="vv")
        vrm = S.emit("vector", lambda e: e.reduce_max(sc1[0:1, 0:1], dtf, AX.X), inc="vv")
        S.emit("vector", lambda e: e.tensor_scalar(sc1[0:1, 1:2], sc1[0:1, 0:1], -1.0, None, Alu.mult),
               waits=[("vv", vrm)], inc="vv")
        v = S.n("vv")
        S.emit("scalar", lambda e: e.activation(dtf, dtf, ActF.Exp, bias=sc1[0:1, 1:2], scale=1.0),
               waits=[("vv", v)], inc="ss")
        s = S.n("ss")
        vrs = S.emit("vector", lambda e: e.reduce_sum(sc1[0:1, 2:3], dtf, AX.X),
               waits=[("ss", s)], inc="vv")
        vrc = S.emit("vector", lambda e: e.reciprocal(sc1[0:1, 3:4], sc1[0:1, 2:3]),
               waits=[("vv", vrs)], inc="vv")
        S.emit("vector", lambda e: e.nop(), waits=[("vv", vrc)], inc="vv")
        v = S.n("vv")
        S.emit("tensor", lambda e: e.matmul(pS[:, 0:1], ones1[:], sc1[0:1, 1:2], start=True, stop=True),
               waits=[("vv", v)], inc="mm")
        S.emit("tensor", lambda e: e.matmul(pS[:, 1:2], ones1[:], sc1[0:1, 3:4], start=True, stop=True), inc="mm")
        mm = S.n("mm")
        S.emit("vector", lambda e: e.tensor_copy(mb1[:], pS[:, 0:1]), waits=[("mm", mm)], inc="vv")
        S.emit("vector", lambda e: e.tensor_copy(rb1[:], pS[:, 1:2]), inc="vv")
        S.emit("vector", lambda e: e.tensor_scalar(dtp[:], dtp[:], -1.0, None, Alu.mult), inc="vv")
        v = S.n("vv")
        S.emit("scalar", lambda e: e.activation(dtw[:], dtp[:], ActF.Exp, bias=mb1[:], scale=1.0),
               waits=[("vv", v)], inc="ss")
        s = S.n("ss")
        S.emit("vector", lambda e: e.tensor_scalar(dtw[:], dtw[:], rb1[:], None, Alu.mult),
               waits=[("ss", s)], inc="vv")
        v_init = S.n("vv")
        tap("dtw", dtw[:], [128, ECP])
        tap("sc1", sc1[:], [1, 8])
        tap("mb1", mb1[:], [128, 1])
        tap("rb1", rb1[:], [128, 1])
        tap("dtf", dtf, [1, EI])
        tap("dtp2", dtp[:], [128, ECP])

        def barrier(name):
            if name not in barriers:
                return
            w = [(s, S.n(s)) for s in ("lds", "ldg", "mm", "vv", "ss", "stq", "wq")]
            for eng in ("vector", "scalar", "tensor", "sync", "gpsimd"):
                S.emit(eng, lambda e: e.nop(), waits=w)

        def newton_rsqrt(dst):
            qau16 = qau[:, 0:512]
            S.emit("vector", lambda e: e.memset(useed, 0),
                   waits=[("wq", S.n("wq"))], inc="vv")
            S.emit("vector", lambda e: e.tensor_scalar(
                qau16, e3.bitcast(U16).rearrange("p (a two) -> p a two", two=2)[:, :, 1],
                1, None, Alu.logical_shift_right), inc="vv")
            S.emit("vector", lambda e: e.tensor_scalar(
                qau16.bitcast(I16), qau16.bitcast(I16), -1, 0x5F37, Alu.mult, Alu.add), inc="vv")
            S.emit("vector", lambda e: e.tensor_copy(
                useed.bitcast(U16).rearrange("p (a two) -> p a two", two=2)[:, :, 1],
                qau16), inc="vv")
            S.emit("vector", lambda e: e.tensor_tensor(e1, useed.bitcast(F32), useed.bitcast(F32), Alu.mult), inc="vv")
            S.emit("vector", lambda e: e.tensor_tensor(e1, e1, e3, Alu.mult), inc="vv")
            S.emit("vector", lambda e: e.tensor_scalar(e1, e1, -0.5, 1.5, Alu.mult, Alu.add), inc="vv")
            S.emit("vector", lambda e: e.tensor_tensor(e2, useed.bitcast(F32), e1, Alu.mult), inc="vv")
            S.emit("vector", lambda e: e.tensor_tensor(e1, e2, e2, Alu.mult), inc="vv")
            S.emit("vector", lambda e: e.tensor_tensor(e1, e1, e3, Alu.mult), inc="vv")
            S.emit("vector", lambda e: e.tensor_scalar(e1, e1, -0.5, 1.5, Alu.mult, Alu.add), inc="vv")
            S.emit("vector", lambda e: e.tensor_tensor(e2, e2, e1, Alu.mult), inc="vv")
            S.emit("vector", lambda e: e.tensor_tensor(e1, e2, e2, Alu.mult), inc="vv")
            S.emit("vector", lambda e: e.tensor_tensor(e1, e1, e3, Alu.mult), inc="vv")
            S.emit("vector", lambda e: e.tensor_scalar(e1, e1, -0.5, 1.5, Alu.mult, Alu.add), inc="vv")
            S.emit("vector", lambda e: e.tensor_tensor(dst, e2, e1, Alu.mult), inc="vv")
            return S.n("vv")

        def ln_stats(extra_first_waits=()):
            for dc in range(DC):
                w = list(extra_first_waits) if dc == 0 else []
                w.append(("mm", S.n("mm")))
                S.emit("vector", lambda e, dc=dc: e.tensor_copy(xr, X[:, dc, :]), waits=w, inc="vv")
                S.emit("vector", lambda e: e.tensor_tensor(xsq, xr, xr, Alu.mult), inc="vv")
                v = S.n("vv")
                S.emit("tensor", lambda e, dc=dc: e.matmul(pS[:], ones_r[:], xr, start=(dc == 0), stop=(dc == DC - 1)),
                       waits=[("vv", v)], inc="mm")
                S.emit("tensor", lambda e, dc=dc: e.matmul(pS2[:], ones_r[:], xsq, start=(dc == 0), stop=(dc == DC - 1)),
                       inc="mm")
            return S.n("mm")

        def ln_mu_r(mm_done):
            S.emit("vector", lambda e: e.tensor_scalar(mu_bc, pS[:], 1.0 / D, None, Alu.mult),
                   waits=[("mm", mm_done)], inc="vv")
            S.emit("vector", lambda e: e.tensor_tensor(e3, mu_bc, mu_bc, Alu.mult), inc="vv")
            S.emit("vector", lambda e: e.scalar_tensor_tensor(e3, pS2[:], 1.0 / D, e3, Alu.mult, Alu.subtract), inc="vv")
            S.emit("vector", lambda e: e.tensor_scalar(e3, e3, 1.0, LN_EPS, Alu.mult, Alu.add), inc="vv")
            return newton_rsqrt(r_bc)

        quant_tiles = []
        for t in range(H // 128):
            for c in range(D // QW):
                quant_tiles.append((wfc[128 * t:128 * (t + 1), QW * c:QW * (c + 1)],
                                    qfc[128 * t:128 * (t + 1), QW * c:QW * (c + 1)]))
        n_fc_tiles = len(quant_tiles)
        for hw in range(H // QW):
            for dr in range(D // 128):
                quant_tiles.append((wproj[128 * dr:128 * (dr + 1), QW * hw:QW * (hw + 1)],
                                    qproj[128 * dr:128 * (dr + 1), QW * hw:QW * (hw + 1)]))
        qt_idx = [0]
        q_par = [0]
        q_cons_vv = {0: 0, 1: 0}
        q_wb_vv = [0]

        q_loads = []

        def _emit_quant_load(phase, guard_vv):
            k = len(q_loads)
            if k >= len(quant_tiles):
                return
            src_ = quant_tiles[k][0]
            par = k % 2
            qw = qw0 if par == 0 else (qw1_attn if phase == "attn" else qw1_ffn)
            S.dma("gpsimd", lambda e, src_=src_, qw=qw: e.dma_start(out=qw, in_=src_),
                  waits=[("vv", max(q_cons_vv[par], guard_vv))], inc="ldg")
            q_loads.append((S.n("ldg"), qw))

        def emit_quant_tile(phase="attn", guard_vv=0):
            if qt_idx[0] >= len(quant_tiles):
                return
            k = qt_idx[0]
            qt_idx[0] += 1
            if k == 0:
                _emit_quant_load(phase, guard_vv)
            _emit_quant_load(phase, guard_vv)
            src_, dst = quant_tiles[k]
            par = k % 2
            ld, qw = q_loads[k]
            S.emit("scalar", lambda e, qw=qw: e.activation(qwb.bitcast(BF16), qw, ActF.Copy),
                   waits=[("ldg", ld), ("vv", q_wb_vv[0])], inc="ss")
            sc_ = S.n("ss")
            va = S.emit("vector", lambda e: e.tensor_scalar(qau, qwb, 0x7FFF, None, Alu.bitwise_and),
                   waits=[("ss", sc_)], inc="vv")
            S.emit("scalar", lambda e: e.activation(qwb.bitcast(BF16), qwb.bitcast(BF16), ActF.Sign),
                   waits=[("vv", va)], inc="ss")
            vp = S.emit("vector", lambda e: e.tensor_reduce(
                qgs[:].rearrange("p (g o) -> p g o", o=1),
                qau.bitcast(BF16).rearrange("p (g k) -> p g k", g=QG),
                AX.X, Alu.add), waits=[("vv", va)], inc="vv")
            vp = S.emit("vector", lambda e: e.tensor_scalar(qs16[:].bitcast(BF16), qgs[:], 1.0 / 64, 1e-8, Alu.mult, Alu.max),
                   waits=[("vv", vp)], inc="vv")
            vp = S.emit("vector", lambda e: e.tensor_copy(
                qsf[:].bitcast(U16).rearrange("p (g two) -> p g two", two=2)[:, :, 1],
                qs16[:].rearrange("p (g o) -> p g o", o=1)[:, :, 0]), waits=[("vv", vp)], inc="vv")
            vp = S.emit("vector", lambda e: e.tensor_scalar(qthr[:].bitcast(F32), qsf[:].bitcast(F32), M_BND, None, Alu.mult),
                   waits=[("vv", vp)], inc="vv")
            vp = S.emit("vector", lambda e: e.scalar_tensor_tensor(qthr[:].bitcast(F32), qsf[:].bitcast(F32), P25, qthr[:].bitcast(F32), Alu.mult, Alu.add),
                   waits=[("vv", vp)], inc="vv")
            vp = S.emit("vector", lambda e: e.tensor_tensor(
                qmk.bitcast(BF16).rearrange("p (g k) -> p g k", g=QG),
                qau.bitcast(BF16).rearrange("p (g k) -> p g k", g=QG),
                qthr[:].bitcast(BF16).rearrange("p (g two) -> p g two", two=2)[:, :, 1:2].broadcast_to([128, QG, 64]),
                Alu.is_gt), waits=[("wq", S.n("wq")), ("vv", vp)], inc="vv")
            vp = S.emit("vector", lambda e: e.tensor_tensor(
                qmk.bitcast(BF16).rearrange("p (g k) -> p g k", g=QG),
                qmk.bitcast(BF16).rearrange("p (g k) -> p g k", g=QG),
                qs16[:].bitcast(BF16).rearrange("p (g o) -> p g o", o=1).broadcast_to([128, QG, 64]),
                Alu.mult), waits=[("vv", vp)], inc="vv")
            S.emit("vector", lambda e: e.tensor_tensor(qmk.bitcast(BF16), qmk.bitcast(BF16), qwb.bitcast(BF16), Alu.mult),
                   waits=[("vv", vp)], inc="vv")
            v = S.n("vv")
            q_wb_vv[0] = v
            q_cons_vv[par] = v
            S.dma("gpsimd", lambda e, dst=dst: e.dma_start(out=dst, in_=qmk.bitcast(BF16)),
                  waits=[("vv", v)], inc="wq")

        win_mm_end = {}
        wsplit_vv = [0, 0]
        wuse_mm = [0, 0]
        wq2_mm = [0, 0]
        bank_vv = {}
        bank_ss = {}
        last_mm_xn = 0

        def bank_waits(bank):
            return [("vv", bank_vv.get(id(bank), 0)), ("ss", bank_ss.get(id(bank), 0))]

        for half in range(2):
            tok0 = half * THF
            xw = [("vv", S.n("vv")), ("stq", S.n("stq"))] if half else [("lds", ld0)]
            S.dma("sync", lambda e, tok0=tok0: e.dma_start(
                out=X,
                in_=x_in.rearrange("(c p) t -> p c t", p=128)[:, :, tok0:tok0 + THF]),
                  waits=xw, inc="lds")
            ldx = S.n("lds")

            mmS = ln_stats(extra_first_waits=[("lds", ldx), ("vv", v_init)])
            vR = ln_mu_r(mmS)
            for dc in range(DC):
                S.emit("vector", lambda e, dc=dc: e.tensor_tensor(e3, X[:, dc, :], mu_bc, Alu.subtract),
                       waits=([("vv", vR), ("mm", last_mm_xn)] if dc == 0 else []), inc="vv")
                S.emit("vector", lambda e: e.tensor_tensor(e3, e3, r_bc, Alu.mult), inc="vv")
                S.emit("vector", lambda e, dc=dc: e.tensor_scalar(e4, e3, gam[:, dc:dc + 1], bet[:, dc:dc + 1], Alu.mult, Alu.add), inc="vv")
                v = S.n("vv")
                S.emit("scalar", lambda e, dc=dc: e.activation(xnh[:, dc, :], e4, ActF.Copy),
                       waits=[("vv", v)], inc="ss")
                s = S.n("ss")
                S.emit("vector", lambda e, dc=dc: e.tensor_tensor(xnl[:, dc, :], e4, xnh[:, dc, :], Alu.subtract),
                       waits=[("ss", s)], inc="vv")
            v_xn = S.n("vv")
            if half == 0:
                tap("mu0", mu_bc, [128, THF])
                tap("r0", r_bc, [128, THF])
                tap("xnh0", B[:, 0:8192], [128, 8192], BF16)
                tap("xnl0", B[:, 8192:16384], [128, 8192], BF16)

            def a1_load_split(ec):
                out = []
                for hp in range(2):
                    par = hp
                    wb3 = wbuf.rearrange("p (q c e) -> p q c e", q=2, c=8)[:, par]
                    wh3 = whi.rearrange("p (q c e) -> p q c e", q=2, c=8)[:, par]
                    wl3 = wlo.rearrange("p (q c e) -> p q c e", q=2, c=8)[:, par]
                    dc0 = 8 * hp
                    S.dma("sync", lambda e, ec=ec, dc0=dc0, wb3=wb3: e.dma_start(
                        out=wb3[:, :, 0:128],
                        in_=winT.rearrange("(c p) e -> p c e", p=128)[:, dc0:dc0 + 8, 128 * ec:128 * (ec + 1)]),
                          waits=[("vv", wsplit_vv[par])], inc="lds")
                    S.dma("sync", lambda e, ec=ec, dc0=dc0, wb3=wb3: e.dma_start(
                        out=wb3[:, :, 128:256],
                        in_=winT.rearrange("(c p) e -> p c e", p=128)[:, dc0:dc0 + 8, EI + 128 * ec:EI + 128 * (ec + 1)]),
                          inc="lds")
                    ldw = S.n("lds")
                    S.emit("scalar", lambda e, wh3=wh3, wb3=wb3: e.activation(
                        wh3.rearrange("p c e -> p (c e)"), wb3.rearrange("p c e -> p (c e)"), ActF.Copy),
                           waits=[("lds", ldw), ("mm", wuse_mm[par])], inc="ss")
                    s = S.n("ss")
                    S.emit("vector", lambda e, wl3=wl3, wb3=wb3, wh3=wh3: e.tensor_tensor(
                        wl3.rearrange("p c e -> p (c e)"), wb3.rearrange("p c e -> p (c e)"),
                        wh3.rearrange("p c e -> p (c e)"), Alu.subtract),
                           waits=[("ss", s)], inc="vv")
                    wsplit_vv[par] = S.n("vv")
                    out.append((wh3, wl3, dc0, S.n("vv")))
                return out

            pend = a1_load_split(0)
            for ec in range(ECP):
                zb, gb = pZ[ec % 2], pG[ec % 2]
                bw = bank_waits(zb) + bank_waits(gb) + [("vv", v_xn)]
                cur, pend = pend, None
                for hi, (wh3, wl3, dc0, vw) in enumerate(cur):
                    first = True
                    for tgt, col0 in ((zb, 0), (gb, 128)):
                        for ci in range(8):
                            dc = dc0 + ci
                            for wsp, xsp in ((wh3, xnh), (wl3, xnh), (wh3, xnl)):
                                st = (hi == 0 and ci == 0 and wsp is wh3 and xsp is xnh)
                                sp = (hi == 1 and ci == 7 and wsp is wh3 and xsp is xnl)
                                w = ([("vv", vw)] + (bw if hi == 0 else [])) if first else []
                                S.emit("tensor", lambda e, wsp=wsp, xsp=xsp, ci=ci, dc=dc, tgt=tgt, col0=col0, st=st, sp=sp: e.matmul(
                                    tgt[:], wsp[:, ci, col0:col0 + 128], xsp[:, dc, :], start=st, stop=sp),
                                       waits=w, inc="mm")
                                first = False
                    wuse_mm[hi] = S.n("mm")
                mmA = S.n("mm")
                last_mm_xn = mmA
                if ec + 1 < ECP:
                    pend = a1_load_split(ec + 1)
                for _ in range(2):
                    emit_quant_tile("attn", v_init)
                S.emit("vector", lambda e, zb=zb: e.tensor_scalar(e1, zb[:], 15.0, -15.0, Alu.min, Alu.max),
                       waits=[("mm", mmA)], inc="vv")
                v1 = S.n("vv")
                S.emit("scalar", lambda e: e.activation(e2, e1, ActF.Sigmoid),
                       waits=[("vv", v1)], inc="ss")
                s1 = S.n("ss")
                S.emit("vector", lambda e, gb=gb: e.tensor_scalar(e3, gb[:], 15.0, -15.0, Alu.min, Alu.max), inc="vv")
                v2 = S.n("vv")
                S.emit("scalar", lambda e: e.activation(e4, e3, ActF.Sigmoid),
                       waits=[("vv", v2)], inc="ss")
                s2 = S.n("ss")
                S.emit("vector", lambda e, ec=ec, zb=zb: e.scalar_tensor_tensor(e1, zb[:], dtw[:, ec:ec + 1], e2, Alu.mult, Alu.mult),
                       waits=[("ss", s1)], inc="vv")
                S.emit("vector", lambda e: e.tensor_tensor(e3, e3, e4, Alu.mult),
                       waits=[("ss", s2)], inc="vv")
                S.emit("vector", lambda e: e.tensor_tensor(e1, e1, e3, Alu.mult), inc="vv")
                vT = S.n("vv")
                S.emit("scalar", lambda e, ec=ec: e.activation(Th[:, ec, :], e1, ActF.Copy),
                       waits=[("vv", vT)], inc="ss")
                sT = S.n("ss")
                S.emit("vector", lambda e, ec=ec: e.tensor_tensor(Tl[:, ec, :], e1, Th[:, ec, :], Alu.subtract),
                       waits=[("ss", sT)], inc="vv")
                bank_vv[id(zb)] = bank_vv[id(gb)] = S.n("vv")
            v_T = S.n("vv")
            if half == 0:
                tap("Th0", B[:, 16384:32768], [128, 16384], BF16)
                tap("Tl0", B[:, 32768:49152], [128, 16384], BF16)

            def a2_load_split(dc):
                out = []
                for hp in range(2):
                    par = hp
                    wb3 = wbuf.rearrange("p (q c e) -> p q c e", q=2, c=16)[:, par]
                    wh3 = whi.rearrange("p (q c e) -> p q c e", q=2, c=16)[:, par]
                    wl3 = wlo.rearrange("p (q c e) -> p q c e", q=2, c=16)[:, par]
                    ec0 = 16 * hp
                    S.dma("sync", lambda e, dc=dc, ec0=ec0, wb3=wb3: e.dma_start(
                        out=wb3[:],
                        in_=woutT.rearrange("(c p) d -> p c d", p=128)[:, ec0:ec0 + 16, 128 * dc:128 * (dc + 1)]),
                          waits=[("vv", wsplit_vv[par])], inc="lds")
                    ldw = S.n("lds")
                    S.emit("scalar", lambda e, wh3=wh3, wb3=wb3: e.activation(
                        wh3.rearrange("p c e -> p (c e)"), wb3.rearrange("p c e -> p (c e)"), ActF.Copy),
                           waits=[("lds", ldw), ("mm", wuse_mm[par])], inc="ss")
                    s = S.n("ss")
                    S.emit("vector", lambda e, wl3=wl3, wb3=wb3, wh3=wh3: e.tensor_tensor(
                        wl3.rearrange("p c e -> p (c e)"), wb3.rearrange("p c e -> p (c e)"),
                        wh3.rearrange("p c e -> p (c e)"), Alu.subtract),
                           waits=[("ss", s)], inc="vv")
                    wsplit_vv[par] = S.n("vv")
                    out.append((wh3, wl3, ec0, S.n("vv")))
                return out

            pend = a2_load_split(0)
            for dc in range(DC):
                bank = pY[dc % 2]
                bw = bank_waits(bank) + [("vv", v_T)]
                cur, pend = pend, None
                for hi, (wh3, wl3, ec0, vw) in enumerate(cur):
                    first = True
                    for ci in range(16):
                        ec = ec0 + ci
                        for wsp, tsp in ((wh3, Th), (wl3, Th), (wh3, Tl)):
                            st = (hi == 0 and ci == 0 and wsp is wh3 and tsp is Th)
                            sp = (hi == 1 and ci == 15 and wsp is wh3 and tsp is Tl)
                            w = ([("vv", vw)] + (bw if hi == 0 else [])) if first else []
                            S.emit("tensor", lambda e, wsp=wsp, tsp=tsp, ci=ci, ec=ec, bank=bank, st=st, sp=sp: e.matmul(
                                bank[:], wsp[:, ci, 0:128], tsp[:, ec, :], start=st, stop=sp),
                                   waits=w, inc="mm")
                            first = False
                    wuse_mm[hi] = S.n("mm")
                mmY = S.n("mm")
                if dc + 1 < DC:
                    pend = a2_load_split(dc + 1)
                for _ in range(4):
                    emit_quant_tile("attn", v_init)
                S.emit("vector", lambda e, dc=dc, bank=bank: e.scalar_tensor_tensor(
                    X[:, dc, :], X[:, dc, :], 2.0, bank[:], Alu.mult, Alu.add),
                       waits=[("mm", mmY)], inc="vv")
                bank_vv[id(bank)] = S.n("vv")
            a2_mm_end = S.n("mm")
            if half == 0:
                tap("x1", F[:, 0:8192], [128, 8192])
            mmS = ln_stats()
            vR = ln_mu_r(mmS)
            for dc in range(DC):
                S.emit("vector", lambda e, dc=dc: e.tensor_tensor(e3, X[:, dc, :], mu_bc, Alu.subtract),
                       waits=([("vv", vR)] if dc == 0 else []), inc="vv")
                S.emit("vector", lambda e, dc=dc: e.tensor_tensor(X[:, dc, :], e3, r_bc, Alu.mult), inc="vv")
                v = S.n("vv")
                S.emit("scalar", lambda e, dc=dc: e.activation(x2h[:, dc, :], X[:, dc, :], ActF.Copy),
                       waits=[("vv", v)], inc="ss")
                s = S.n("ss")
                S.emit("vector", lambda e, dc=dc: e.tensor_tensor(x2l[:, dc, :], X[:, dc, :], x2h[:, dc, :], Alu.subtract),
                       waits=[("ss", s)], inc="vv")
            v_x2 = S.n("vv")
            tap("x2" if half == 0 else "x2_h1", F[:, 0:8192], [128, 8192])
            if half == 0:
                tap("x2h0", B[:, 0:8192], [128, 8192], BF16)

            ms_started = False
            oacc_first = [True] * DC
            mm_gend = 0
            for grp in range(NGRP):
                for hci in range(HGRP):
                    hc = grp * HGRP + hci
                    hw = hc // 4
                    if hc % 4 == 0:
                        buf = wq1[hw % 2]
                        need = 16 * min((hw + 1) * 8, n_fc_tiles)
                        assert need <= S.n("wq"), f"F1 wq deadlock: need {need} planned {S.n('wq')}"
                        prev_mm = max(win_mm_end.get((half, hw - 2), 0), a2_mm_end)
                        for dc in range(DC):
                            S.dma("sync", lambda e, hw=hw, dc=dc, buf=buf: e.dma_start(
                                out=buf[:, dc, :],
                                in_=qfc[512 * hw:512 * (hw + 1), 128 * dc:128 * (dc + 1)],
                                transpose=True),
                                  waits=([("wq", need), ("mm", prev_mm)] if dc == 0 else []), inc="lds")
                        ld_w1 = S.n("lds")
                    buf = wq1[hw % 2]
                    bank = pH[hc % 4]
                    first = True
                    idx = 0
                    for dc in range(DC):
                        for xs in (x2h, x2l):
                            st, sp = (idx == 0), (idx == 2 * DC - 1)
                            w = ([("lds", ld_w1), ("vv", v_x2)] + bank_waits(bank)) if first else []
                            S.emit("tensor", lambda e, buf=buf, dc=dc, hc=hc, xs=xs, st=st, sp=sp, bank=bank: e.matmul(
                                bank[:], buf[:, dc, 128 * (hc % 4):128 * (hc % 4) + 128], xs[:, dc, :], start=st, stop=sp),
                                   waits=w, inc="mm")
                            first = False
                            idx += 1
                    mmH = S.n("mm")
                    if hc % 4 == 3:
                        win_mm_end[(half, hw)] = mmH
                    par = hci % 2
                    S.emit("scalar", lambda e, bank=bank, par=par: e.activation(gsc[par], bank[:], ActF.Relu),
                           waits=[("mm", mmH)], inc="ss")
                    s = S.n("ss")
                    bank_ss[id(bank)] = s
                    S.emit("vector", lambda e, par=par: e.tensor_tensor(gsc[par], gsc[par], gsc[par], Alu.mult),
                           waits=[("ss", s)], inc="vv")
                    S.emit("vector", lambda e, par=par, hci=hci: e.tensor_copy(gh[:, hci, :], gsc[par]), inc="vv")
                    S.emit("vector", lambda e, par=par, hci=hci: e.tensor_tensor(gl[:, hci, :], gsc[par], gh[:, hci, :], Alu.subtract), inc="vv")
                    S.emit("vector", lambda e, par=par: e.tensor_tensor(g2[par], gsc[par], gsc[par], Alu.mult), inc="vv")
                    v = S.n("vv")
                    S.emit("tensor", lambda e, par=par, st=(not ms_started), hc=hc: e.matmul(
                        pS2[:], ones_r[:], g2[par], start=st, stop=(hc == HC - 1)),
                           waits=[("vv", v)], inc="mm")
                    ms_started = True
                    emit_quant_tile("ffn", v_x2)
                    emit_quant_tile("ffn", v_x2)
                    if half == 0 and hc == 0:
                        tap("g0", gsc[0], [128, THF])
                    if half == 0 and hc == 1:
                        tap("g1", gsc[1], [128, THF])
                v_g = S.n("vv")
                mm_gend = S.n("mm")
                for dcq in range(4):
                    buf2 = wq2[dcq % 2]
                    need = 16 * (n_fc_tiles + min(16 * (grp + 1), len(quant_tiles) - n_fc_tiles))
                    assert need <= S.n("wq"), f"F2 wq deadlock: need {need} planned {S.n('wq')}"
                    prev_mm = max(wq2_mm[dcq % 2], a2_mm_end)
                    for hci in range(HGRP):
                        hc = grp * HGRP + hci
                        S.dma("sync", lambda e, hc=hc, dcq=dcq, hci=hci, buf2=buf2: e.dma_start(
                            out=buf2[:, hci, :],
                            in_=qproj[512 * dcq:512 * (dcq + 1), 128 * hc:128 * (hc + 1)],
                            transpose=True),
                              waits=([("wq", need), ("mm", prev_mm)] if hci == 0 else []), inc="lds")
                    ld_w2 = S.n("lds")
                    for dci in range(4):
                        dc = dcq * 4 + dci
                        bank = pY[dci % 2]
                        first = True
                        idx = 0
                        for hci in range(HGRP):
                            for gs in (gh, gl):
                                st, sp = (idx == 0), (idx == 2 * HGRP - 1)
                                w = ([("lds", ld_w2), ("vv", v_g)] + bank_waits(bank)) if first else []
                                S.emit("tensor", lambda e, buf2=buf2, hci=hci, dci=dci, gs=gs, st=st, sp=sp, bank=bank: e.matmul(
                                    bank[:], buf2[:, hci, 128 * dci:128 * dci + 128], gs[:, hci, :], start=st, stop=sp),
                                       waits=w, inc="mm")
                                first = False
                                idx += 1
                        mmO = S.n("mm")
                        if oacc_first[dc]:
                            S.emit("vector", lambda e, dc=dc, bank=bank: e.tensor_copy(Oacc[:, dc, :], bank[:]),
                                   waits=[("mm", mmO)], inc="vv")
                            oacc_first[dc] = False
                        else:
                            S.emit("vector", lambda e, dc=dc, bank=bank: e.tensor_tensor(Oacc[:, dc, :], Oacc[:, dc, :], bank[:], Alu.add),
                                   waits=[("mm", mmO)], inc="vv")
                        bank_vv[id(bank)] = S.n("vv")
                    wq2_mm[dcq % 2] = S.n("mm")
                    emit_quant_tile("ffn", v_x2)

            S.emit("vector", lambda e: e.tensor_scalar(e3, pS2[:], 1.0 / H, EPS32, Alu.mult, Alu.add),
                   waits=[("mm", mm_gend), ("vv", S.n("vv")), ("ldg", S.n("ldg"))], inc="vv")
            vS = newton_rsqrt(r_bc)
            for dc in range(DC):
                S.emit("vector", lambda e, dc=dc: e.tensor_tensor(e3, Oacc[:, dc, :], r_bc, Alu.mult),
                       waits=([("vv", vS)] if dc == 0 else []), inc="vv")
                S.emit("vector", lambda e, dc=dc: e.tensor_tensor(X[:, dc, :], X[:, dc, :], e3, Alu.add), inc="vv")
            tap("x3" if half == 0 else "x3_h1", F[:, 0:8192], [128, 8192])
            tap("Oacc0" if half == 0 else "Oacc_h1", share, [128, 8192])
            if half == 0:
                tap("sF", r_bc, [128, THF])
            mmS = ln_stats()
            vR = ln_mu_r(mmS)
            st_tgts = []
            for dc in range(DC):
                ob = ob0 if (dc % 2 == 0) else ob1
                w = [("vv", vR)] if dc == 0 else []
                if dc >= 2:
                    w.append(("stq", st_tgts[dc - 2]))
                S.emit("vector", lambda e, dc=dc: e.tensor_tensor(e3, X[:, dc, :], mu_bc, Alu.subtract),
                       waits=w, inc="vv")
                S.emit("vector", lambda e, dc=dc, ob=ob: e.tensor_tensor(ob, e3, r_bc, Alu.mult), inc="vv")
                v = S.n("vv")
                S.dma("sync", lambda e, dc=dc, tok0=tok0, ob=ob: e.dma_start(
                    out=out.rearrange("(c p) t -> p c t", p=128)[:, dc, tok0:tok0 + THF],
                    in_=ob),
                      waits=[("vv", v)], inc="stq")
                st_tgts.append(S.n("stq"))

        if "qfc" in taps:
            t_qfc = nc.dram_tensor("tap_qfc", [H, D], BF16, kind="ExternalOutput").ap()
            S.dma("sync", lambda e: e.dma_start(out=t_qfc, in_=qfc),
                  waits=[("wq", S.n("wq"))], inc="stq")
        if "qproj" in taps:
            t_qp = nc.dram_tensor("tap_qproj", [D, H], BF16, kind="ExternalOutput").ap()
            S.dma("sync", lambda e: e.dma_start(out=t_qp, in_=qproj),
                  waits=[("wq", S.n("wq"))], inc="stq")

        S.emit("sync", lambda e: None,
               waits=[("stq", S.n("stq")), ("wq", S.n("wq")),
                      ("lds", S.n("lds")), ("ldg", S.n("ldg"))])

        def run_queue(eng_obj, name):
            for waits, fn, inc, amt in S.q[name]:
                for sname, val in waits:
                    eng_obj.wait_ge(sems[sname], val)
                ins = fn(eng_obj)
                if inc is not None:
                    if ins is None:
                        ins = eng_obj.nop()
                    ins.then_inc(sems[inc], amt)

        @blk.sync
        def _(sync):
            run_queue(sync, "sync")

        @blk.gpsimd
        def _(gpsimd):
            run_queue(gpsimd, "gpsimd")

        @blk.tensor
        def _(tensor):
            run_queue(tensor, "tensor")

        @blk.vector
        def _(vector):
            run_queue(vector, "vector")

        @blk.scalar
        def _(scalar):
            run_queue(scalar, "scalar")

    return nc


_NC_CACHE = {}


def kernel(L0_attn_x, L0_attn_gamma, L0_attn_beta, L0_attn_W_in, L0_attn_W_out,
           L0_attn_dt, L0_ffn_W_fc, L0_ffn_W_proj):
    x = np.ascontiguousarray(np.asarray(L0_attn_x, dtype=np.float32)).reshape(8192, 2048)
    winT = np.ascontiguousarray(np.asarray(L0_attn_W_in, dtype=np.float32).T)
    woutT = np.ascontiguousarray(np.asarray(L0_attn_W_out, dtype=np.float32).T)
    wfc = np.ascontiguousarray(np.asarray(L0_ffn_W_fc, dtype=np.float32))
    wproj = np.ascontiguousarray(np.asarray(L0_ffn_W_proj, dtype=np.float32))
    gamma = np.ascontiguousarray(np.asarray(L0_attn_gamma, dtype=np.float32).reshape(16, 128).T)
    beta = np.ascontiguousarray(np.asarray(L0_attn_beta, dtype=np.float32).reshape(16, 128).T)
    dt = np.asarray(L0_attn_dt, dtype=np.float32)
    dt_fl = np.ascontiguousarray(dt.reshape(1, 4096))
    dt_pc = np.ascontiguousarray(dt.reshape(32, 128).T)
    ones = np.ones((128, 128), dtype=np.float32)

    if "nc" not in _NC_CACHE:
        _NC_CACHE["nc"] = build_core()
    nc = _NC_CACHE["nc"]

    in_maps = []
    for i in range(NCORES):
        xs = np.ascontiguousarray(x[1024 * i:1024 * (i + 1), :].T)
        in_maps.append({"x": xs, "winT": winT, "woutT": woutT, "wfc": wfc,
                        "wproj": wproj, "gamma": gamma, "beta": beta,
                        "dt_fl": dt_fl, "dt_pc": dt_pc, "ones_in": ones})
    res = run_bass_kernel_spmd(nc, in_maps, core_ids=list(range(NCORES)))
    o = np.empty((8192, 2048), dtype=np.float32)
    for i in range(NCORES):
        o[1024 * i:1024 * (i + 1), :] = res.results[i]["out"].T
    return o.reshape(2, 4096, 2048)


# version 10
# speedup vs baseline: 1.9240x; 1.0025x over previous
import contextlib
import numpy as np
import concourse.bass as bass
import concourse.mybir as mybir
from concourse.bass_utils import run_bass_kernel_spmd

F32 = mybir.dt.float32
F32R = mybir.dt.float32r
BF16 = mybir.dt.bfloat16
U16 = mybir.dt.uint16
U32 = mybir.dt.uint32
I32 = mybir.dt.int32
I16 = mybir.dt.int16
Alu = mybir.AluOpType
ActF = mybir.ActivationFunctionType
AX = mybir.AxisListType

NCORES = 8
D, DC = 2048, 16
EI, ECP = 4096, 32
H, HC = 8192, 64
NT, THF = 1024, 512
HGRP = 8
NGRP = HC // HGRP
M_BND = 0.501953125
P25 = 2.0 ** -25
LN_EPS = 1e-5
EPS32 = float(np.finfo(np.float32).eps)
MAGIC = 0x5F3759DF
QW = 1024
QG = QW // 64

ENGINES = ("sync", "gpsimd", "tensor", "vector", "scalar")


class Sched:

    def __init__(self):
        self.q = {e: [] for e in ENGINES}
        self.cnt = {}

    def n(self, sem):
        return self.cnt.get(sem, 0)

    def emit(self, eng, fn, waits=(), inc=None, amt=1):
        waits = tuple((s, v) for (s, v) in waits if v > 0)
        self.q[eng].append((waits, fn, inc, amt))
        if inc is not None:
            self.cnt[inc] = self.cnt.get(inc, 0) + amt
        return self.cnt.get(inc, 0)

    def dma(self, eng, fn, waits=(), inc=None):
        return self.emit(eng, fn, waits, inc, 16)


def build_core(taps=None, barriers=None):
    taps = taps or []
    barriers = barriers or []
    nc = bass.Bass("TRN2", target_bir_lowering=False, debug=False,
                   detect_race_conditions=False)

    x_in = nc.dram_tensor("x", [D, NT], F32, kind="ExternalInput").ap()
    winT = nc.dram_tensor("winT", [D, 2 * EI], F32, kind="ExternalInput").ap()
    woutT = nc.dram_tensor("woutT", [EI, D], F32, kind="ExternalInput").ap()
    wfc = nc.dram_tensor("wfc", [H, D], F32, kind="ExternalInput").ap()
    wproj = nc.dram_tensor("wproj", [D, H], F32, kind="ExternalInput").ap()
    gamma = nc.dram_tensor("gamma", [128, DC], F32, kind="ExternalInput").ap()
    beta = nc.dram_tensor("beta", [128, DC], F32, kind="ExternalInput").ap()
    dt_fl = nc.dram_tensor("dt_fl", [1, EI], F32, kind="ExternalInput").ap()
    dt_pc = nc.dram_tensor("dt_pc", [128, ECP], F32, kind="ExternalInput").ap()
    ones_in = nc.dram_tensor("ones_in", [128, 128], F32, kind="ExternalInput").ap()
    out = nc.dram_tensor("out", [D, NT], F32, kind="ExternalOutput").ap()
    qfc = nc.dram_tensor("qfc", [H, D], BF16, kind="Internal").ap()
    qproj = nc.dram_tensor("qproj", [D, H], BF16, kind="Internal").ap()

    S = Sched()
    es = contextlib.ExitStack()
    with es:
        sb = lambda n, sh, dt: es.enter_context(nc.sbuf_tensor(n, sh, dt))
        psu = lambda n, sh: es.enter_context(nc.psum_tensor(n, sh, F32))

        B = sb("B", [128, 57344], BF16)
        xnh = B[:, 0:8192].rearrange("p (c t) -> p c t", c=DC)
        xnl = B[:, 8192:16384].rearrange("p (c t) -> p c t", c=DC)
        Th = B[:, 16384:32768].rearrange("p (c t) -> p c t", c=ECP)
        Tl = B[:, 32768:49152].rearrange("p (c t) -> p c t", c=ECP)
        whi = B[:, 49152:53248]
        wlo = B[:, 53248:57344]
        x2h = B[:, 0:8192].rearrange("p (c t) -> p c t", c=DC)
        x2l = B[:, 8192:16384].rearrange("p (c t) -> p c t", c=DC)
        gh = B[:, 16384:20480].rearrange("p (c t) -> p c t", c=HGRP)
        gl = B[:, 20480:24576].rearrange("p (c t) -> p c t", c=HGRP)
        wq1 = [B[:, 24576:32768].rearrange("p (c t) -> p c t", c=DC),
               B[:, 32768:40960].rearrange("p (c t) -> p c t", c=DC)]
        wq2 = [B[:, 40960:45056].rearrange("p (c t) -> p c t", c=HGRP),
               B[:, 45056:49152].rearrange("p (c t) -> p c t", c=HGRP)]

        F = sb("F", [128, 20480], F32)
        X = F[:, 0:8192].rearrange("p (c t) -> p c t", c=DC)
        e1 = F[:, 8192:8704]
        e2 = F[:, 8704:9216]
        e3 = F[:, 9216:9728]
        e4 = F[:, 9728:10240]
        mu_bc = F[:, 10240:10752]
        r_bc = F[:, 10752:11264]
        share = F[:, 11264:19456]
        wbuf = share[:, 0:4096]
        Oacc = share.rearrange("p (c t) -> p c t", c=DC)
        dtf = share[0:1, 4096:8192]
        qw0 = F[:, 19456:20480]
        qw1_attn = F[:, 15360:16384]
        qw1_ffn = F[:, 9728:10752]
        ob0 = F[:, 19456:19968]
        ob1 = F[:, 19968:20480]
        gsc = [e1, e2]

        R = sb("R", [128, 1024], F32R)
        xr = R[:, 0:512]
        xsq = R[:, 512:1024]
        g2 = [xr, xsq]

        IA = sb("IA", [128, 3072], U16)
        qwb = IA[:, 0:1024]
        qau = IA[:, 1024:2048]
        qmk = IA[:, 2048:3072]
        useed = IA[:, 2048:3072].bitcast(U32)

        qgs = sb("qgs", [128, QG], F32)
        qs16 = sb("qs16", [128, QG], U16)
        qsf = sb("qsf", [128, QG], U32)
        qthr = sb("qthr", [128, QG], U32)

        ones_f = sb("ones_f", [128, 128], F32)
        ones_r = sb("ones_r", [128, 128], F32R)
        ones1 = sb("ones1", [1, 128], F32)
        gam = sb("gam", [128, DC], F32)
        bet = sb("bet", [128, DC], F32)
        dtw = sb("dtw", [128, ECP], F32)
        dtp = sb("dtp", [128, ECP], F32)
        sc1 = sb("sc1", [1, 8], F32)
        mb1 = sb("mb1", [128, 1], F32)
        rb1 = sb("rb1", [128, 1], F32)

        pZ = [psu("pZ0", [128, THF]), psu("pZ1", [128, THF])]
        pG = [psu("pG0", [128, THF]), psu("pG1", [128, THF])]
        pY = [psu("pY0", [128, THF]), psu("pY1", [128, THF])]
        pH = [pZ[0], pZ[1], pG[0], pG[1]]
        pS = psu("pS", [128, THF])
        pS2 = psu("pS2", [128, THF])
        pB = pS[:, 0:8]

        sems = {}
        for s in ("lds", "ldg", "lds2", "mm", "vv", "ss", "stq", "wq"):
            sems[s] = es.enter_context(nc.semaphore(s))

        blk = es.enter_context(nc.Block())

        def tap(name, src_ap, shape, dtype=F32):
            if name not in taps:
                return
            t = nc.dram_tensor("tap_" + name, shape, dtype, kind="ExternalOutput").ap()
            S.dma("sync", lambda e: e.dma_start(out=t, in_=src_ap),
                  waits=[("vv", S.n("vv")), ("ss", S.n("ss")), ("mm", S.n("mm"))], inc="stq")
            v = S.n("stq")
            S.emit("vector", lambda e: e.nop(), waits=[("stq", v)], inc="vv")
            S.emit("scalar", lambda e: e.nop(), waits=[("stq", v)], inc="ss")

        S.dma("sync", lambda e: e.dma_start(out=gam[:], in_=gamma), inc="lds")
        S.dma("sync", lambda e: e.dma_start(out=bet[:], in_=beta), inc="lds")
        S.dma("sync", lambda e: e.dma_start(out=dtf, in_=dt_fl), inc="lds")
        S.dma("sync", lambda e: e.dma_start(out=dtp[:], in_=dt_pc), inc="lds")
        S.dma("sync", lambda e: e.dma_start(out=ones_f[:], in_=ones_in), inc="lds")
        ld0 = S.n("lds")

        S.emit("vector", lambda e: e.tensor_copy(ones_r[:], ones_f[:]),
               waits=[("lds", ld0)], inc="vv")
        S.emit("vector", lambda e: e.tensor_copy(ones1[:], ones_f[0:1, :]), inc="vv")
        S.emit("vector", lambda e: e.memset(qsf[:], 0), inc="vv")
        S.emit("vector", lambda e: e.tensor_scalar(dtf, dtf, -1.0, None, Alu.mult), inc="vv")
        vrm = S.emit("vector", lambda e: e.reduce_max(sc1[0:1, 0:1], dtf, AX.X), inc="vv")
        S.emit("vector", lambda e: e.tensor_scalar(sc1[0:1, 1:2], sc1[0:1, 0:1], -1.0, None, Alu.mult),
               waits=[("vv", vrm)], inc="vv")
        v = S.n("vv")
        S.emit("scalar", lambda e: e.activation(dtf, dtf, ActF.Exp, bias=sc1[0:1, 1:2], scale=1.0),
               waits=[("vv", v)], inc="ss")
        s = S.n("ss")
        vrs = S.emit("vector", lambda e: e.reduce_sum(sc1[0:1, 2:3], dtf, AX.X),
               waits=[("ss", s)], inc="vv")
        vrc = S.emit("vector", lambda e: e.reciprocal(sc1[0:1, 3:4], sc1[0:1, 2:3]),
               waits=[("vv", vrs)], inc="vv")
        S.emit("vector", lambda e: e.nop(), waits=[("vv", vrc)], inc="vv")
        v = S.n("vv")
        S.emit("tensor", lambda e: e.matmul(pS[:, 0:1], ones1[:], sc1[0:1, 1:2], start=True, stop=True),
               waits=[("vv", v)], inc="mm")
        S.emit("tensor", lambda e: e.matmul(pS[:, 1:2], ones1[:], sc1[0:1, 3:4], start=True, stop=True), inc="mm")
        mm = S.n("mm")
        S.emit("vector", lambda e: e.tensor_copy(mb1[:], pS[:, 0:1]), waits=[("mm", mm)], inc="vv")
        S.emit("vector", lambda e: e.tensor_copy(rb1[:], pS[:, 1:2]), inc="vv")
        S.emit("vector", lambda e: e.tensor_scalar(dtp[:], dtp[:], -1.0, None, Alu.mult), inc="vv")
        v = S.n("vv")
        S.emit("scalar", lambda e: e.activation(dtw[:], dtp[:], ActF.Exp, bias=mb1[:], scale=1.0),
               waits=[("vv", v)], inc="ss")
        s = S.n("ss")
        S.emit("vector", lambda e: e.tensor_scalar(dtw[:], dtw[:], rb1[:], None, Alu.mult),
               waits=[("ss", s)], inc="vv")
        v_init = S.n("vv")
        tap("dtw", dtw[:], [128, ECP])
        tap("sc1", sc1[:], [1, 8])
        tap("mb1", mb1[:], [128, 1])
        tap("rb1", rb1[:], [128, 1])
        tap("dtf", dtf, [1, EI])
        tap("dtp2", dtp[:], [128, ECP])

        def barrier(name):
            if name not in barriers:
                return
            w = [(s, S.n(s)) for s in ("lds", "ldg", "mm", "vv", "ss", "stq", "wq")]
            for eng in ("vector", "scalar", "tensor", "sync", "gpsimd"):
                S.emit(eng, lambda e: e.nop(), waits=w)

        def newton_rsqrt(dst):
            qau16 = qau[:, 0:512]
            S.emit("vector", lambda e: e.memset(useed, 0),
                   waits=[("wq", S.n("wq"))], inc="vv")
            S.emit("vector", lambda e: e.tensor_scalar(
                qau16, e3.bitcast(U16).rearrange("p (a two) -> p a two", two=2)[:, :, 1],
                1, None, Alu.logical_shift_right), inc="vv")
            S.emit("vector", lambda e: e.tensor_scalar(
                qau16.bitcast(I16), qau16.bitcast(I16), -1, 0x5F37, Alu.mult, Alu.add), inc="vv")
            S.emit("vector", lambda e: e.tensor_copy(
                useed.bitcast(U16).rearrange("p (a two) -> p a two", two=2)[:, :, 1],
                qau16), inc="vv")
            S.emit("vector", lambda e: e.tensor_tensor(e1, useed.bitcast(F32), useed.bitcast(F32), Alu.mult), inc="vv")
            S.emit("vector", lambda e: e.tensor_tensor(e1, e1, e3, Alu.mult), inc="vv")
            S.emit("vector", lambda e: e.tensor_scalar(e1, e1, -0.5, 1.5, Alu.mult, Alu.add), inc="vv")
            S.emit("vector", lambda e: e.tensor_tensor(e2, useed.bitcast(F32), e1, Alu.mult), inc="vv")
            S.emit("vector", lambda e: e.tensor_tensor(e1, e2, e2, Alu.mult), inc="vv")
            S.emit("vector", lambda e: e.tensor_tensor(e1, e1, e3, Alu.mult), inc="vv")
            S.emit("vector", lambda e: e.tensor_scalar(e1, e1, -0.5, 1.5, Alu.mult, Alu.add), inc="vv")
            S.emit("vector", lambda e: e.tensor_tensor(e2, e2, e1, Alu.mult), inc="vv")
            S.emit("vector", lambda e: e.tensor_tensor(e1, e2, e2, Alu.mult), inc="vv")
            S.emit("vector", lambda e: e.tensor_tensor(e1, e1, e3, Alu.mult), inc="vv")
            S.emit("vector", lambda e: e.tensor_scalar(e1, e1, -0.5, 1.5, Alu.mult, Alu.add), inc="vv")
            S.emit("vector", lambda e: e.tensor_tensor(dst, e2, e1, Alu.mult), inc="vv")
            return S.n("vv")

        def ln_stats(extra_first_waits=()):
            for dc in range(DC):
                w = list(extra_first_waits) if dc == 0 else []
                w.append(("mm", S.n("mm")))
                S.emit("vector", lambda e, dc=dc: e.tensor_copy(xr, X[:, dc, :]), waits=w, inc="vv")
                S.emit("vector", lambda e: e.tensor_tensor(xsq, xr, xr, Alu.mult), inc="vv")
                v = S.n("vv")
                S.emit("tensor", lambda e, dc=dc: e.matmul(pS[:], ones_r[:], xr, start=(dc == 0), stop=(dc == DC - 1)),
                       waits=[("vv", v)], inc="mm")
                S.emit("tensor", lambda e, dc=dc: e.matmul(pS2[:], ones_r[:], xsq, start=(dc == 0), stop=(dc == DC - 1)),
                       inc="mm")
            return S.n("mm")

        def ln_mu_r(mm_done):
            S.emit("vector", lambda e: e.tensor_scalar(mu_bc, pS[:], 1.0 / D, None, Alu.mult),
                   waits=[("mm", mm_done)], inc="vv")
            S.emit("vector", lambda e: e.tensor_tensor(e3, mu_bc, mu_bc, Alu.mult), inc="vv")
            S.emit("vector", lambda e: e.scalar_tensor_tensor(e3, pS2[:], 1.0 / D, e3, Alu.mult, Alu.subtract), inc="vv")
            S.emit("vector", lambda e: e.tensor_scalar(e3, e3, 1.0, LN_EPS, Alu.mult, Alu.add), inc="vv")
            return newton_rsqrt(r_bc)

        quant_tiles = []
        for t in range(H // 128):
            for c in range(D // QW):
                quant_tiles.append((wfc[128 * t:128 * (t + 1), QW * c:QW * (c + 1)],
                                    qfc[128 * t:128 * (t + 1), QW * c:QW * (c + 1)]))
        n_fc_tiles = len(quant_tiles)
        for hw in range(H // QW):
            for dr in range(D // 128):
                quant_tiles.append((wproj[128 * dr:128 * (dr + 1), QW * hw:QW * (hw + 1)],
                                    qproj[128 * dr:128 * (dr + 1), QW * hw:QW * (hw + 1)]))
        qt_idx = [0]
        q_par = [0]
        q_cons_vv = {0: 0, 1: 0}
        q_wb_vv = [0]

        q_loads = []

        def _emit_quant_load(phase, guard_vv):
            k = len(q_loads)
            if k >= len(quant_tiles):
                return
            src_ = quant_tiles[k][0]
            par = k % 2
            qw = qw0 if par == 0 else (qw1_attn if phase == "attn" else qw1_ffn)
            S.dma("gpsimd", lambda e, src_=src_, qw=qw: e.dma_start(out=qw, in_=src_),
                  waits=[("vv", max(q_cons_vv[par], guard_vv))], inc="ldg")
            q_loads.append((S.n("ldg"), qw))

        def emit_quant_tile(phase="attn", guard_vv=0):
            if qt_idx[0] >= len(quant_tiles):
                return
            k = qt_idx[0]
            qt_idx[0] += 1
            if k == 0:
                _emit_quant_load(phase, guard_vv)
            _emit_quant_load(phase, guard_vv)
            src_, dst = quant_tiles[k]
            par = k % 2
            ld, qw = q_loads[k]
            S.emit("scalar", lambda e, qw=qw: e.activation(qwb.bitcast(BF16), qw, ActF.Copy),
                   waits=[("ldg", ld), ("vv", q_wb_vv[0])], inc="ss")
            sc_ = S.n("ss")
            va = S.emit("vector", lambda e: e.tensor_scalar(qau, qwb, 0x7FFF, None, Alu.bitwise_and),
                   waits=[("ss", sc_)], inc="vv")
            S.emit("scalar", lambda e: e.activation(qwb.bitcast(BF16), qwb.bitcast(BF16), ActF.Sign),
                   waits=[("vv", va)], inc="ss")
            vp = S.emit("vector", lambda e: e.tensor_reduce(
                qgs[:].rearrange("p (g o) -> p g o", o=1),
                qau.bitcast(BF16).rearrange("p (g k) -> p g k", g=QG),
                AX.X, Alu.add), waits=[("vv", va)], inc="vv")
            vp = S.emit("vector", lambda e: e.tensor_scalar(qs16[:].bitcast(BF16), qgs[:], 1.0 / 64, 1e-8, Alu.mult, Alu.max),
                   waits=[("vv", vp)], inc="vv")
            vp = S.emit("vector", lambda e: e.tensor_copy(
                qsf[:].bitcast(U16).rearrange("p (g two) -> p g two", two=2)[:, :, 1],
                qs16[:].rearrange("p (g o) -> p g o", o=1)[:, :, 0]), waits=[("vv", vp)], inc="vv")
            vp = S.emit("vector", lambda e: e.tensor_scalar(qthr[:].bitcast(F32), qsf[:].bitcast(F32), M_BND, None, Alu.mult),
                   waits=[("vv", vp)], inc="vv")
            vp = S.emit("vector", lambda e: e.scalar_tensor_tensor(qthr[:].bitcast(F32), qsf[:].bitcast(F32), P25, qthr[:].bitcast(F32), Alu.mult, Alu.add),
                   waits=[("vv", vp)], inc="vv")
            vp = S.emit("vector", lambda e: e.tensor_tensor(
                qmk.bitcast(BF16).rearrange("p (g k) -> p g k", g=QG),
                qau.bitcast(BF16).rearrange("p (g k) -> p g k", g=QG),
                qthr[:].bitcast(BF16).rearrange("p (g two) -> p g two", two=2)[:, :, 1:2].broadcast_to([128, QG, 64]),
                Alu.is_gt), waits=[("wq", S.n("wq")), ("vv", vp)], inc="vv")
            vp = S.emit("vector", lambda e: e.tensor_tensor(
                qmk.bitcast(BF16).rearrange("p (g k) -> p g k", g=QG),
                qmk.bitcast(BF16).rearrange("p (g k) -> p g k", g=QG),
                qs16[:].bitcast(BF16).rearrange("p (g o) -> p g o", o=1).broadcast_to([128, QG, 64]),
                Alu.mult), waits=[("vv", vp)], inc="vv")
            S.emit("vector", lambda e: e.tensor_tensor(qmk.bitcast(BF16), qmk.bitcast(BF16), qwb.bitcast(BF16), Alu.mult),
                   waits=[("vv", vp)], inc="vv")
            v = S.n("vv")
            q_wb_vv[0] = v
            q_cons_vv[par] = v
            S.dma("gpsimd", lambda e, dst=dst: e.dma_start(out=dst, in_=qmk.bitcast(BF16)),
                  waits=[("vv", v)], inc="wq")

        win_mm_end = {}
        wsplit_vv = [0, 0]
        wuse_mm = [0, 0]
        wq2_mm = [0, 0]
        bank_vv = {}
        bank_ss = {}
        last_mm_xn = 0

        def bank_waits(bank):
            return [("vv", bank_vv.get(id(bank), 0)), ("ss", bank_ss.get(id(bank), 0))]

        for half in range(2):
            tok0 = half * THF
            xw = [("vv", S.n("vv")), ("stq", S.n("stq"))] if half else [("lds", ld0)]
            S.dma("sync", lambda e, tok0=tok0: e.dma_start(
                out=X,
                in_=x_in.rearrange("(c p) t -> p c t", p=128)[:, :, tok0:tok0 + THF]),
                  waits=xw, inc="lds")
            ldx = S.n("lds")

            mmS = ln_stats(extra_first_waits=[("lds", ldx), ("vv", v_init)])
            vR = ln_mu_r(mmS)
            for dc in range(DC):
                S.emit("vector", lambda e, dc=dc: e.tensor_tensor(e3, X[:, dc, :], mu_bc, Alu.subtract),
                       waits=([("vv", vR), ("mm", last_mm_xn)] if dc == 0 else []), inc="vv")
                S.emit("vector", lambda e: e.tensor_tensor(e3, e3, r_bc, Alu.mult), inc="vv")
                S.emit("vector", lambda e, dc=dc: e.tensor_scalar(e4, e3, gam[:, dc:dc + 1], bet[:, dc:dc + 1], Alu.mult, Alu.add), inc="vv")
                v = S.n("vv")
                S.emit("scalar", lambda e, dc=dc: e.activation(xnh[:, dc, :], e4, ActF.Copy),
                       waits=[("vv", v)], inc="ss")
                s = S.n("ss")
                S.emit("vector", lambda e, dc=dc: e.tensor_tensor(xnl[:, dc, :], e4, xnh[:, dc, :], Alu.subtract),
                       waits=[("ss", s)], inc="vv")
            v_xn = S.n("vv")
            if half == 0:
                tap("mu0", mu_bc, [128, THF])
                tap("r0", r_bc, [128, THF])
                tap("xnh0", B[:, 0:8192], [128, 8192], BF16)
                tap("xnl0", B[:, 8192:16384], [128, 8192], BF16)

            def a1_load_split(ec):
                out = []
                for hp in range(2):
                    par = hp
                    wb3 = wbuf.rearrange("p (q c e) -> p q c e", q=2, c=8)[:, par]
                    wh3 = whi.rearrange("p (q c e) -> p q c e", q=2, c=8)[:, par]
                    wl3 = wlo.rearrange("p (q c e) -> p q c e", q=2, c=8)[:, par]
                    dc0 = 8 * hp
                    S.dma("sync", lambda e, ec=ec, dc0=dc0, wb3=wb3: e.dma_start(
                        out=wb3[:, :, 0:128],
                        in_=winT.rearrange("(c p) e -> p c e", p=128)[:, dc0:dc0 + 8, 128 * ec:128 * (ec + 1)]),
                          waits=[("vv", wsplit_vv[par])], inc="lds")
                    S.dma("sync", lambda e, ec=ec, dc0=dc0, wb3=wb3: e.dma_start(
                        out=wb3[:, :, 128:256],
                        in_=winT.rearrange("(c p) e -> p c e", p=128)[:, dc0:dc0 + 8, EI + 128 * ec:EI + 128 * (ec + 1)]),
                          inc="lds")
                    ldw = S.n("lds")
                    S.emit("scalar", lambda e, wh3=wh3, wb3=wb3: e.activation(
                        wh3.rearrange("p c e -> p (c e)"), wb3.rearrange("p c e -> p (c e)"), ActF.Copy),
                           waits=[("lds", ldw), ("mm", wuse_mm[par])], inc="ss")
                    s = S.n("ss")
                    S.emit("vector", lambda e, wl3=wl3, wb3=wb3, wh3=wh3: e.tensor_tensor(
                        wl3.rearrange("p c e -> p (c e)"), wb3.rearrange("p c e -> p (c e)"),
                        wh3.rearrange("p c e -> p (c e)"), Alu.subtract),
                           waits=[("ss", s)], inc="vv")
                    wsplit_vv[par] = S.n("vv")
                    out.append((wh3, wl3, dc0, S.n("vv")))
                return out

            pend = a1_load_split(0)
            for ec in range(ECP):
                zb, gb = pZ[ec % 2], pG[ec % 2]
                bw = bank_waits(zb) + bank_waits(gb) + [("vv", v_xn)]
                cur, pend = pend, None
                for hi, (wh3, wl3, dc0, vw) in enumerate(cur):
                    first = True
                    for tgt, col0 in ((zb, 0), (gb, 128)):
                        for ci in range(8):
                            dc = dc0 + ci
                            for wsp, xsp in ((wh3, xnh), (wl3, xnh), (wh3, xnl)):
                                st = (hi == 0 and ci == 0 and wsp is wh3 and xsp is xnh)
                                sp = (hi == 1 and ci == 7 and wsp is wh3 and xsp is xnl)
                                w = ([("vv", vw)] + (bw if hi == 0 else [])) if first else []
                                S.emit("tensor", lambda e, wsp=wsp, xsp=xsp, ci=ci, dc=dc, tgt=tgt, col0=col0, st=st, sp=sp: e.matmul(
                                    tgt[:], wsp[:, ci, col0:col0 + 128], xsp[:, dc, :], start=st, stop=sp),
                                       waits=w, inc="mm")
                                first = False
                    wuse_mm[hi] = S.n("mm")
                mmA = S.n("mm")
                last_mm_xn = mmA
                if ec + 1 < ECP:
                    pend = a1_load_split(ec + 1)
                for _ in range(2):
                    emit_quant_tile("attn", v_init)
                S.emit("vector", lambda e, zb=zb: e.tensor_scalar(e1, zb[:], 15.0, -15.0, Alu.min, Alu.max),
                       waits=[("mm", mmA)], inc="vv")
                v1 = S.n("vv")
                S.emit("scalar", lambda e: e.activation(e2, e1, ActF.Sigmoid),
                       waits=[("vv", v1)], inc="ss")
                s1 = S.n("ss")
                S.emit("vector", lambda e, gb=gb: e.tensor_scalar(e3, gb[:], 15.0, -15.0, Alu.min, Alu.max), inc="vv")
                v2 = S.n("vv")
                S.emit("scalar", lambda e: e.activation(e4, e3, ActF.Sigmoid),
                       waits=[("vv", v2)], inc="ss")
                s2 = S.n("ss")
                S.emit("vector", lambda e, ec=ec, zb=zb: e.scalar_tensor_tensor(e1, zb[:], dtw[:, ec:ec + 1], e2, Alu.mult, Alu.mult),
                       waits=[("ss", s1)], inc="vv")
                S.emit("vector", lambda e: e.tensor_tensor(e3, e3, e4, Alu.mult),
                       waits=[("ss", s2)], inc="vv")
                S.emit("vector", lambda e: e.tensor_tensor(e1, e1, e3, Alu.mult), inc="vv")
                vT = S.n("vv")
                S.emit("scalar", lambda e, ec=ec: e.activation(Th[:, ec, :], e1, ActF.Copy),
                       waits=[("vv", vT)], inc="ss")
                sT = S.n("ss")
                S.emit("vector", lambda e, ec=ec: e.tensor_tensor(Tl[:, ec, :], e1, Th[:, ec, :], Alu.subtract),
                       waits=[("ss", sT)], inc="vv")
                bank_vv[id(zb)] = bank_vv[id(gb)] = S.n("vv")
            v_T = S.n("vv")
            if half == 0:
                tap("Th0", B[:, 16384:32768], [128, 16384], BF16)
                tap("Tl0", B[:, 32768:49152], [128, 16384], BF16)

            def a2_load_split(dc):
                out = []
                for hp in range(2):
                    par = hp
                    wb3 = wbuf.rearrange("p (q c e) -> p q c e", q=2, c=16)[:, par]
                    wh3 = whi.rearrange("p (q c e) -> p q c e", q=2, c=16)[:, par]
                    wl3 = wlo.rearrange("p (q c e) -> p q c e", q=2, c=16)[:, par]
                    ec0 = 16 * hp
                    S.dma("sync", lambda e, dc=dc, ec0=ec0, wb3=wb3: e.dma_start(
                        out=wb3[:],
                        in_=woutT.rearrange("(c p) d -> p c d", p=128)[:, ec0:ec0 + 16, 128 * dc:128 * (dc + 1)]),
                          waits=[("vv", wsplit_vv[par])], inc="lds")
                    ldw = S.n("lds")
                    S.emit("scalar", lambda e, wh3=wh3, wb3=wb3: e.activation(
                        wh3.rearrange("p c e -> p (c e)"), wb3.rearrange("p c e -> p (c e)"), ActF.Copy),
                           waits=[("lds", ldw), ("mm", wuse_mm[par])], inc="ss")
                    s = S.n("ss")
                    S.emit("vector", lambda e, wl3=wl3, wb3=wb3, wh3=wh3: e.tensor_tensor(
                        wl3.rearrange("p c e -> p (c e)"), wb3.rearrange("p c e -> p (c e)"),
                        wh3.rearrange("p c e -> p (c e)"), Alu.subtract),
                           waits=[("ss", s)], inc="vv")
                    wsplit_vv[par] = S.n("vv")
                    out.append((wh3, wl3, ec0, S.n("vv")))
                return out

            pend = a2_load_split(0)
            for dc in range(DC):
                bank = pY[dc % 2]
                bw = bank_waits(bank) + [("vv", v_T)]
                cur, pend = pend, None
                for hi, (wh3, wl3, ec0, vw) in enumerate(cur):
                    first = True
                    for ci in range(16):
                        ec = ec0 + ci
                        for wsp, tsp in ((wh3, Th), (wl3, Th), (wh3, Tl)):
                            st = (hi == 0 and ci == 0 and wsp is wh3 and tsp is Th)
                            sp = (hi == 1 and ci == 15 and wsp is wh3 and tsp is Tl)
                            w = ([("vv", vw)] + (bw if hi == 0 else [])) if first else []
                            S.emit("tensor", lambda e, wsp=wsp, tsp=tsp, ci=ci, ec=ec, bank=bank, st=st, sp=sp: e.matmul(
                                bank[:], wsp[:, ci, 0:128], tsp[:, ec, :], start=st, stop=sp),
                                   waits=w, inc="mm")
                            first = False
                    wuse_mm[hi] = S.n("mm")
                mmY = S.n("mm")
                if dc + 1 < DC:
                    pend = a2_load_split(dc + 1)
                for _ in range(4):
                    emit_quant_tile("attn", v_init)
                S.emit("vector", lambda e, dc=dc, bank=bank: e.scalar_tensor_tensor(
                    X[:, dc, :], X[:, dc, :], 2.0, bank[:], Alu.mult, Alu.add),
                       waits=[("mm", mmY)], inc="vv")
                bank_vv[id(bank)] = S.n("vv")
            a2_mm_end = S.n("mm")
            if half == 0:
                tap("x1", F[:, 0:8192], [128, 8192])
            mmS = ln_stats()
            vR = ln_mu_r(mmS)
            for dc in range(DC):
                S.emit("vector", lambda e, dc=dc: e.tensor_tensor(e3, X[:, dc, :], mu_bc, Alu.subtract),
                       waits=([("vv", vR)] if dc == 0 else []), inc="vv")
                S.emit("vector", lambda e, dc=dc: e.tensor_tensor(X[:, dc, :], e3, r_bc, Alu.mult), inc="vv")
                v = S.n("vv")
                S.emit("scalar", lambda e, dc=dc: e.activation(x2h[:, dc, :], X[:, dc, :], ActF.Copy),
                       waits=[("vv", v)], inc="ss")
                s = S.n("ss")
                S.emit("vector", lambda e, dc=dc: e.tensor_tensor(x2l[:, dc, :], X[:, dc, :], x2h[:, dc, :], Alu.subtract),
                       waits=[("ss", s)], inc="vv")
            v_x2 = S.n("vv")
            tap("x2" if half == 0 else "x2_h1", F[:, 0:8192], [128, 8192])
            if half == 0:
                tap("x2h0", B[:, 0:8192], [128, 8192], BF16)

            ms_started = False
            oacc_first = [True] * DC
            mm_gend = 0
            for grp in range(NGRP):
                for hci in range(HGRP):
                    hc = grp * HGRP + hci
                    hw = hc // 4
                    if hc % 4 == 0:
                        buf = wq1[hw % 2]
                        need = 16 * min((hw + 1) * 8, n_fc_tiles)
                        assert need <= S.n("wq"), f"F1 wq deadlock: need {need} planned {S.n('wq')}"
                        prev_mm = max(win_mm_end.get((half, hw - 2), 0), a2_mm_end)
                        for dc in range(DC):
                            S.dma("sync", lambda e, hw=hw, dc=dc, buf=buf: e.dma_start(
                                out=buf[:, dc, :],
                                in_=qfc[512 * hw:512 * (hw + 1), 128 * dc:128 * (dc + 1)],
                                transpose=True),
                                  waits=([("wq", need), ("mm", prev_mm)] if dc == 0 else []), inc="lds")
                        ld_w1 = (S.n("lds"), 0)
                    buf = wq1[hw % 2]
                    bank = pH[hc % 4]
                    first = True
                    idx = 0
                    for dc in range(DC):
                        for xs in (x2h, x2l):
                            st, sp = (idx == 0), (idx == 2 * DC - 1)
                            w = ([("lds", ld_w1[0]), ("lds2", ld_w1[1]), ("vv", v_x2)] + bank_waits(bank)) if first else []
                            S.emit("tensor", lambda e, buf=buf, dc=dc, hc=hc, xs=xs, st=st, sp=sp, bank=bank: e.matmul(
                                bank[:], buf[:, dc, 128 * (hc % 4):128 * (hc % 4) + 128], xs[:, dc, :], start=st, stop=sp),
                                   waits=w, inc="mm")
                            first = False
                            idx += 1
                    mmH = S.n("mm")
                    if hc % 4 == 3:
                        win_mm_end[(half, hw)] = mmH
                    par = hci % 2
                    S.emit("scalar", lambda e, bank=bank, par=par: e.activation(gsc[par], bank[:], ActF.Relu),
                           waits=[("mm", mmH)], inc="ss")
                    s = S.n("ss")
                    bank_ss[id(bank)] = s
                    S.emit("vector", lambda e, par=par: e.tensor_tensor(gsc[par], gsc[par], gsc[par], Alu.mult),
                           waits=[("ss", s)], inc="vv")
                    S.emit("vector", lambda e, par=par, hci=hci: e.tensor_copy(gh[:, hci, :], gsc[par]), inc="vv")
                    S.emit("vector", lambda e, par=par, hci=hci: e.tensor_tensor(gl[:, hci, :], gsc[par], gh[:, hci, :], Alu.subtract), inc="vv")
                    S.emit("vector", lambda e, par=par: e.tensor_tensor(g2[par], gsc[par], gsc[par], Alu.mult), inc="vv")
                    v = S.n("vv")
                    S.emit("tensor", lambda e, par=par, st=(not ms_started), hc=hc: e.matmul(
                        pS2[:], ones_r[:], g2[par], start=st, stop=(hc == HC - 1)),
                           waits=[("vv", v)], inc="mm")
                    ms_started = True
                    emit_quant_tile("ffn", v_x2)
                    emit_quant_tile("ffn", v_x2)
                    if half == 0 and hc == 0:
                        tap("g0", gsc[0], [128, THF])
                    if half == 0 and hc == 1:
                        tap("g1", gsc[1], [128, THF])
                v_g = S.n("vv")
                mm_gend = S.n("mm")
                for dcq in range(4):
                    buf2 = wq2[dcq % 2]
                    need = 16 * (n_fc_tiles + min(16 * (grp + 1), len(quant_tiles) - n_fc_tiles))
                    assert need <= S.n("wq"), f"F2 wq deadlock: need {need} planned {S.n('wq')}"
                    prev_mm = max(wq2_mm[dcq % 2], a2_mm_end)
                    for hci in range(HGRP):
                        hc = grp * HGRP + hci
                        S.dma("sync", lambda e, hc=hc, dcq=dcq, hci=hci, buf2=buf2: e.dma_start(
                            out=buf2[:, hci, :],
                            in_=qproj[512 * dcq:512 * (dcq + 1), 128 * hc:128 * (hc + 1)],
                            transpose=True),
                              waits=([("wq", need), ("mm", prev_mm)] if hci == 0 else []), inc="lds")
                    ld_w2 = (S.n("lds"), 0)
                    for dci in range(4):
                        dc = dcq * 4 + dci
                        bank = pY[dci % 2]
                        first = True
                        idx = 0
                        for hci in range(HGRP):
                            for gs in (gh, gl):
                                st, sp = (idx == 0), (idx == 2 * HGRP - 1)
                                w = ([("lds", ld_w2[0]), ("lds2", ld_w2[1]), ("vv", v_g)] + bank_waits(bank)) if first else []
                                S.emit("tensor", lambda e, buf2=buf2, hci=hci, dci=dci, gs=gs, st=st, sp=sp, bank=bank: e.matmul(
                                    bank[:], buf2[:, hci, 128 * dci:128 * dci + 128], gs[:, hci, :], start=st, stop=sp),
                                       waits=w, inc="mm")
                                first = False
                                idx += 1
                        mmO = S.n("mm")
                        if oacc_first[dc]:
                            S.emit("vector", lambda e, dc=dc, bank=bank: e.tensor_copy(Oacc[:, dc, :], bank[:]),
                                   waits=[("mm", mmO)], inc="vv")
                            oacc_first[dc] = False
                        else:
                            S.emit("vector", lambda e, dc=dc, bank=bank: e.tensor_tensor(Oacc[:, dc, :], Oacc[:, dc, :], bank[:], Alu.add),
                                   waits=[("mm", mmO)], inc="vv")
                        bank_vv[id(bank)] = S.n("vv")
                    wq2_mm[dcq % 2] = S.n("mm")
                    emit_quant_tile("ffn", v_x2)

            S.emit("vector", lambda e: e.tensor_scalar(e3, pS2[:], 1.0 / H, EPS32, Alu.mult, Alu.add),
                   waits=[("mm", mm_gend), ("vv", S.n("vv")), ("ldg", S.n("ldg"))], inc="vv")
            vS = newton_rsqrt(r_bc)
            for dc in range(DC):
                S.emit("vector", lambda e, dc=dc: e.tensor_tensor(e3, Oacc[:, dc, :], r_bc, Alu.mult),
                       waits=([("vv", vS)] if dc == 0 else []), inc="vv")
                S.emit("vector", lambda e, dc=dc: e.tensor_tensor(X[:, dc, :], X[:, dc, :], e3, Alu.add), inc="vv")
            tap("x3" if half == 0 else "x3_h1", F[:, 0:8192], [128, 8192])
            tap("Oacc0" if half == 0 else "Oacc_h1", share, [128, 8192])
            if half == 0:
                tap("sF", r_bc, [128, THF])
            mmS = ln_stats()
            vR = ln_mu_r(mmS)
            st_tgts = []
            for dc in range(DC):
                ob = ob0 if (dc % 2 == 0) else ob1
                w = [("vv", vR)] if dc == 0 else []
                if dc >= 2:
                    w.append(("stq", st_tgts[dc - 2]))
                S.emit("vector", lambda e, dc=dc: e.tensor_tensor(e3, X[:, dc, :], mu_bc, Alu.subtract),
                       waits=w, inc="vv")
                S.emit("vector", lambda e, dc=dc, ob=ob: e.tensor_tensor(ob, e3, r_bc, Alu.mult), inc="vv")
                v = S.n("vv")
                S.dma("sync", lambda e, dc=dc, tok0=tok0, ob=ob: e.dma_start(
                    out=out.rearrange("(c p) t -> p c t", p=128)[:, dc, tok0:tok0 + THF],
                    in_=ob),
                      waits=[("vv", v)], inc="stq")
                st_tgts.append(S.n("stq"))

        if "qfc" in taps:
            t_qfc = nc.dram_tensor("tap_qfc", [H, D], BF16, kind="ExternalOutput").ap()
            S.dma("sync", lambda e: e.dma_start(out=t_qfc, in_=qfc),
                  waits=[("wq", S.n("wq"))], inc="stq")
        if "qproj" in taps:
            t_qp = nc.dram_tensor("tap_qproj", [D, H], BF16, kind="ExternalOutput").ap()
            S.dma("sync", lambda e: e.dma_start(out=t_qp, in_=qproj),
                  waits=[("wq", S.n("wq"))], inc="stq")

        S.emit("sync", lambda e: None,
               waits=[("stq", S.n("stq")), ("wq", S.n("wq")),
                      ("lds", S.n("lds")), ("ldg", S.n("ldg"))])

        def run_queue(eng_obj, name):
            for waits, fn, inc, amt in S.q[name]:
                for sname, val in waits:
                    eng_obj.wait_ge(sems[sname], val)
                ins = fn(eng_obj)
                if inc is not None:
                    if ins is None:
                        ins = eng_obj.nop()
                    ins.then_inc(sems[inc], amt)

        @blk.sync
        def _(sync):
            run_queue(sync, "sync")

        @blk.gpsimd
        def _(gpsimd):
            run_queue(gpsimd, "gpsimd")

        @blk.tensor
        def _(tensor):
            run_queue(tensor, "tensor")

        @blk.vector
        def _(vector):
            run_queue(vector, "vector")

        @blk.scalar
        def _(scalar):
            run_queue(scalar, "scalar")

    return nc


_NC_CACHE = {}


def kernel(L0_attn_x, L0_attn_gamma, L0_attn_beta, L0_attn_W_in, L0_attn_W_out,
           L0_attn_dt, L0_ffn_W_fc, L0_ffn_W_proj):
    x = np.ascontiguousarray(np.asarray(L0_attn_x, dtype=np.float32)).reshape(8192, 2048)
    winT = np.ascontiguousarray(np.asarray(L0_attn_W_in, dtype=np.float32).T)
    woutT = np.ascontiguousarray(np.asarray(L0_attn_W_out, dtype=np.float32).T)
    wfc = np.ascontiguousarray(np.asarray(L0_ffn_W_fc, dtype=np.float32))
    wproj = np.ascontiguousarray(np.asarray(L0_ffn_W_proj, dtype=np.float32))
    gamma = np.ascontiguousarray(np.asarray(L0_attn_gamma, dtype=np.float32).reshape(16, 128).T)
    beta = np.ascontiguousarray(np.asarray(L0_attn_beta, dtype=np.float32).reshape(16, 128).T)
    dt = np.asarray(L0_attn_dt, dtype=np.float32)
    dt_fl = np.ascontiguousarray(dt.reshape(1, 4096))
    dt_pc = np.ascontiguousarray(dt.reshape(32, 128).T)
    ones = np.ones((128, 128), dtype=np.float32)

    if "nc" not in _NC_CACHE:
        _NC_CACHE["nc"] = build_core()
    nc = _NC_CACHE["nc"]

    in_maps = []
    for i in range(NCORES):
        xs = np.ascontiguousarray(x[1024 * i:1024 * (i + 1), :].T)
        in_maps.append({"x": xs, "winT": winT, "woutT": woutT, "wfc": wfc,
                        "wproj": wproj, "gamma": gamma, "beta": beta,
                        "dt_fl": dt_fl, "dt_pc": dt_pc, "ones_in": ones})
    res = run_bass_kernel_spmd(nc, in_maps, core_ids=list(range(NCORES)))
    o = np.empty((8192, 2048), dtype=np.float32)
    for i in range(NCORES):
        o[1024 * i:1024 * (i + 1), :] = res.results[i]["out"].T
    return o.reshape(2, 4096, 2048)


# version 12
# speedup vs baseline: 1.9685x; 1.0231x over previous
import contextlib
import numpy as np
import concourse.bass as bass
import concourse.mybir as mybir
from concourse.bass_utils import run_bass_kernel_spmd

F32 = mybir.dt.float32
F32R = mybir.dt.float32r
BF16 = mybir.dt.bfloat16
U16 = mybir.dt.uint16
U32 = mybir.dt.uint32
I32 = mybir.dt.int32
I16 = mybir.dt.int16
Alu = mybir.AluOpType
ActF = mybir.ActivationFunctionType
AX = mybir.AxisListType

NCORES = 8
D, DC = 2048, 16
EI, ECP = 4096, 32
H, HC = 8192, 64
NT, THF = 1024, 512
HGRP = 8
NGRP = HC // HGRP
M_BND = 0.501953125
P25 = 2.0 ** -25
LN_EPS = 1e-5
EPS32 = float(np.finfo(np.float32).eps)
MAGIC = 0x5F3759DF
QW = 1024
QG = QW // 64

ENGINES = ("sync", "gpsimd", "tensor", "vector", "scalar")


class Sched:

    def __init__(self):
        self.q = {e: [] for e in ENGINES}
        self.cnt = {}

    def n(self, sem):
        return self.cnt.get(sem, 0)

    def emit(self, eng, fn, waits=(), inc=None, amt=1):
        waits = tuple((s, v) for (s, v) in waits if v > 0)
        self.q[eng].append((waits, fn, inc, amt))
        if inc is not None:
            self.cnt[inc] = self.cnt.get(inc, 0) + amt
        return self.cnt.get(inc, 0)

    def dma(self, eng, fn, waits=(), inc=None):
        return self.emit(eng, fn, waits, inc, 16)


def build_core(taps=None, barriers=None):
    taps = taps or []
    barriers = barriers or []
    nc = bass.Bass("TRN2", target_bir_lowering=False, debug=False,
                   detect_race_conditions=False)

    x_in = nc.dram_tensor("x", [D, NT], F32, kind="ExternalInput").ap()
    winT = nc.dram_tensor("winT", [D, 2 * EI], F32, kind="ExternalInput").ap()
    woutT = nc.dram_tensor("woutT", [EI, D], F32, kind="ExternalInput").ap()
    wfc = nc.dram_tensor("wfc", [H, D], F32, kind="ExternalInput").ap()
    wproj = nc.dram_tensor("wproj", [D, H], F32, kind="ExternalInput").ap()
    gamma = nc.dram_tensor("gamma", [128, DC], F32, kind="ExternalInput").ap()
    beta = nc.dram_tensor("beta", [128, DC], F32, kind="ExternalInput").ap()
    dt_fl = nc.dram_tensor("dt_fl", [1, EI], F32, kind="ExternalInput").ap()
    dt_pc = nc.dram_tensor("dt_pc", [128, ECP], F32, kind="ExternalInput").ap()
    ones_in = nc.dram_tensor("ones_in", [128, 128], F32, kind="ExternalInput").ap()
    out = nc.dram_tensor("out", [D, NT], F32, kind="ExternalOutput").ap()
    qfc = nc.dram_tensor("qfc", [H, D], BF16, kind="Internal").ap()
    qproj = nc.dram_tensor("qproj", [D, H], BF16, kind="Internal").ap()

    S = Sched()
    es = contextlib.ExitStack()
    with es:
        sb = lambda n, sh, dt: es.enter_context(nc.sbuf_tensor(n, sh, dt))
        psu = lambda n, sh: es.enter_context(nc.psum_tensor(n, sh, F32))

        B = sb("B", [128, 57344], BF16)
        xnh = B[:, 0:8192].rearrange("p (c t) -> p c t", c=DC)
        xnl = B[:, 8192:16384].rearrange("p (c t) -> p c t", c=DC)
        Th = B[:, 16384:32768].rearrange("p (c t) -> p c t", c=ECP)
        Tl = B[:, 32768:49152].rearrange("p (c t) -> p c t", c=ECP)
        whi = B[:, 49152:53248]
        wlo = B[:, 53248:57344]
        x2h = B[:, 0:8192].rearrange("p (c t) -> p c t", c=DC)
        x2l = B[:, 8192:16384].rearrange("p (c t) -> p c t", c=DC)
        gh = B[:, 16384:20480].rearrange("p (c t) -> p c t", c=HGRP)
        gl = B[:, 20480:24576].rearrange("p (c t) -> p c t", c=HGRP)
        wq1 = [B[:, 24576:32768].rearrange("p (c t) -> p c t", c=DC),
               B[:, 32768:40960].rearrange("p (c t) -> p c t", c=DC)]
        wq2 = [B[:, 40960:45056].rearrange("p (c t) -> p c t", c=HGRP),
               B[:, 45056:49152].rearrange("p (c t) -> p c t", c=HGRP)]

        F = sb("F", [128, 20480], F32)
        X = F[:, 0:8192].rearrange("p (c t) -> p c t", c=DC)
        e1 = F[:, 8192:8704]
        e2 = F[:, 8704:9216]
        e3 = F[:, 9216:9728]
        e4 = F[:, 9728:10240]
        mu_bc = F[:, 10240:10752]
        r_bc = F[:, 10752:11264]
        share = F[:, 11264:19456]
        wbuf = share[:, 0:4096]
        Oacc = share.rearrange("p (c t) -> p c t", c=DC)
        dtf = share[0:1, 4096:8192]
        qw0 = F[:, 19456:20480]
        qw1_attn = F[:, 15360:16384]
        qw1_ffn = F[:, 9728:10752]
        ob0 = F[:, 19456:19968]
        ob1 = F[:, 19968:20480]
        gsc = [e1, e2]

        R = sb("R", [128, 1024], F32R)
        xr = R[:, 0:512]
        xsq = R[:, 512:1024]
        g2 = [xr, xsq]

        IA = sb("IA", [128, 3072], U16)
        qwb = IA[:, 0:1024]
        qau = IA[:, 1024:2048]
        qmk = IA[:, 2048:3072]
        useed = IA[:, 2048:3072].bitcast(U32)

        qgs = sb("qgs", [128, QG], F32)
        qs16 = sb("qs16", [128, QG], U16)
        qsf = sb("qsf", [128, QG], U32)
        qthr = sb("qthr", [128, QG], U32)

        ones_f = sb("ones_f", [128, 128], F32)
        ones_r = sb("ones_r", [128, 128], F32R)
        ones1 = sb("ones1", [1, 128], F32)
        gam = sb("gam", [128, DC], F32)
        bet = sb("bet", [128, DC], F32)
        dtw = sb("dtw", [128, ECP], F32)
        dtp = sb("dtp", [128, ECP], F32)
        sc1 = sb("sc1", [1, 8], F32)
        mb1 = sb("mb1", [128, 1], F32)
        rb1 = sb("rb1", [128, 1], F32)

        pZ = [psu("pZ0", [128, THF]), psu("pZ1", [128, THF])]
        pG = [psu("pG0", [128, THF]), psu("pG1", [128, THF])]
        pY = [psu("pY0", [128, THF]), psu("pY1", [128, THF])]
        pH = [pZ[0], pZ[1], pG[0], pG[1]]
        pS = psu("pS", [128, THF])
        pS2 = psu("pS2", [128, THF])
        pB = pS[:, 0:8]

        sems = {}
        for s in ("lds", "ldg", "lds2", "mm", "vv", "ss", "stq", "wq"):
            sems[s] = es.enter_context(nc.semaphore(s))

        blk = es.enter_context(nc.Block())

        def tap(name, src_ap, shape, dtype=F32):
            if name not in taps:
                return
            t = nc.dram_tensor("tap_" + name, shape, dtype, kind="ExternalOutput").ap()
            S.dma("sync", lambda e: e.dma_start(out=t, in_=src_ap),
                  waits=[("vv", S.n("vv")), ("ss", S.n("ss")), ("mm", S.n("mm"))], inc="stq")
            v = S.n("stq")
            S.emit("vector", lambda e: e.nop(), waits=[("stq", v)], inc="vv")
            S.emit("scalar", lambda e: e.nop(), waits=[("stq", v)], inc="ss")

        S.dma("sync", lambda e: e.dma_start(out=gam[:], in_=gamma), inc="lds")
        S.dma("sync", lambda e: e.dma_start(out=bet[:], in_=beta), inc="lds")
        S.dma("sync", lambda e: e.dma_start(out=dtf, in_=dt_fl), inc="lds")
        S.dma("sync", lambda e: e.dma_start(out=dtp[:], in_=dt_pc), inc="lds")
        S.dma("sync", lambda e: e.dma_start(out=ones_f[:], in_=ones_in), inc="lds")
        ld0 = S.n("lds")

        S.emit("vector", lambda e: e.tensor_copy(ones_r[:], ones_f[:]),
               waits=[("lds", ld0)], inc="vv")
        S.emit("vector", lambda e: e.tensor_copy(ones1[:], ones_f[0:1, :]), inc="vv")
        S.emit("vector", lambda e: e.memset(qsf[:], 0), inc="vv")
        S.emit("vector", lambda e: e.tensor_scalar(dtf, dtf, -1.0, None, Alu.mult), inc="vv")
        vrm = S.emit("vector", lambda e: e.reduce_max(sc1[0:1, 0:1], dtf, AX.X), inc="vv")
        S.emit("vector", lambda e: e.tensor_scalar(sc1[0:1, 1:2], sc1[0:1, 0:1], -1.0, None, Alu.mult),
               waits=[("vv", vrm)], inc="vv")
        v = S.n("vv")
        S.emit("scalar", lambda e: e.activation(dtf, dtf, ActF.Exp, bias=sc1[0:1, 1:2], scale=1.0),
               waits=[("vv", v)], inc="ss")
        s = S.n("ss")
        vrs = S.emit("vector", lambda e: e.reduce_sum(sc1[0:1, 2:3], dtf, AX.X),
               waits=[("ss", s)], inc="vv")
        vrc = S.emit("vector", lambda e: e.reciprocal(sc1[0:1, 3:4], sc1[0:1, 2:3]),
               waits=[("vv", vrs)], inc="vv")
        S.emit("vector", lambda e: e.nop(), waits=[("vv", vrc)], inc="vv")
        v = S.n("vv")
        S.emit("tensor", lambda e: e.matmul(pS[:, 0:1], ones1[:], sc1[0:1, 1:2], start=True, stop=True),
               waits=[("vv", v)], inc="mm")
        S.emit("tensor", lambda e: e.matmul(pS[:, 1:2], ones1[:], sc1[0:1, 3:4], start=True, stop=True), inc="mm")
        mm = S.n("mm")
        S.emit("vector", lambda e: e.tensor_copy(mb1[:], pS[:, 0:1]), waits=[("mm", mm)], inc="vv")
        S.emit("vector", lambda e: e.tensor_copy(rb1[:], pS[:, 1:2]), inc="vv")
        S.emit("vector", lambda e: e.tensor_scalar(dtp[:], dtp[:], -1.0, None, Alu.mult), inc="vv")
        v = S.n("vv")
        S.emit("scalar", lambda e: e.activation(dtw[:], dtp[:], ActF.Exp, bias=mb1[:], scale=1.0),
               waits=[("vv", v)], inc="ss")
        s = S.n("ss")
        S.emit("vector", lambda e: e.tensor_scalar(dtw[:], dtw[:], rb1[:], None, Alu.mult),
               waits=[("ss", s)], inc="vv")
        v_init = S.n("vv")
        tap("dtw", dtw[:], [128, ECP])
        tap("sc1", sc1[:], [1, 8])
        tap("mb1", mb1[:], [128, 1])
        tap("rb1", rb1[:], [128, 1])
        tap("dtf", dtf, [1, EI])
        tap("dtp2", dtp[:], [128, ECP])

        def barrier(name):
            if name not in barriers:
                return
            w = [(s, S.n(s)) for s in ("lds", "ldg", "mm", "vv", "ss", "stq", "wq")]
            for eng in ("vector", "scalar", "tensor", "sync", "gpsimd"):
                S.emit(eng, lambda e: e.nop(), waits=w)

        def newton_rsqrt(dst):
            qau16 = qau[:, 0:512]
            S.emit("vector", lambda e: e.memset(useed, 0),
                   waits=[("wq", S.n("wq"))], inc="vv")
            S.emit("vector", lambda e: e.tensor_scalar(
                qau16, e3.bitcast(U16).rearrange("p (a two) -> p a two", two=2)[:, :, 1],
                1, None, Alu.logical_shift_right), inc="vv")
            S.emit("vector", lambda e: e.tensor_scalar(
                qau16.bitcast(I16), qau16.bitcast(I16), -1, 0x5F37, Alu.mult, Alu.add), inc="vv")
            S.emit("vector", lambda e: e.tensor_copy(
                useed.bitcast(U16).rearrange("p (a two) -> p a two", two=2)[:, :, 1],
                qau16), inc="vv")
            S.emit("vector", lambda e: e.tensor_tensor(e1, useed.bitcast(F32), useed.bitcast(F32), Alu.mult), inc="vv")
            S.emit("vector", lambda e: e.tensor_tensor(e1, e1, e3, Alu.mult), inc="vv")
            S.emit("vector", lambda e: e.tensor_scalar(e1, e1, -0.5, 1.5, Alu.mult, Alu.add), inc="vv")
            S.emit("vector", lambda e: e.tensor_tensor(e2, useed.bitcast(F32), e1, Alu.mult), inc="vv")
            S.emit("vector", lambda e: e.tensor_tensor(e1, e2, e2, Alu.mult), inc="vv")
            S.emit("vector", lambda e: e.tensor_tensor(e1, e1, e3, Alu.mult), inc="vv")
            S.emit("vector", lambda e: e.tensor_scalar(e1, e1, -0.5, 1.5, Alu.mult, Alu.add), inc="vv")
            S.emit("vector", lambda e: e.tensor_tensor(e2, e2, e1, Alu.mult), inc="vv")
            S.emit("vector", lambda e: e.tensor_tensor(e1, e2, e2, Alu.mult), inc="vv")
            S.emit("vector", lambda e: e.tensor_tensor(e1, e1, e3, Alu.mult), inc="vv")
            S.emit("vector", lambda e: e.tensor_scalar(e1, e1, -0.5, 1.5, Alu.mult, Alu.add), inc="vv")
            S.emit("vector", lambda e: e.tensor_tensor(dst, e2, e1, Alu.mult), inc="vv")
            return S.n("vv")

        sq_s = [xr, xsq]

        def ln_stats(extra_first_waits=()):
            mm_slot = [0, 0]
            vX = S.n("vv")
            for dc in range(DC):
                sl = dc % 2
                w = (list(extra_first_waits) + [("vv", vX)]) if dc == 0 else []
                S.emit("vector", lambda e, dc=dc, sl=sl: e.tensor_tensor(sq_s[sl], X[:, dc, :], X[:, dc, :], Alu.mult),
                       waits=w + [("mm", mm_slot[sl])], inc="vv")
                v = S.n("vv")
                S.emit("tensor", lambda e, dc=dc: e.matmul(pS[:], ones_f[:], X[:, dc, :], start=(dc == 0), stop=(dc == DC - 1)),
                       waits=((list(extra_first_waits) + [("vv", vX)]) if dc == 0 else []), inc="mm")
                S.emit("tensor", lambda e, dc=dc, sl=sl: e.matmul(pS2[:], ones_r[:], sq_s[sl], start=(dc == 0), stop=(dc == DC - 1)),
                       waits=[("vv", v)], inc="mm")
                mm_slot[sl] = S.n("mm")
            return S.n("mm")

        def ln_mu_r(mm_done):
            S.emit("vector", lambda e: e.tensor_scalar(mu_bc, pS[:], 1.0 / D, None, Alu.mult),
                   waits=[("mm", mm_done)], inc="vv")
            S.emit("vector", lambda e: e.tensor_tensor(e3, mu_bc, mu_bc, Alu.mult), inc="vv")
            S.emit("vector", lambda e: e.scalar_tensor_tensor(e3, pS2[:], 1.0 / D, e3, Alu.mult, Alu.subtract), inc="vv")
            S.emit("vector", lambda e: e.tensor_scalar(e3, e3, 1.0, LN_EPS, Alu.mult, Alu.add), inc="vv")
            return newton_rsqrt(r_bc)

        quant_tiles = []
        for t in range(H // 128):
            for c in range(D // QW):
                quant_tiles.append((wfc[128 * t:128 * (t + 1), QW * c:QW * (c + 1)],
                                    qfc[128 * t:128 * (t + 1), QW * c:QW * (c + 1)]))
        n_fc_tiles = len(quant_tiles)
        for hw in range(H // QW):
            for dr in range(D // 128):
                quant_tiles.append((wproj[128 * dr:128 * (dr + 1), QW * hw:QW * (hw + 1)],
                                    qproj[128 * dr:128 * (dr + 1), QW * hw:QW * (hw + 1)]))
        qt_idx = [0]
        q_par = [0]
        q_cons_vv = {0: 0, 1: 0}
        q_wb_vv = [0]

        q_loads = []

        def _emit_quant_load(phase, guard_vv):
            k = len(q_loads)
            if k >= len(quant_tiles):
                return
            src_ = quant_tiles[k][0]
            par = k % 2
            qw = qw0 if par == 0 else (qw1_attn if phase == "attn" else qw1_ffn)
            S.dma("gpsimd", lambda e, src_=src_, qw=qw: e.dma_start(out=qw, in_=src_),
                  waits=[("vv", max(q_cons_vv[par], guard_vv))], inc="ldg")
            q_loads.append((S.n("ldg"), qw))

        def emit_quant_tile(phase="attn", guard_vv=0):
            if qt_idx[0] >= len(quant_tiles):
                return
            k = qt_idx[0]
            qt_idx[0] += 1
            if k == 0:
                _emit_quant_load(phase, guard_vv)
            _emit_quant_load(phase, guard_vv)
            src_, dst = quant_tiles[k]
            par = k % 2
            ld, qw = q_loads[k]
            S.emit("scalar", lambda e, qw=qw: e.activation(qwb.bitcast(BF16), qw, ActF.Copy),
                   waits=[("ldg", ld), ("vv", q_wb_vv[0])], inc="ss")
            sc_ = S.n("ss")
            va = S.emit("vector", lambda e: e.tensor_scalar(qau, qwb, 0x7FFF, None, Alu.bitwise_and),
                   waits=[("ss", sc_)], inc="vv")
            S.emit("scalar", lambda e: e.activation(qwb.bitcast(BF16), qwb.bitcast(BF16), ActF.Sign),
                   waits=[("vv", va)], inc="ss")
            vp = S.emit("vector", lambda e: e.tensor_reduce(
                qgs[:].rearrange("p (g o) -> p g o", o=1),
                qau.bitcast(BF16).rearrange("p (g k) -> p g k", g=QG),
                AX.X, Alu.add), waits=[("vv", va)], inc="vv")
            vp = S.emit("vector", lambda e: e.tensor_scalar(qs16[:].bitcast(BF16), qgs[:], 1.0 / 64, 1e-8, Alu.mult, Alu.max),
                   waits=[("vv", vp)], inc="vv")
            vp = S.emit("vector", lambda e: e.tensor_copy(
                qsf[:].bitcast(U16).rearrange("p (g two) -> p g two", two=2)[:, :, 1],
                qs16[:].rearrange("p (g o) -> p g o", o=1)[:, :, 0]), waits=[("vv", vp)], inc="vv")
            vp = S.emit("vector", lambda e: e.tensor_scalar(qthr[:].bitcast(F32), qsf[:].bitcast(F32), M_BND, None, Alu.mult),
                   waits=[("vv", vp)], inc="vv")
            vp = S.emit("vector", lambda e: e.scalar_tensor_tensor(qthr[:].bitcast(F32), qsf[:].bitcast(F32), P25, qthr[:].bitcast(F32), Alu.mult, Alu.add),
                   waits=[("vv", vp)], inc="vv")
            vp = S.emit("vector", lambda e: e.tensor_tensor(
                qmk.bitcast(BF16).rearrange("p (g k) -> p g k", g=QG),
                qau.bitcast(BF16).rearrange("p (g k) -> p g k", g=QG),
                qthr[:].bitcast(BF16).rearrange("p (g two) -> p g two", two=2)[:, :, 1:2].broadcast_to([128, QG, 64]),
                Alu.is_gt), waits=[("wq", S.n("wq")), ("vv", vp)], inc="vv")
            vp = S.emit("vector", lambda e: e.tensor_tensor(
                qmk.bitcast(BF16).rearrange("p (g k) -> p g k", g=QG),
                qmk.bitcast(BF16).rearrange("p (g k) -> p g k", g=QG),
                qs16[:].bitcast(BF16).rearrange("p (g o) -> p g o", o=1).broadcast_to([128, QG, 64]),
                Alu.mult), waits=[("vv", vp)], inc="vv")
            S.emit("vector", lambda e: e.tensor_tensor(qmk.bitcast(BF16), qmk.bitcast(BF16), qwb.bitcast(BF16), Alu.mult),
                   waits=[("vv", vp)], inc="vv")
            v = S.n("vv")
            q_wb_vv[0] = v
            q_cons_vv[par] = v
            S.dma("gpsimd", lambda e, dst=dst: e.dma_start(out=dst, in_=qmk.bitcast(BF16)),
                  waits=[("vv", v)], inc="wq")

        win_mm_end = {}
        wsplit_vv = [0, 0]
        wuse_mm = [0, 0]
        wq2_mm = [0, 0]
        bank_vv = {}
        bank_ss = {}
        last_mm_xn = 0

        def bank_waits(bank):
            return [("vv", bank_vv.get(id(bank), 0)), ("ss", bank_ss.get(id(bank), 0))]

        for half in range(2):
            tok0 = half * THF
            xw = [("vv", S.n("vv")), ("stq", S.n("stq"))] if half else [("lds", ld0)]
            S.dma("sync", lambda e, tok0=tok0: e.dma_start(
                out=X,
                in_=x_in.rearrange("(c p) t -> p c t", p=128)[:, :, tok0:tok0 + THF]),
                  waits=xw, inc="lds")
            ldx = S.n("lds")

            mmS = ln_stats(extra_first_waits=[("lds", ldx), ("vv", v_init)])
            vR = ln_mu_r(mmS)
            for dc in range(DC):
                S.emit("vector", lambda e, dc=dc: e.tensor_tensor(e3, X[:, dc, :], mu_bc, Alu.subtract),
                       waits=([("vv", vR), ("mm", last_mm_xn)] if dc == 0 else []), inc="vv")
                S.emit("vector", lambda e: e.tensor_tensor(e3, e3, r_bc, Alu.mult), inc="vv")
                S.emit("vector", lambda e, dc=dc: e.tensor_scalar(e4, e3, gam[:, dc:dc + 1], bet[:, dc:dc + 1], Alu.mult, Alu.add), inc="vv")
                v = S.n("vv")
                S.emit("scalar", lambda e, dc=dc: e.activation(xnh[:, dc, :], e4, ActF.Copy),
                       waits=[("vv", v)], inc="ss")
                s = S.n("ss")
                S.emit("vector", lambda e, dc=dc: e.tensor_tensor(xnl[:, dc, :], e4, xnh[:, dc, :], Alu.subtract),
                       waits=[("ss", s)], inc="vv")
            v_xn = S.n("vv")
            if half == 0:
                tap("mu0", mu_bc, [128, THF])
                tap("r0", r_bc, [128, THF])
                tap("xnh0", B[:, 0:8192], [128, 8192], BF16)
                tap("xnl0", B[:, 8192:16384], [128, 8192], BF16)

            def a1_load_split(ec):
                out = []
                for hp in range(2):
                    par = hp
                    wb3 = wbuf.rearrange("p (q c e) -> p q c e", q=2, c=8)[:, par]
                    wh3 = whi.rearrange("p (q c e) -> p q c e", q=2, c=8)[:, par]
                    wl3 = wlo.rearrange("p (q c e) -> p q c e", q=2, c=8)[:, par]
                    dc0 = 8 * hp
                    S.dma("sync", lambda e, ec=ec, dc0=dc0, wb3=wb3: e.dma_start(
                        out=wb3[:, :, 0:128],
                        in_=winT.rearrange("(c p) e -> p c e", p=128)[:, dc0:dc0 + 8, 128 * ec:128 * (ec + 1)]),
                          waits=[("vv", wsplit_vv[par])], inc="lds")
                    S.dma("sync", lambda e, ec=ec, dc0=dc0, wb3=wb3: e.dma_start(
                        out=wb3[:, :, 128:256],
                        in_=winT.rearrange("(c p) e -> p c e", p=128)[:, dc0:dc0 + 8, EI + 128 * ec:EI + 128 * (ec + 1)]),
                          inc="lds")
                    ldw = S.n("lds")
                    S.emit("scalar", lambda e, wh3=wh3, wb3=wb3: e.activation(
                        wh3.rearrange("p c e -> p (c e)"), wb3.rearrange("p c e -> p (c e)"), ActF.Copy),
                           waits=[("lds", ldw), ("mm", wuse_mm[par])], inc="ss")
                    s = S.n("ss")
                    S.emit("vector", lambda e, wl3=wl3, wb3=wb3, wh3=wh3: e.tensor_tensor(
                        wl3.rearrange("p c e -> p (c e)"), wb3.rearrange("p c e -> p (c e)"),
                        wh3.rearrange("p c e -> p (c e)"), Alu.subtract),
                           waits=[("ss", s)], inc="vv")
                    wsplit_vv[par] = S.n("vv")
                    out.append((wh3, wl3, dc0, S.n("vv")))
                return out

            pend = a1_load_split(0)
            for ec in range(ECP):
                zb, gb = pZ[ec % 2], pG[ec % 2]
                bw = bank_waits(zb) + bank_waits(gb) + [("vv", v_xn)]
                cur, pend = pend, None
                for hi, (wh3, wl3, dc0, vw) in enumerate(cur):
                    first = True
                    for tgt, col0 in ((zb, 0), (gb, 128)):
                        for ci in range(8):
                            dc = dc0 + ci
                            for wsp, xsp in ((wh3, xnh), (wl3, xnh), (wh3, xnl)):
                                st = (hi == 0 and ci == 0 and wsp is wh3 and xsp is xnh)
                                sp = (hi == 1 and ci == 7 and wsp is wh3 and xsp is xnl)
                                w = ([("vv", vw)] + (bw if hi == 0 else [])) if first else []
                                S.emit("tensor", lambda e, wsp=wsp, xsp=xsp, ci=ci, dc=dc, tgt=tgt, col0=col0, st=st, sp=sp: e.matmul(
                                    tgt[:], wsp[:, ci, col0:col0 + 128], xsp[:, dc, :], start=st, stop=sp),
                                       waits=w, inc="mm")
                                first = False
                    wuse_mm[hi] = S.n("mm")
                mmA = S.n("mm")
                last_mm_xn = mmA
                if ec + 1 < ECP:
                    pend = a1_load_split(ec + 1)
                for _ in range(2):
                    emit_quant_tile("attn", v_init)
                S.emit("vector", lambda e, zb=zb: e.tensor_scalar(e1, zb[:], 15.0, -15.0, Alu.min, Alu.max),
                       waits=[("mm", mmA)], inc="vv")
                v1 = S.n("vv")
                S.emit("scalar", lambda e: e.activation(e2, e1, ActF.Sigmoid),
                       waits=[("vv", v1)], inc="ss")
                s1 = S.n("ss")
                S.emit("vector", lambda e, gb=gb: e.tensor_scalar(e3, gb[:], 15.0, -15.0, Alu.min, Alu.max), inc="vv")
                v2 = S.n("vv")
                S.emit("scalar", lambda e: e.activation(e4, e3, ActF.Sigmoid),
                       waits=[("vv", v2)], inc="ss")
                s2 = S.n("ss")
                S.emit("vector", lambda e, ec=ec, zb=zb: e.scalar_tensor_tensor(e1, zb[:], dtw[:, ec:ec + 1], e2, Alu.mult, Alu.mult),
                       waits=[("ss", s1)], inc="vv")
                S.emit("vector", lambda e: e.tensor_tensor(e3, e3, e4, Alu.mult),
                       waits=[("ss", s2)], inc="vv")
                S.emit("vector", lambda e: e.tensor_tensor(e1, e1, e3, Alu.mult), inc="vv")
                vT = S.n("vv")
                S.emit("scalar", lambda e, ec=ec: e.activation(Th[:, ec, :], e1, ActF.Copy),
                       waits=[("vv", vT)], inc="ss")
                sT = S.n("ss")
                S.emit("vector", lambda e, ec=ec: e.tensor_tensor(Tl[:, ec, :], e1, Th[:, ec, :], Alu.subtract),
                       waits=[("ss", sT)], inc="vv")
                bank_vv[id(zb)] = bank_vv[id(gb)] = S.n("vv")
            v_T = S.n("vv")
            if half == 0:
                tap("Th0", B[:, 16384:32768], [128, 16384], BF16)
                tap("Tl0", B[:, 32768:49152], [128, 16384], BF16)

            def a2_load_split(dc):
                out = []
                for hp in range(2):
                    par = hp
                    wb3 = wbuf.rearrange("p (q c e) -> p q c e", q=2, c=16)[:, par]
                    wh3 = whi.rearrange("p (q c e) -> p q c e", q=2, c=16)[:, par]
                    wl3 = wlo.rearrange("p (q c e) -> p q c e", q=2, c=16)[:, par]
                    ec0 = 16 * hp
                    S.dma("sync", lambda e, dc=dc, ec0=ec0, wb3=wb3: e.dma_start(
                        out=wb3[:],
                        in_=woutT.rearrange("(c p) d -> p c d", p=128)[:, ec0:ec0 + 16, 128 * dc:128 * (dc + 1)]),
                          waits=[("vv", wsplit_vv[par])], inc="lds")
                    ldw = S.n("lds")
                    S.emit("scalar", lambda e, wh3=wh3, wb3=wb3: e.activation(
                        wh3.rearrange("p c e -> p (c e)"), wb3.rearrange("p c e -> p (c e)"), ActF.Copy),
                           waits=[("lds", ldw), ("mm", wuse_mm[par])], inc="ss")
                    s = S.n("ss")
                    S.emit("vector", lambda e, wl3=wl3, wb3=wb3, wh3=wh3: e.tensor_tensor(
                        wl3.rearrange("p c e -> p (c e)"), wb3.rearrange("p c e -> p (c e)"),
                        wh3.rearrange("p c e -> p (c e)"), Alu.subtract),
                           waits=[("ss", s)], inc="vv")
                    wsplit_vv[par] = S.n("vv")
                    out.append((wh3, wl3, ec0, S.n("vv")))
                return out

            pend = a2_load_split(0)
            for dc in range(DC):
                bank = pY[dc % 2]
                bw = bank_waits(bank) + [("vv", v_T)]
                cur, pend = pend, None
                for hi, (wh3, wl3, ec0, vw) in enumerate(cur):
                    first = True
                    for ci in range(16):
                        ec = ec0 + ci
                        for wsp, tsp in ((wh3, Th), (wl3, Th), (wh3, Tl)):
                            st = (hi == 0 and ci == 0 and wsp is wh3 and tsp is Th)
                            sp = (hi == 1 and ci == 15 and wsp is wh3 and tsp is Tl)
                            w = ([("vv", vw)] + (bw if hi == 0 else [])) if first else []
                            S.emit("tensor", lambda e, wsp=wsp, tsp=tsp, ci=ci, ec=ec, bank=bank, st=st, sp=sp: e.matmul(
                                bank[:], wsp[:, ci, 0:128], tsp[:, ec, :], start=st, stop=sp),
                                   waits=w, inc="mm")
                            first = False
                    wuse_mm[hi] = S.n("mm")
                mmY = S.n("mm")
                if dc + 1 < DC:
                    pend = a2_load_split(dc + 1)
                for _ in range(4):
                    emit_quant_tile("attn", v_init)
                S.emit("vector", lambda e, dc=dc, bank=bank: e.scalar_tensor_tensor(
                    X[:, dc, :], X[:, dc, :], 2.0, bank[:], Alu.mult, Alu.add),
                       waits=[("mm", mmY)], inc="vv")
                bank_vv[id(bank)] = S.n("vv")
            a2_mm_end = S.n("mm")
            if half == 0:
                tap("x1", F[:, 0:8192], [128, 8192])
            mmS = ln_stats()
            vR = ln_mu_r(mmS)
            for dc in range(DC):
                S.emit("vector", lambda e, dc=dc: e.tensor_tensor(e3, X[:, dc, :], mu_bc, Alu.subtract),
                       waits=([("vv", vR)] if dc == 0 else []), inc="vv")
                S.emit("vector", lambda e, dc=dc: e.tensor_tensor(X[:, dc, :], e3, r_bc, Alu.mult), inc="vv")
                v = S.n("vv")
                S.emit("scalar", lambda e, dc=dc: e.activation(x2h[:, dc, :], X[:, dc, :], ActF.Copy),
                       waits=[("vv", v)], inc="ss")
                s = S.n("ss")
                S.emit("vector", lambda e, dc=dc: e.tensor_tensor(x2l[:, dc, :], X[:, dc, :], x2h[:, dc, :], Alu.subtract),
                       waits=[("ss", s)], inc="vv")
            v_x2 = S.n("vv")
            tap("x2" if half == 0 else "x2_h1", F[:, 0:8192], [128, 8192])
            if half == 0:
                tap("x2h0", B[:, 0:8192], [128, 8192], BF16)

            ms_started = False
            oacc_first = [True] * DC
            mm_gend = 0
            for grp in range(NGRP):
                for hci in range(HGRP):
                    hc = grp * HGRP + hci
                    hw = hc // 4
                    if hc % 4 == 0:
                        buf = wq1[hw % 2]
                        need = 16 * min((hw + 1) * 8, n_fc_tiles)
                        assert need <= S.n("wq"), f"F1 wq deadlock: need {need} planned {S.n('wq')}"
                        prev_mm = max(win_mm_end.get((half, hw - 2), 0), a2_mm_end)
                        for dc in range(DC):
                            S.dma("sync", lambda e, hw=hw, dc=dc, buf=buf: e.dma_start(
                                out=buf[:, dc, :],
                                in_=qfc[512 * hw:512 * (hw + 1), 128 * dc:128 * (dc + 1)],
                                transpose=True),
                                  waits=([("wq", need), ("mm", prev_mm)] if dc == 0 else []), inc="lds")
                        ld_w1 = (S.n("lds"), 0)
                    buf = wq1[hw % 2]
                    bank = pH[hc % 4]
                    first = True
                    idx = 0
                    for dc in range(DC):
                        for xs in (x2h, x2l):
                            st, sp = (idx == 0), (idx == 2 * DC - 1)
                            w = ([("lds", ld_w1[0]), ("lds2", ld_w1[1]), ("vv", v_x2)] + bank_waits(bank)) if first else []
                            S.emit("tensor", lambda e, buf=buf, dc=dc, hc=hc, xs=xs, st=st, sp=sp, bank=bank: e.matmul(
                                bank[:], buf[:, dc, 128 * (hc % 4):128 * (hc % 4) + 128], xs[:, dc, :], start=st, stop=sp),
                                   waits=w, inc="mm")
                            first = False
                            idx += 1
                    mmH = S.n("mm")
                    if hc % 4 == 3:
                        win_mm_end[(half, hw)] = mmH
                    par = hci % 2
                    S.emit("scalar", lambda e, bank=bank, par=par: e.activation(gsc[par], bank[:], ActF.Relu),
                           waits=[("mm", mmH)], inc="ss")
                    s = S.n("ss")
                    bank_ss[id(bank)] = s
                    S.emit("vector", lambda e, par=par: e.tensor_tensor(gsc[par], gsc[par], gsc[par], Alu.mult),
                           waits=[("ss", s)], inc="vv")
                    S.emit("vector", lambda e, par=par, hci=hci: e.tensor_copy(gh[:, hci, :], gsc[par]), inc="vv")
                    S.emit("vector", lambda e, par=par, hci=hci: e.tensor_tensor(gl[:, hci, :], gsc[par], gh[:, hci, :], Alu.subtract), inc="vv")
                    S.emit("vector", lambda e, par=par: e.tensor_tensor(g2[par], gsc[par], gsc[par], Alu.mult), inc="vv")
                    v = S.n("vv")
                    S.emit("tensor", lambda e, par=par, st=(not ms_started), hc=hc: e.matmul(
                        pS2[:], ones_r[:], g2[par], start=st, stop=(hc == HC - 1)),
                           waits=[("vv", v)], inc="mm")
                    ms_started = True
                    emit_quant_tile("ffn", v_x2)
                    emit_quant_tile("ffn", v_x2)
                    if half == 0 and hc == 0:
                        tap("g0", gsc[0], [128, THF])
                    if half == 0 and hc == 1:
                        tap("g1", gsc[1], [128, THF])
                v_g = S.n("vv")
                mm_gend = S.n("mm")
                for dcq in range(4):
                    buf2 = wq2[dcq % 2]
                    need = 16 * (n_fc_tiles + min(16 * (grp + 1), len(quant_tiles) - n_fc_tiles))
                    assert need <= S.n("wq"), f"F2 wq deadlock: need {need} planned {S.n('wq')}"
                    prev_mm = max(wq2_mm[dcq % 2], a2_mm_end)
                    for hci in range(HGRP):
                        hc = grp * HGRP + hci
                        S.dma("sync", lambda e, hc=hc, dcq=dcq, hci=hci, buf2=buf2: e.dma_start(
                            out=buf2[:, hci, :],
                            in_=qproj[512 * dcq:512 * (dcq + 1), 128 * hc:128 * (hc + 1)],
                            transpose=True),
                              waits=([("wq", need), ("mm", prev_mm)] if hci == 0 else []), inc="lds")
                    ld_w2 = (S.n("lds"), 0)
                    for dci in range(4):
                        dc = dcq * 4 + dci
                        bank = pY[dci % 2]
                        first = True
                        idx = 0
                        for hci in range(HGRP):
                            for gs in (gh, gl):
                                st, sp = (idx == 0), (idx == 2 * HGRP - 1)
                                w = ([("lds", ld_w2[0]), ("lds2", ld_w2[1]), ("vv", v_g)] + bank_waits(bank)) if first else []
                                S.emit("tensor", lambda e, buf2=buf2, hci=hci, dci=dci, gs=gs, st=st, sp=sp, bank=bank: e.matmul(
                                    bank[:], buf2[:, hci, 128 * dci:128 * dci + 128], gs[:, hci, :], start=st, stop=sp),
                                       waits=w, inc="mm")
                                first = False
                                idx += 1
                        mmO = S.n("mm")
                        if oacc_first[dc]:
                            S.emit("vector", lambda e, dc=dc, bank=bank: e.tensor_copy(Oacc[:, dc, :], bank[:]),
                                   waits=[("mm", mmO)], inc="vv")
                            oacc_first[dc] = False
                        else:
                            S.emit("vector", lambda e, dc=dc, bank=bank: e.tensor_tensor(Oacc[:, dc, :], Oacc[:, dc, :], bank[:], Alu.add),
                                   waits=[("mm", mmO)], inc="vv")
                        bank_vv[id(bank)] = S.n("vv")
                    wq2_mm[dcq % 2] = S.n("mm")
                    emit_quant_tile("ffn", v_x2)

            S.emit("vector", lambda e: e.tensor_scalar(e3, pS2[:], 1.0 / H, EPS32, Alu.mult, Alu.add),
                   waits=[("mm", mm_gend), ("vv", S.n("vv")), ("ldg", S.n("ldg"))], inc="vv")
            vS = newton_rsqrt(r_bc)
            for dc in range(DC):
                S.emit("vector", lambda e, dc=dc: e.tensor_tensor(e3, Oacc[:, dc, :], r_bc, Alu.mult),
                       waits=([("vv", vS)] if dc == 0 else []), inc="vv")
                S.emit("vector", lambda e, dc=dc: e.tensor_tensor(X[:, dc, :], X[:, dc, :], e3, Alu.add), inc="vv")
            tap("x3" if half == 0 else "x3_h1", F[:, 0:8192], [128, 8192])
            tap("Oacc0" if half == 0 else "Oacc_h1", share, [128, 8192])
            if half == 0:
                tap("sF", r_bc, [128, THF])
            mmS = ln_stats()
            vR = ln_mu_r(mmS)
            st_tgts = []
            for dc in range(DC):
                ob = ob0 if (dc % 2 == 0) else ob1
                w = [("vv", vR)] if dc == 0 else []
                if dc >= 2:
                    w.append(("stq", st_tgts[dc - 2]))
                S.emit("vector", lambda e, dc=dc: e.tensor_tensor(e3, X[:, dc, :], mu_bc, Alu.subtract),
                       waits=w, inc="vv")
                S.emit("vector", lambda e, dc=dc, ob=ob: e.tensor_tensor(ob, e3, r_bc, Alu.mult), inc="vv")
                v = S.n("vv")
                S.dma("sync", lambda e, dc=dc, tok0=tok0, ob=ob: e.dma_start(
                    out=out.rearrange("(c p) t -> p c t", p=128)[:, dc, tok0:tok0 + THF],
                    in_=ob),
                      waits=[("vv", v)], inc="stq")
                st_tgts.append(S.n("stq"))

        if "qfc" in taps:
            t_qfc = nc.dram_tensor("tap_qfc", [H, D], BF16, kind="ExternalOutput").ap()
            S.dma("sync", lambda e: e.dma_start(out=t_qfc, in_=qfc),
                  waits=[("wq", S.n("wq"))], inc="stq")
        if "qproj" in taps:
            t_qp = nc.dram_tensor("tap_qproj", [D, H], BF16, kind="ExternalOutput").ap()
            S.dma("sync", lambda e: e.dma_start(out=t_qp, in_=qproj),
                  waits=[("wq", S.n("wq"))], inc="stq")

        S.emit("sync", lambda e: None,
               waits=[("stq", S.n("stq")), ("wq", S.n("wq")),
                      ("lds", S.n("lds")), ("ldg", S.n("ldg"))])

        def run_queue(eng_obj, name):
            for waits, fn, inc, amt in S.q[name]:
                for sname, val in waits:
                    eng_obj.wait_ge(sems[sname], val)
                ins = fn(eng_obj)
                if inc is not None:
                    if ins is None:
                        ins = eng_obj.nop()
                    ins.then_inc(sems[inc], amt)

        @blk.sync
        def _(sync):
            run_queue(sync, "sync")

        @blk.gpsimd
        def _(gpsimd):
            run_queue(gpsimd, "gpsimd")

        @blk.tensor
        def _(tensor):
            run_queue(tensor, "tensor")

        @blk.vector
        def _(vector):
            run_queue(vector, "vector")

        @blk.scalar
        def _(scalar):
            run_queue(scalar, "scalar")

    return nc


_NC_CACHE = {}


def kernel(L0_attn_x, L0_attn_gamma, L0_attn_beta, L0_attn_W_in, L0_attn_W_out,
           L0_attn_dt, L0_ffn_W_fc, L0_ffn_W_proj):
    x = np.ascontiguousarray(np.asarray(L0_attn_x, dtype=np.float32)).reshape(8192, 2048)
    winT = np.ascontiguousarray(np.asarray(L0_attn_W_in, dtype=np.float32).T)
    woutT = np.ascontiguousarray(np.asarray(L0_attn_W_out, dtype=np.float32).T)
    wfc = np.ascontiguousarray(np.asarray(L0_ffn_W_fc, dtype=np.float32))
    wproj = np.ascontiguousarray(np.asarray(L0_ffn_W_proj, dtype=np.float32))
    gamma = np.ascontiguousarray(np.asarray(L0_attn_gamma, dtype=np.float32).reshape(16, 128).T)
    beta = np.ascontiguousarray(np.asarray(L0_attn_beta, dtype=np.float32).reshape(16, 128).T)
    dt = np.asarray(L0_attn_dt, dtype=np.float32)
    dt_fl = np.ascontiguousarray(dt.reshape(1, 4096))
    dt_pc = np.ascontiguousarray(dt.reshape(32, 128).T)
    ones = np.ones((128, 128), dtype=np.float32)

    if "nc" not in _NC_CACHE:
        _NC_CACHE["nc"] = build_core()
    nc = _NC_CACHE["nc"]

    in_maps = []
    for i in range(NCORES):
        xs = np.ascontiguousarray(x[1024 * i:1024 * (i + 1), :].T)
        in_maps.append({"x": xs, "winT": winT, "woutT": woutT, "wfc": wfc,
                        "wproj": wproj, "gamma": gamma, "beta": beta,
                        "dt_fl": dt_fl, "dt_pc": dt_pc, "ones_in": ones})
    res = run_bass_kernel_spmd(nc, in_maps, core_ids=list(range(NCORES)))
    o = np.empty((8192, 2048), dtype=np.float32)
    for i in range(NCORES):
        o[1024 * i:1024 * (i + 1), :] = res.results[i]["out"].T
    return o.reshape(2, 4096, 2048)
